# version 07
# baseline (speedup 1.0000x reference)
import numpy as np
import contextlib
import concourse.bass as bass
import concourse.mybir as mybir
from concourse.bass_utils import run_bass_kernel_spmd

F32 = mybir.dt.float32
BF16 = mybir.dt.bfloat16
U8 = mybir.dt.uint8
FP8 = mybir.dt.float8e4
AF = mybir.ActivationFunctionType
ALU = mybir.AluOpType

P = 128
D = 1024
KT = 8
S = 2048
NT = S // P
CH = 256
NCH = S // CH
R = 16
H1 = 4096
HT = H1 // P
FT = 512
NFT = S // FT
TSUB = FT // P
LR = 0.01 * 2.0
EPS = 1e-5
NSEQ = 2
NCORES = 8

PS_Y0 = 0
PS_Y1 = 1024
PS_PI = 2048
PS_PE = 2560
PS_DB = 3072
PS_N = 3584
PS_BC = 3586
PS_MID = 3600


class Sem:
    def __init__(self, stack, nc, name):
        self.h = stack.enter_context(nc.semaphore(name))
        self.n = 0

    def inc(self, instr, k=1):
        instr.then_inc(self.h, k)
        self.n += k
        return (self, self.n)


class Waiter:
    def __init__(self):
        self.seen = {}

    def wait(self, eng, ename, tok):
        if tok is None:
            return
        sem, val = tok
        if val is None or val <= 0:
            return
        key = (ename, id(sem))
        if self.seen.get(key, -1) >= val:
            return
        eng.wait_ge(sem.h, val)
        self.seen[key] = val


SERIALIZE_EVAC = False
NO_SMALL_MM = False


def build_program(debug_dump=None, nolora=False, skip_small=False):
    nc = bass.Bass("TRN2", target_bir_lowering=False)

    x_in = nc.dram_tensor("x", [NSEQ, S, D], F32, kind="ExternalInput").ap()
    wb_in = nc.dram_tensor("wb", [D, D], F32, kind="ExternalInput").ap()
    wp_in = nc.dram_tensor("wp", [D, D], F32, kind="ExternalInput").ap()
    a0_in = nc.dram_tensor("a0", [D, R], F32, kind="ExternalInput").ap()
    b0_in = nc.dram_tensor("b0", [R, D], F32, kind="ExternalInput").ap()
    gam_in = nc.dram_tensor("gam", [1, D], F32, kind="ExternalInput").ap()
    bet_in = nc.dram_tensor("bet", [1, D], F32, kind="ExternalInput").ap()
    k0_in = nc.dram_tensor("k0", [2 * D, H1], U8, kind="ExternalInput").ap()
    k1_in = nc.dram_tensor("k1", [H1, H1], U8, kind="ExternalInput").ap()
    k2_in = nc.dram_tensor("k2", [H1, D], U8, kind="ExternalInput").ap()
    s0_in = nc.dram_tensor("s0", [1, 1], F32, kind="ExternalInput").ap()
    s1_in = nc.dram_tensor("s1", [1, 1], F32, kind="ExternalInput").ap()
    s2_in = nc.dram_tensor("s2", [1, 1], F32, kind="ExternalInput").ap()
    eye_in = nc.dram_tensor("eye", [P, P], F32, kind="ExternalInput").ap()
    ones_in = nc.dram_tensor("ones", [P, P], F32, kind="ExternalInput").ap()
    y_out = nc.dram_tensor("y", [NSEQ, S, D], F32, kind="ExternalOutput").ap()
    x1_out = None
    ao_out = None
    if debug_dump in ("x1", "all"):
        x1_out = nc.dram_tensor("x1dbg", [NSEQ, S, D], F32, kind="ExternalOutput").ap()
    if debug_dump == "all":
        ao_out = nc.dram_tensor("aodbg", [NSEQ, S, D], F32, kind="ExternalOutput").ap()


    scratch_kind = "ExternalOutput" if debug_dump == "all" else "Internal"
    wb_bf = nc.dram_tensor("wb_bf", [D, D], BF16, kind=scratch_kind).ap()
    wp_bf = nc.dram_tensor("wp_bf", [D, D], BF16, kind="Internal").ap()
    w0_dr = nc.dram_tensor("w0_dr", [D, H1], FP8, kind=scratch_kind).ap()
    w1_dr = nc.dram_tensor("w1_dr", [H1, H1], FP8, kind="Internal").ap()
    w2_dr = nc.dram_tensor("w2_dr", [H1, D], FP8, kind="Internal").ap()
    u_dr = nc.dram_tensor("u_dr", [S, D], BF16, kind=scratch_kind).ap()
    xn_dr = nc.dram_tensor("xn_dr", [S, D], BF16, kind=scratch_kind).ap()
    v_dr = nc.dram_tensor("v_dr", [S, D], BF16, kind=scratch_kind).ap()
    u2_dr = nc.dram_tensor("u2_dr", [S, D], BF16, kind=scratch_kind).ap()
    x1_dr = nc.dram_tensor("x1_dr", [S, D], F32, kind=scratch_kind).ap()

    u_tmv = u_dr.rearrange("(t p) k -> p t k", p=P)
    v_tmv = v_dr.rearrange("(t p) k -> p t k", p=P)
    u2_tmv = u2_dr.rearrange("(t p) k -> p t k", p=P)
    x1_tmv = x1_dr.rearrange("(t p) k -> p t k", p=P)
    w0_kt = w0_dr.rearrange("(t p) j -> p t j", p=P)
    w1_kt = w1_dr.rearrange("(t p) j -> p t j", p=P)
    w2_kt = w2_dr.rearrange("(t p) j -> p t j", p=P)
    k2_kt = k2_in.rearrange("(t p) n -> p t n", p=P)

    st = contextlib.ExitStack()
    wt = Waiter()
    with st:
        eyeb = st.enter_context(nc.sbuf_tensor("eyeb", [P, P], F32))
        onesb = st.enter_context(nc.sbuf_tensor("onesb", [P, P], F32))
        onesbb = st.enter_context(nc.sbuf_tensor("onesbb", [P, P], BF16))
        grow = st.enter_context(nc.sbuf_tensor("grow", [P, D], F32))
        brow = st.enter_context(nc.sbuf_tensor("brow", [P, D], F32))
        grep = st.enter_context(nc.sbuf_tensor("grep", [P, D], F32))
        brep = st.enter_context(nc.sbuf_tensor("brep", [P, D], F32))
        spack = st.enter_context(nc.sbuf_tensor("spack", [P, 4], F32))
        scola = st.enter_context(nc.sbuf_tensor("scola", [P, 4], F32))
        s0c2 = st.enter_context(nc.sbuf_tensor("s0c2", [P, 1], F32))
        a0f = st.enter_context(nc.sbuf_tensor("a0f", [P, KT, R], F32))
        a0nb = st.enter_context(nc.sbuf_tensor("a0nb", [P, KT, R], BF16))
        a0t2 = st.enter_context(nc.sbuf_tensor("a0t2", [R, D], F32))
        b0f = st.enter_context(nc.sbuf_tensor("b0f", [R, D], F32))
        b0tf = st.enter_context(nc.sbuf_tensor("b0tf", [P, KT, R], F32))
        b0tb = st.enter_context(nc.sbuf_tensor("b0tb", [P, KT, R], BF16))
        zcol = st.enter_context(nc.sbuf_tensor("zcol", [P, 1], F32))
        epscol = st.enter_context(nc.sbuf_tensor("epscol", [P, 1], F32))
        zrow = st.enter_context(nc.sbuf_tensor("zrow", [1, D], BF16))
        SAb = st.enter_context(nc.sbuf_tensor("SAb", [R, D], F32))
        SBb = st.enter_context(nc.sbuf_tensor("SBb", [P, P], F32))

        ps = st.enter_context(nc.psum_tensor("psar", [P, 4096], F32))

        sT = Sem(st, nc, "sT")
        sD = Sem(st, nc, "sD")
        sA = Sem(st, nc, "sA")
        sP = Sem(st, nc, "sP")
        _semn = [0]

        def mk(tag):
            _semn[0] += 1
            return Sem(st, nc, f"q{_semn[0]}_{tag}")

        def dma(eng, sem, out, in_, transpose=False):
            return sem.inc(eng.dma_start(out=out, in_=in_, transpose=transpose), 16)

        q_cl = mk("cl")
        q_cvL = [mk("cvL0"), mk("cvL1")]
        q_cvS = [mk("cvS0"), mk("cvS1")]
        q_cvA = mk("cvA")
        q_cvAS = [mk("cvAS0"), mk("cvAS1")]
        q_cvP = mk("cvP")
        q_cvPS = [mk("cvPS0"), mk("cvPS1")]
        q_zr = mk("zr")
        q_a1x = [mk("a1x0"), mk("a1x1"), mk("a1x2")]
        q_a1u = [mk("a1u0"), mk("a1u1")]
        q_a1n = [mk("a1n0"), mk("a1n1")]
        q_a2wp = mk("a2wp")
        q_a2s = [mk("a2s0"), mk("a2s1")]
        q_a2v = [mk("a2v0"), mk("a2v1")]
        q_a3wb = mk("a3wb")
        q_a3u = [mk("a3u0"), mk("a3u1")]
        q_a3v = [mk("a3v0"), mk("a3v1")]
        q_a3x = [mk("a3x0"), mk("a3x1")]
        q_a3s = [mk("a3s0"), mk("a3s1")]
        q_bx = [mk("bx0"), mk("bx1")]
        q_bu = [mk("bu0"), mk("bu1")]
        q_bf = [mk("bf0"), mk("bf1")]
        q_bw0 = [mk("bw00"), mk("bw01")]
        q_bw1 = [mk("bw10"), mk("bw11")]
        q_bw2 = [mk("bw20"), mk("bw21")]
        q_bo = [mk("bo0"), mk("bo1")]
        q_dbg = mk("dbg")

        misc = {}

        for s in range(NSEQ):
            x_s = x_in[s].rearrange("(t p) k -> p t k", p=P)
            y_s = y_out[s].rearrange("(t p) k -> p t k", p=P)

            front = contextlib.ExitStack()
            with front:
                if s == 0:
                    cf32 = front.enter_context(nc.sbuf_tensor("p0cf32", [P, 2, D], F32))
                    cbf = front.enter_context(nc.sbuf_tensor("p0cbf", [P, 2, D], BF16))
                    u8a = front.enter_context(nc.sbuf_tensor("p0u8a", [P, 2, H1], U8))
                    u8b = front.enter_context(nc.sbuf_tensor("p0u8b", [P, 2, H1], U8))
                    f8d = front.enter_context(nc.sbuf_tensor("p0f8d", [P, 2, H1], FP8))
                    u8s = front.enter_context(nc.sbuf_tensor("p0u8s", [P, 2, H1], U8))
                    f8s = front.enter_context(nc.sbuf_tensor("p0f8s", [P, 2, H1], FP8))
                    u8p = front.enter_context(nc.sbuf_tensor("p0u8p", [P, 2, H1], U8))
                    f8p = front.enter_context(nc.sbuf_tensor("p0f8p", [P, 2, H1], FP8))
                xin = front.enter_context(nc.sbuf_tensor(f"a1xin{s}", [P, 3, D], F32))
                st6 = front.enter_context(nc.sbuf_tensor(f"a1st6{s}", [P, 12], F32))
                mvb = front.enter_context(nc.sbuf_tensor(f"a1mv{s}", [P, 2], F32))
                stdb = front.enter_context(nc.sbuf_tensor(f"a1std{s}", [P, 3], F32))
                rsb = front.enter_context(nc.sbuf_tensor(f"a1rs{s}", [P, 2], F32))
                acol = front.enter_context(nc.sbuf_tensor(f"a1ac{s}", [P, 1], F32))
                ubf = front.enter_context(nc.sbuf_tensor(f"a1ubf{s}", [P, 2, D], BF16))
                xtb = front.enter_context(nc.sbuf_tensor(f"a1xt{s}", [P, 2, D], F32))
                xgb = front.enter_context(nc.sbuf_tensor(f"a1xg{s}", [P, 2, D], F32))
                xnb = front.enter_context(nc.sbuf_tensor(f"a1xn{s}", [P, 2, D], BF16))

                blkF = front.enter_context(nc.Block(f"front{s}"))
                if s == 0:

                    @blkF.vector
                    def _(v):
                        v.memset(grow[:], 0.0)
                        v.memset(brow[:], 0.0)
                        misc["d_pad0"] = sD.inc(v.memset(spack[:], 0.0))

                    @blkF.sync
                    def _(sync):
                        wt.wait(sync, "sync", misc["d_pad0"])
                        for dst, src in [(eyeb[:], eye_in), (onesb[:], ones_in),
                                         (grow[0:1, :], gam_in), (brow[0:1, :], bet_in),
                                         (b0f[:], b0_in)]:
                            dma(sync, q_cl, dst, src)
                        dma(sync, q_cl, a0f[:], a0_in.rearrange("(j p) r -> p j r", p=P))
                        for i, sc_ in enumerate([s0_in, s1_in, s2_in]):
                            dma(sync, q_cl, spack[0:1, i:i + 1], sc_)
                        misc["Lconst"] = (q_cl, q_cl.n)

                    @blkF.vector
                    def _(v):
                        v.memset(zcol[:], 0.0)
                        v.memset(epscol[:], EPS)
                        v.memset(zrow[:], 0.0)
                        v.memset(SAb[:], 0.0)
                        v.memset(SBb[:], 0.0)
                        wt.wait(v, "v", misc["Lconst"])
                        v.tensor_copy(onesbb[:], onesb[:])
                        misc["d_mem"] = sD.inc(v.tensor_copy(a0nb[:], a0f[:]))

                    @blkF.gpsimd
                    def _(g):
                        wt.wait(g, "g", misc["d_mem"])
                        dma(g, q_zr, xn_dr[S - 1:S, :], zrow[0:1, :])
                        misc["zr"] = (q_zr, q_zr.n)

                    @blkF.tensor
                    def _(t):
                        wt.wait(t, "t", misc["Lconst"])
                        t.matmul(ps[:, 0:512], onesb[:], grow[:, 0:512], start=True, stop=True)
                        mm = t.matmul(ps[:, 512:1024], onesb[:], grow[:, 512:1024],
                                      start=True, stop=True)
                        misc["t_gam"] = sT.inc(mm)

                    @blkF.vector
                    def _(v):
                        wt.wait(v, "v", misc["t_gam"])
                        v.tensor_copy(grep[:, 0:512], ps[:, 0:512])
                        misc["d_gam"] = sD.inc(v.tensor_copy(grep[:, 512:1024], ps[:, 512:1024]))

                    @blkF.tensor
                    def _(t):
                        wt.wait(t, "t", misc["d_gam"])
                        t.matmul(ps[:, 0:512], onesb[:], brow[:, 0:512], start=True, stop=True)
                        mm = t.matmul(ps[:, 512:1024], onesb[:], brow[:, 512:1024],
                                      start=True, stop=True)
                        misc["t_bet"] = sT.inc(mm)

                    @blkF.vector
                    def _(v):
                        wt.wait(v, "v", misc["t_bet"])
                        v.tensor_copy(brep[:, 0:512], ps[:, 0:512])
                        misc["d_bet"] = sD.inc(v.tensor_copy(brep[:, 512:1024], ps[:, 512:1024]))

                    @blkF.tensor
                    def _(t):
                        wt.wait(t, "t", misc["d_bet"])
                        t.matmul(ps[:, PS_PI:PS_PI + 4], onesb[:], spack[:, 0:4],
                                 start=True, stop=True)
                        t.drain()
                        for j in range(KT):
                            t.transpose(ps[0:R, 2560 + 128 * j:2560 + 128 * (j + 1)],
                                        a0f[:, j, :], eyeb[:])
                        mm = None
                        for j in range(KT):
                            mm = t.transpose(ps[:, PS_N + R * j:PS_N + R * (j + 1)],
                                             b0f[0:R, P * j:P * (j + 1)], eyeb[0:R, 0:R])
                        misc["t_misc"] = sT.inc(mm)
                        t.drain()

                    @blkF.vector
                    def _(v):
                        wt.wait(v, "v", misc["t_misc"])
                        v.tensor_copy(scola[:], ps[:, PS_PI:PS_PI + 4])
                        v.tensor_copy(b0tf.rearrange("p j r -> p (j r)")[:], ps[:, PS_N:PS_N + KT * R])
                        v.drain()
                        v.tensor_scalar(s0c2[:], scola[:, 0:1], 2.0, None, ALU.mult)
                        misc["d_misc"] = sD.inc(
                            v.tensor_copy(b0tb.rearrange("p j r -> p (j r)")[:],
                                          ps[:, PS_N:PS_N + KT * R]))

                    @blkF.scalar
                    def _(sc):
                        sc.preload_act_table(AF.Sqrt)
                        wt.wait(sc, "sc", misc["t_misc"])
                        misc["a_misc"] = sA.inc(
                            sc.activation(a0t2[0:R, :], ps[0:R, 2560:3584], AF.Copy,
                                          bias=0.0, scale=2.0))

                    pool_tiles = ([("cast", wp_in, wp_bf, i) for i in range(8)]
                                  + [("cast", wb_in, wb_bf, i) for i in range(8)]
                                  + [("k2", None, None, i) for i in range(8)])
                    poolC = {}
                    poolS = {}
                    for i, (kind, psrc, pdst, j) in enumerate(pool_tiles):
                        @blkF.gpsimd
                        def _(g, i=i, kind=kind, psrc=psrc, j=j):
                            wt.wait(g, "g", poolS.get(i - 2))
                            if kind == "cast":
                                li = dma(g, q_cvP, cf32[:, i % 2, :], psrc[P * j:P * (j + 1), :])
                            else:
                                li = dma(g, q_cvP, u8p[:, i % 2, :].rearrange("p (t n) -> p t n", n=D),
                                         k2_kt[:, 4 * j:4 * (j + 1), :])
                            wt.wait(g, "g", li)
                            if kind == "cast":
                                poolC[i] = sP.inc(g.tensor_copy(cbf[:, i % 2, :], cf32[:, i % 2, :]))
                            else:
                                poolC[i] = sP.inc(g.tensor_scalar(f8p[:, i % 2, :], u8p[:, i % 2, :],
                                                                  4.0, -2.0, ALU.mult, ALU.add))

                        @blkF.sync
                        def _(sync, i=i, kind=kind, pdst=pdst, j=j):
                            wt.wait(sync, "sync", poolC[i])
                            if kind == "cast":
                                dma(sync, q_cvPS[i % 2], pdst[P * j:P * (j + 1), :], cbf[:, i % 2, :])
                            else:
                                dma(sync, q_cvPS[i % 2], w2_kt[:, 4 * j:4 * (j + 1), :],
                                    f8p[:, i % 2, :].rearrange("p (t n) -> p t n", n=D))
                            poolS[i] = (q_cvPS[i % 2], q_cvPS[i % 2].n)
                            if i == 7:
                                misc["wp_done"] = [(qq, qq.n) for qq in q_cvPS]
                            if i == 15:
                                misc["wb_done"] = [(qq, qq.n) for qq in q_cvPS]
                            if i == 23:
                                misc["k2_done"] = [(qq, qq.n) for qq in q_cvPS]

                    def emit_dve_conv():
                        dve_tiles = ([("k0", None, None, i) for i in range(8)]
                                     + [("aff", k1_in, w1_dr, i) for i in range(16)])
                        dveL = {}
                        dveC = {}
                        dveS = {}
                        for i, (kind, src2, dst, j) in enumerate(dve_tiles):
                            @blkF.sync
                            def _(sync, i=i, kind=kind, src2=src2, j=j):
                                wt.wait(sync, "sync", dveC.get(i - 2))
                                if kind == "k0":
                                    dma(sync, q_cvL[i % 2], u8a[:, i % 2, :], k0_in[P * j:P * (j + 1), :])
                                    dma(sync, q_cvL[i % 2], u8b[:, i % 2, :],
                                        k0_in[D + P * j:D + P * (j + 1), :])
                                else:
                                    dma(sync, q_cvL[i % 2], u8a[:, i % 2, :], src2[P * j:P * (j + 1), :])
                                dveL[i] = (q_cvL[i % 2], q_cvL[i % 2].n)

                            @blkF.vector
                            def _(v, i=i, kind=kind):
                                wt.wait(v, "v", dveL[i])
                                wt.wait(v, "v", dveS.get(i - 2))
                                if kind == "k0":
                                    ins = v.tensor_tensor(f8d[:, i % 2, :], u8a[:, i % 2, :],
                                                          u8b[:, i % 2, :], ALU.subtract)
                                else:
                                    ins = v.tensor_scalar(f8d[:, i % 2, :], u8a[:, i % 2, :],
                                                          4.0, -2.0, ALU.mult, ALU.add)
                                dveC[i] = sD.inc(ins)

                            @blkF.sync
                            def _(sync, i=i, kind=kind, dst=dst, j=j):
                                wt.wait(sync, "sync", dveC[i])
                                if kind == "k0":
                                    dma(sync, q_cvS[i % 2], w0_dr[P * j:P * (j + 1), :], f8d[:, i % 2, :])
                                else:
                                    dma(sync, q_cvS[i % 2], dst[P * j:P * (j + 1), :], f8d[:, i % 2, :])
                                dveS[i] = (q_cvS[i % 2], q_cvS[i % 2].n)
                                if i == 7:
                                    misc["k0_done"] = [(qq, qq.n) for qq in q_cvS]

                    def emit_act_conv():
                        act_tiles = [(k1_in, w1_dr, 16 + i) for i in range(16)]

                        @blkF.scalar
                        def _(sc):
                            prevS = {}
                            for i, (src2, dst, j) in enumerate(act_tiles):
                                wt.wait(sc, "sc", prevS.get(i - 2))
                                li = dma(sc, q_cvA, u8s[:, i % 2, :], src2[P * j:P * (j + 1), :])
                                wt.wait(sc, "sc", li)
                                sA.inc(sc.activation(f8s[:, i % 2, :], u8s[:, i % 2, :],
                                                     AF.Copy, bias=-2.0, scale=4.0))
                                sc.drain()
                                prevS[i] = dma(sc, q_cvAS[i % 2], dst[P * j:P * (j + 1), :],
                                               f8s[:, i % 2, :])
                Lx = {}
                Dst = {}
                Dub = {}
                Dxt = {}
                Ast = {}
                Pxn = {}
                Ust = {}
                Nst = {}

                for i in range(NT):
                    @blkF.sync
                    def _(sync, i=i):
                        wt.wait(sync, "sync", Dxt.get(i - 3))
                        Lx[i] = dma(sync, q_a1x[i % 3], xin[:, i % 3, :], x_s[:, i, :])

                    @blkF.vector
                    def _(v, i=i):
                        wt.wait(v, "v", Lx[i])
                        v.bn_stats(st6[:, 0:6], xin[:, i % 3, 0:512])
                        v.bn_stats(st6[:, 6:12], xin[:, i % 3, 512:1024])
                        v.drain()
                        Dst[i] = sD.inc(v.bn_aggr(mvb[:], st6[:]))

                    @blkF.scalar
                    def _(sc, i=i):
                        wt.wait(sc, "sc", Dst[i])
                        Ast[i] = sA.inc(sc.activation(stdb[:, 0:1], mvb[:, 1:2], AF.Sqrt,
                                                      bias=epscol[:], scale=1.0))

                    @blkF.vector
                    def _(v, i=i):
                        wt.wait(v, "v", Ast[i])
                        v.reciprocal(rsb[:, 0:1], stdb[:, 0:1])
                        v.drain()
                        v.tensor_scalar(stdb[:, 1:2], mvb[:, 1:2], rsb[:, 0:1], rsb[:, 0:1],
                                        ALU.mult, ALU.mult)
                        wt.wait(v, "v", Ust.get(i - 2))
                        Dub[i] = sD.inc(v.tensor_scalar(ubf[:, i % 2, :], xin[:, i % 3, :],
                                                        mvb[:, 0:1], rsb[:, 0:1],
                                                        ALU.subtract, ALU.mult))

                    @blkF.scalar
                    def _(sc, i=i):
                        wt.wait(sc, "sc", Dub[i])
                        Ast[i + 100] = sA.inc(sc.activation(stdb[:, 2:3], stdb[:, 1:2],
                                                            AF.Sqrt, bias=epscol[:], scale=1.0))

                    @blkF.vector
                    def _(v, i=i):
                        wt.wait(v, "v", Ast[i + 100])
                        v.reciprocal(rsb[:, 1:2], stdb[:, 2:3])
                        v.drain()
                        v.tensor_scalar(acol[:], rsb[:, 0:1], rsb[:, 1:2], None, ALU.mult)
                        v.drain()
                        Dxt[i] = sD.inc(v.tensor_scalar(xtb[:, i % 2, :], xin[:, i % 3, :],
                                                        mvb[:, 0:1], acol[:],
                                                        ALU.subtract, ALU.mult))
                        v.drain()
                        v.tensor_tensor(xgb[:, i % 2, :], xtb[:, i % 2, :],
                                        grep[:], ALU.mult)
                        v.drain()
                        wt.wait(v, "v", Nst.get(i - 2))
                        Pxn[i] = sD.inc(v.tensor_tensor(xnb[:, i % 2, :], xgb[:, i % 2, :],
                                                        brep[:], ALU.add))

                    @blkF.scalar
                    def _(sc, i=i):
                        wt.wait(sc, "sc", Dub[i])
                        Ust[i] = dma(sc, q_a1u[i % 2], u_tmv[:, i, :], ubf[:, i % 2, :])
                        wt.wait(sc, "sc", Pxn[i])
                        if i == 0:
                            Nst[i] = dma(sc, q_a1n[0], xn_dr[0:P - 1, :], xnb[1:P, 0, :])
                        else:
                            Nst[i] = dma(sc, q_a1n[i % 2],
                                         xn_dr[P * i - 1:P * (i + 1) - 1, :], xnb[:, i % 2, :])

                @blkF.scalar
                def _(sc):
                    for qq in q_a1u + q_a1n:
                        wt.wait(sc, "sc", (qq, qq.n))

                A1u_done = [(qq, qq.n) for qq in q_a1u]
                A1n_done = [(qq, qq.n) for qq in q_a1n]

                wpT = front.enter_context(nc.sbuf_tensor(f"a2wp{s}", [P, KT, D], BF16,
                                                         is_transpose_dest=True))
                shb = front.enter_context(nc.sbuf_tensor(f"a2sh{s}", [P, 2, KT, CH], BF16,
                                                         is_transpose_dest=True))
                vob = front.enter_context(nc.sbuf_tensor(f"a2vo{s}", [P, 2, 2, D], BF16))
                LwpT = [None]
                Lsh = {}
                Tv = {}
                Av = {}
                Sv = {}

                @blkF.sync
                def _(sync):
                    for tok in misc["wp_done"]:
                        wt.wait(sync, "sync", tok)
                    for j in range(KT):
                        dma(sync, q_a2wp, wpT[:, j, :], wp_bf[0:D, P * j:P * (j + 1)],
                            transpose=True)
                    LwpT[0] = (q_a2wp, q_a2wp.n)

                for c in range(NCH):
                    @blkF.sync
                    def _(sync, c=c):
                        wt.wait(sync, "sync", Tv.get((c - 2, 1)))
                        for tok in A1n_done:
                            wt.wait(sync, "sync", tok)
                        wt.wait(sync, "sync", misc["zr"])
                        for j in range(KT):
                            dma(sync, q_a2s[c % 2], shb[:, c % 2, j, :],
                                xn_dr[CH * c:CH * (c + 1), P * j:P * (j + 1)], transpose=True)
                        Lsh[c] = (q_a2s[c % 2], q_a2s[c % 2].n)

                    @blkF.tensor
                    def _(t, c=c):
                        wt.wait(t, "t", Lsh[c])
                        wt.wait(t, "t", LwpT[0])
                        wt.wait(t, "t", Av.get((c - 1, 1)))
                        for ts_ in range(2):
                            base = PS_Y0 if ts_ == 0 else PS_Y1
                            for h in range(2):
                                mm = None
                                for j in range(KT):
                                    mm = t.matmul(ps[:, base + 512 * h:base + 512 * (h + 1)],
                                                  shb[:, c % 2, j, P * ts_:P * (ts_ + 1)],
                                                  wpT[:, j, 512 * h:512 * (h + 1)],
                                                  start=(j == 0), stop=(j == KT - 1))
                            Tv[(c, ts_)] = sT.inc(mm)

                    @blkF.scalar
                    def _(sc, c=c):
                        wt.wait(sc, "sc", Sv.get(c - 2))
                        for ts_ in range(2):
                            base = PS_Y0 if ts_ == 0 else PS_Y1
                            wt.wait(sc, "sc", Tv[(c, ts_)])
                            Av[(c, ts_)] = sA.inc(
                                sc.activation(vob[:, c % 2, ts_, :], ps[:, base:base + D],
                                              AF.Copy, bias=0.0, scale=1.0))
                        sc.drain()
                        for ts_ in range(2):
                            Sv[c] = dma(sc, q_a2v[c % 2], v_tmv[:, 2 * c + ts_, :],
                                        vob[:, c % 2, ts_, :])

                @blkF.scalar
                def _(sc):
                    for qq in q_a2v:
                        wt.wait(sc, "sc", (qq, qq.n))

                if s == 0:
                    emit_dve_conv()
                    emit_act_conv()
                    misc["k1_done"] = [(qq, qq.n) for qq in (q_cvS + q_cvAS)]
                    misc["conv_all"] = [(qq, qq.n) for qq in
                                        (q_cvS + q_cvAS + q_cvPS + [q_cvA, q_cvL[0], q_cvL[1], q_cvP])]

            SvD = dict(Sv)

            a3 = contextlib.ExitStack()
            Gx1_all = {}
            with a3:
                wbT = a3.enter_context(nc.sbuf_tensor(f"a3wb{s}", [P, KT, D], BF16,
                                                      is_transpose_dest=True))
                ufm = a3.enter_context(nc.sbuf_tensor(f"a3uf{s}", [P, 2, KT, CH], BF16,
                                                      is_transpose_dest=True))
                vfm = a3.enter_context(nc.sbuf_tensor(f"a3vf{s}", [P, 2, KT, CH], BF16,
                                                      is_transpose_dest=True))
                utm = a3.enter_context(nc.sbuf_tensor(f"a3ut{s}", [P, 2, 2, D], BF16))
                vtm = a3.enter_context(nc.sbuf_tensor(f"a3vt{s}", [P, 2, 2, D], BF16))
                xck = a3.enter_context(nc.sbuf_tensor(f"a3xc{s}", [P, 2, 2, D], F32))
                pib = a3.enter_context(nc.sbuf_tensor(f"a3pi{s}", [P, 2, P], BF16))
                peb = a3.enter_context(nc.sbuf_tensor(f"a3pe{s}", [P, 32], BF16))
                dAc = a3.enter_context(nc.sbuf_tensor(f"a3da{s}", [R, D], F32))
                dBc = a3.enter_context(nc.sbuf_tensor(f"a3db{s}", [P, P], F32))
                aef = a3.enter_context(nc.sbuf_tensor(f"a3ae{s}", [P, D], BF16))
                bef = a3.enter_context(nc.sbuf_tensor(f"a3be{s}", [P, KT, P], BF16))
                midb = a3.enter_context(nc.sbuf_tensor(f"a3mi{s}", [P, CH], BF16))
                nsq = a3.enter_context(nc.sbuf_tensor(f"a3nq{s}", [1, 8], F32))
                ccb = a3.enter_context(nc.sbuf_tensor(f"a3cc{s}", [P, 2], BF16))
                nab = a3.enter_context(nc.sbuf_tensor(f"a3na{s}", [P, 1], F32))
                nbb = a3.enter_context(nc.sbuf_tensor(f"a3nb{s}", [P, 1], F32))
                nabb = a3.enter_context(nc.sbuf_tensor(f"a3nab{s}", [P, 2], BF16))
                bcb = a3.enter_context(nc.sbuf_tensor(f"a3bc{s}", [P, 2], F32))
                sqd = a3.enter_context(nc.sbuf_tensor(f"a3sq{s}", [P, D], BF16))
                ybf = a3.enter_context(nc.sbuf_tensor(f"a3yb{s}", [P, 2, D], F32))
                syb = a3.enter_context(nc.sbuf_tensor(f"a3sy{s}", [P, 2], F32))
                sy2b = a3.enter_context(nc.sbuf_tensor(f"a3s2{s}", [P, 2], F32))
                lnc = a3.enter_context(nc.sbuf_tensor(f"a3ln{s}", [P, 10], F32))
                tpb = a3.enter_context(nc.sbuf_tensor(f"a3tp{s}", [P, 2, D], F32))
                x1b = a3.enter_context(nc.sbuf_tensor(f"a3x1{s}", [P, 2, 2, D], F32))

                blkA3 = a3.enter_context(nc.Block(f"a3_{s}"))

                LwbT = [None]
                Lu = {}
                Lv = {}
                Gx = {}
                Tpi = {}
                Tpe = {}
                Tdb = {}
                Tda = {}
                Tn1 = {}
                Tbc = {}
                Tmid = {}
                TY = {}
                Dpi = {}
                Dpe = {}
                Ddab = {}
                Dn1 = {}
                Dn2 = {}
                Dchain = {}
                Dx1 = {}
                An = {}
                An2 = {}
                Amid = {}
                Aev = {}
                Asq = {}
                Gx1 = {}
                Dln = {}
                Aln = {}
                d_zero = [None]
                aodump = [None]

                @blkA3.vector
                def _(v):
                    if s > 0:
                        v.memset(SAb[:], 0.0)
                        v.memset(SBb[:], 0.0)
                    v.memset(pib.rearrange("p a b -> p (a b)")[:], 0.0)
                    v.memset(bef.rearrange("p a b -> p (a b)")[:], 0.0)
                    v.memset(midb[:], 0.0)
                    v.memset(aef[:], 0.0)
                    v.memset(nab[:], 0.0)
                    v.memset(nabb[:], 0.0)
                    d_zero[0] = sD.inc(v.memset(ccb[:], 0.0))
                    v.drain()

                wbT_done = [False]

                def _load_wbT(sync):
                    for tok in misc["wb_done"]:
                        wt.wait(sync, "sync", tok)
                    for j in range(KT):
                        dma(sync, q_a3wb, wbT[:, j, :], wb_bf[0:D, P * j:P * (j + 1)],
                            transpose=True)
                    wt.wait(sync, "sync", (q_a3wb, q_a3wb.n))
                    LwbT[0] = sP.inc(sync.nop())
                    wbT_done[0] = True

                for c in range(NCH):
                    @blkA3.sync
                    def _(sync, c=c):
                        wt.wait(sync, "sync", Tmid.get(c - 2))
                        for tok in A1u_done:
                            wt.wait(sync, "sync", tok)
                        for j in range(KT):
                            dma(sync, q_a3u[c % 2], ufm[:, c % 2, j, :],
                                u_dr[CH * c:CH * (c + 1), P * j:P * (j + 1)], transpose=True)
                        tok = dma(sync, q_a3u[c % 2], utm[:, c % 2, :, :],
                                  u_tmv[:, 2 * c:2 * c + 2, :])
                        wt.wait(sync, "sync", tok)
                        Lu[c] = sP.inc(sync.nop())
                        if not wbT_done[0]:
                            _load_wbT(sync)

                    @blkA3.sync
                    def _(sync, c=c):
                        wt.wait(sync, "sync", Tda.get((c - 2, 1)))
                        wt.wait(sync, "sync", SvD[c])
                        for j in range(KT):
                            dma(sync, q_a3v[c % 2], vfm[:, c % 2, j, :],
                                v_dr[CH * c:CH * (c + 1), P * j:P * (j + 1)], transpose=True)
                        tok = dma(sync, q_a3v[c % 2], vtm[:, c % 2, :, :],
                                  v_tmv[:, 2 * c:2 * c + 2, :])
                        wt.wait(sync, "sync", tok)
                        Lv[c] = sP.inc(sync.nop())

                    @blkA3.scalar
                    def _(sc, c=c):
                        wt.wait(sc, "sc", Dx1.get((c - 2, 1)))
                        Gx[c] = dma(sc, q_a3x[c % 2], xck[:, c % 2, :, :],
                                    x_s[:, 2 * c:2 * c + 2, :])

                    @blkA3.tensor
                    def _(t, c=c):
                        wt.wait(t, "t", Lu[c])
                        wt.wait(t, "t", Lv[c])
                        wt.wait(t, "t", misc["d_misc"])
                        wt.wait(t, "t", misc["a_misc"])
                        wt.wait(t, "t", d_zero[0])
                        wt.wait(t, "t", Ddab.get(c - 1))
                        if skip_small:
                            Tpi[c] = sT.inc(t.nop())
                            Tpe[c] = Tpi[c]
                            return
                        if NO_SMALL_MM:
                            Tpi[c] = sT.inc(t.matmul(ps[:, PS_PI:PS_PI + 512],
                                                     ufm[:, c % 2, 0, 0:P],
                                                     wbT[:, 0, 0:512], start=True, stop=True))
                            Tpe[c] = sT.inc(t.matmul(ps[:, PS_PE:PS_PE + 512],
                                                     vfm[:, c % 2, 0, 0:P],
                                                     wbT[:, 0, 0:512], start=True, stop=True))
                            return
                        mm = None
                        for ts_ in range(2):
                            for j in range(KT):
                                mm = t.matmul(ps[:, PS_PI + R * ts_:PS_PI + R * (ts_ + 1)],
                                              ufm[:, c % 2, j, P * ts_:P * (ts_ + 1)],
                                              b0tb[:, j, :], start=(j == 0), stop=(j == KT - 1))
                        Tpi[c] = sT.inc(mm)
                        mm = None
                        for ts_ in range(2):
                            for j in range(KT):
                                mm = t.matmul(ps[:, PS_PE + R * ts_:PS_PE + R * (ts_ + 1)],
                                              vfm[:, c % 2, j, P * ts_:P * (ts_ + 1)],
                                              a0nb[:, j, :], start=(j == 0), stop=(j == KT - 1))
                        Tpe[c] = sT.inc(mm)

                    @blkA3.vector
                    def _(v, c=c):
                        if skip_small:
                            Dpi[c] = Dpe[c] = sD.inc(v.nop())
                            return
                        wt.wait(v, "v", Tpi[c])
                        v.tensor_copy(pib[:, 0, 0:R], ps[:, PS_PI:PS_PI + R])
                        Dpi[c] = sD.inc(v.tensor_copy(pib[:, 1, 0:R],
                                                      ps[:, PS_PI + R:PS_PI + 32]))
                        wt.wait(v, "v", Tpe[c])
                        Dpe[c] = sD.inc(v.tensor_copy(peb[:], ps[:, PS_PE:PS_PE + 32]))

                    @blkA3.tensor
                    def _(t, c=c):
                        wt.wait(t, "t", Aev.get((c - 1, 1)))
                        for ts_ in range(2):
                            base = PS_Y0 if ts_ == 0 else PS_Y1
                            mm = None
                            for h in range(2):
                                for j in range(KT):
                                    mm = t.matmul(ps[:, base + 512 * h:base + 512 * (h + 1)],
                                                  ufm[:, c % 2, j, P * ts_:P * (ts_ + 1)],
                                                  wbT[:, j, 512 * h:512 * (h + 1)],
                                                  start=(j == 0),
                                                  stop=(nolora and j == KT - 1),
                                                  skip_group_check=not nolora)
                            if nolora:
                                TY[(c, ts_)] = sT.inc(mm)
                        if SERIALIZE_EVAC:
                            wt.wait(t, "t", Aev.get((c, 1)))
                        if skip_small:
                            Tdb[c] = sT.inc(t.nop())
                            Tda[(c, 0)] = Tda[(c, 1)] = Tdb[c]
                            return
                        wt.wait(t, "t", Dpe[c])
                        if NO_SMALL_MM:
                            Tdb[c] = sT.inc(t.matmul(ps[:, PS_DB:PS_DB + 512],
                                                     utm[:, c % 2, 0, 0:P],
                                                     wbT[:, 0, 0:512], start=True, stop=True))
                            wt.wait(t, "t", Dpi[c])
                            for h in range(2):
                                off = PS_PI if h == 0 else PS_PE
                                Tda[(c, h)] = sT.inc(t.matmul(ps[:, off:off + 512],
                                                              pib[:, 0, :],
                                                              vtm[:, c % 2, 0, 512 * h:512 * (h + 1)],
                                                              start=True, stop=True))
                            return
                        mm = None
                        for j in range(KT):
                            for ts_ in range(2):
                                mm = t.matmul(ps[:, PS_DB + R * j:PS_DB + R * (j + 1)],
                                              utm[:, c % 2, ts_, P * j:P * (j + 1)],
                                              peb[:, R * ts_:R * (ts_ + 1)],
                                              start=(j == 0 and ts_ == 0),
                                              stop=(j == KT - 1 and ts_ == 1),
                                              skip_group_check=True)
                        Tdb[c] = sT.inc(mm)
                        wt.wait(t, "t", Dpi[c])
                        for h in range(2):
                            off = PS_PI if h == 0 else PS_PE
                            mm = None
                            for ts_ in range(2):
                                mm = t.matmul(ps[:, off:off + 512],
                                              pib[:, ts_, :],
                                              vtm[:, c % 2, ts_, 512 * h:512 * (h + 1)],
                                              start=(ts_ == 0), stop=(ts_ == 1),
                                              skip_group_check=True)
                            Tda[(c, h)] = sT.inc(mm)

                    @blkA3.vector
                    def _(v, c=c):
                        if skip_small:
                            Ddab[c] = sD.inc(v.nop())
                            return
                        wt.wait(v, "v", Tdb[c])
                        v.tensor_copy(dBc[:], ps[:, PS_DB:PS_DB + P])
                        wt.wait(v, "v", Tda[(c, 1)])
                        v.tensor_copy(dAc[0:R, 0:512], ps[0:R, PS_PI:PS_PI + 512])
                        Ddab[c] = sD.inc(
                            v.tensor_copy(dAc[0:R, 512:1024], ps[0:R, PS_PE:PS_PE + 512]))

                    @blkA3.scalar
                    def _(sc, c=c):
                        if skip_small:
                            An[c] = An[c + 100] = sA.inc(sc.nop())
                            return
                        if c == 0:
                            wt.wait(sc, "sc", d_zero[0])
                            wt.wait(sc, "sc", misc["d_mem"])
                        else:
                            wt.wait(sc, "sc", Dchain[c - 1])
                        An[c] = sA.inc(sc.activation(sqd[0:R, :], SAb[:], AF.Square,
                                                     accum_out=nab[0:R, :]))
                        sc.drain()
                        An[c + 100] = sA.inc(sc.activation(sqd[:, 0:P], SBb[:], AF.Square,
                                                           accum_out=nbb[:]))

                    @blkA3.vector
                    def _(v, c=c):
                        if skip_small:
                            An[(c, "cast")] = sD.inc(v.nop())
                            return
                        wt.wait(v, "v", An[c + 100])
                        v.tensor_copy(nabb[:, 0:1], nab[:])
                        An[(c, "cast")] = sD.inc(v.tensor_copy(nabb[:, 1:2], nbb[:]))

                    @blkA3.tensor
                    def _(t, c=c):
                        if skip_small:
                            Tn1[c] = sT.inc(t.nop())
                            return
                        wt.wait(t, "t", An[(c, "cast")])
                        wt.wait(t, "t", Amid.get(c - 1))
                        if NO_SMALL_MM:
                            Tn1[c] = sT.inc(t.matmul(ps[:, PS_N:PS_N + 512], onesbb[:],
                                                     aef[:, 0:512], start=True, stop=True))
                            return
                        t.matmul(ps[:, PS_N:PS_N + 1], onesbb[:], nabb[:, 0:1],
                                 start=True, stop=False, skip_group_check=True)
                        mm = t.matmul(ps[:, PS_N:PS_N + 1], onesbb[:], nabb[:, 1:2],
                                      start=False, stop=True, skip_group_check=True)
                        Tn1[c] = sT.inc(mm)

                    @blkA3.vector
                    def _(v, c=c):
                        if skip_small:
                            Dn1[c] = sD.inc(v.nop())
                            return
                        wt.wait(v, "v", Tn1[c])
                        Dn1[c] = sD.inc(v.tensor_copy(nsq[0:1, 0:1], ps[0:1, PS_N:PS_N + 1]))

                    @blkA3.scalar
                    def _(sc, c=c):
                        if skip_small:
                            An2[c] = sA.inc(sc.nop())
                            return
                        wt.wait(sc, "sc", Dn1[c])
                        An2[c] = sA.inc(sc.activation(nsq[0:1, 1:2], nsq[0:1, 0:1], AF.Sqrt,
                                                      bias=zcol[0:1, :], scale=1.0))

                    @blkA3.vector
                    def _(v, c=c):
                        if skip_small:
                            Dn2[c] = sD.inc(v.nop())
                            return
                        wt.wait(v, "v", An2[c])
                        v.tensor_scalar(nsq[0:1, 2:3], nsq[0:1, 1:2], LR, 1e-6,
                                        ALU.mult, ALU.add)
                        v.drain()
                        v.reciprocal(nsq[0:1, 3:4], nsq[0:1, 2:3])
                        v.drain()
                        v.tensor_scalar(nsq[0:1, 4:5], nsq[0:1, 3:4], 1.0, None, ALU.min)
                        v.drain()
                        v.tensor_scalar(ccb[0:1, 0:1], nsq[0:1, 4:5], -2.0 * LR, None, ALU.mult)
                        Dn2[c] = sD.inc(v.tensor_scalar(ccb[0:1, 1:2], nsq[0:1, 4:5],
                                                        -LR, None, ALU.mult))

                    @blkA3.tensor
                    def _(t, c=c):
                        if skip_small:
                            Tbc[c] = sT.inc(t.nop())
                            return
                        wt.wait(t, "t", Dn2[c])
                        if NO_SMALL_MM:
                            Tbc[c] = sT.inc(t.matmul(ps[:, PS_N:PS_N + 512], onesbb[:],
                                                     aef[:, 0:512], start=True, stop=True,
                                                     skip_group_check=True))
                            return
                        Tbc[c] = sT.inc(t.matmul(ps[:, PS_BC:PS_BC + 2], onesbb[:],
                                                 ccb[:, 0:2], start=True, stop=True,
                                                 skip_group_check=True))

                    @blkA3.vector
                    def _(v, c=c):
                        if skip_small:
                            Dchain[c] = sD.inc(v.nop())
                            return
                        wt.wait(v, "v", Tbc[c])
                        v.tensor_copy(bcb[:], ps[:, PS_BC:PS_BC + 2])
                        v.drain()
                        v.scalar_tensor_tensor(aef[0:R, :], SAb[:], bcb[0:R, 0:1], a0t2[:],
                                               ALU.mult, ALU.add)
                        v.scalar_tensor_tensor(bef[:, :, 0:R],
                                               SBb.rearrange("p (j r) -> p j r", r=R)[:],
                                               bcb[:, 1:2], b0tf[:],
                                               ALU.mult, ALU.add)
                        v.drain()
                        wt.wait(v, "v", Ddab[c])
                        v.tensor_tensor(SAb[:], SAb[:], dAc[:], ALU.add)
                        Dchain[c] = sD.inc(v.tensor_tensor(SBb[:], SBb[:], dBc[:], ALU.add))

                    @blkA3.tensor
                    def _(t, c=c):
                        if skip_small:
                            Tmid[c] = sT.inc(t.nop())
                            return
                        wt.wait(t, "t", Dchain[c])
                        if NO_SMALL_MM:
                            Tmid[c] = sT.inc(t.matmul(ps[:, PS_N:PS_N + 512], onesbb[:],
                                                      aef[:, 0:512], start=True, stop=True,
                                                      skip_group_check=True))
                            return
                        mm = None
                        for j in range(KT):
                            mm = t.matmul(ps[:, PS_MID:PS_MID + CH],
                                          bef[:, j, :],
                                          ufm[:, c % 2, j, :], start=(j == 0),
                                          stop=(j == KT - 1), skip_group_check=True)
                        Tmid[c] = sT.inc(mm)

                    @blkA3.scalar
                    def _(sc, c=c):
                        if skip_small:
                            Amid[c] = sA.inc(sc.nop())
                            return
                        wt.wait(sc, "sc", Tmid[c])
                        Amid[c] = sA.inc(sc.activation(midb[0:R, :],
                                                       ps[0:R, PS_MID:PS_MID + CH],
                                                       AF.Copy, bias=0.0, scale=1.0))

                    if not nolora:
                        @blkA3.tensor
                        def _(t, c=c):
                            wt.wait(t, "t", Amid[c])
                            for ts_ in range(2):
                                base = PS_Y0 if ts_ == 0 else PS_Y1
                                mm = None
                                for h in range(2):
                                    mm = t.matmul(ps[:, base + 512 * h:base + 512 * (h + 1)],
                                                  midb[:, P * ts_:P * (ts_ + 1)],
                                                  aef[:, 512 * h:512 * (h + 1)],
                                                  start=False, stop=True,
                                                  skip_group_check=True)
                                TY[(c, ts_)] = sT.inc(mm)

                    @blkA3.scalar
                    def _(sc, c=c):
                        for ts_ in range(2):
                            base = PS_Y0 if ts_ == 0 else PS_Y1
                            wt.wait(sc, "sc", TY[(c, 1)])
                            wt.wait(sc, "sc", Dx1.get((c - 1, ts_)))
                            Aev[(c, ts_)] = sA.inc(
                                sc.activation(ybf[:, ts_, :], ps[:, base:base + D],
                                              AF.Copy, accum_out=syb[:, ts_:ts_ + 1]))
                        for ts_ in range(2):
                            sc.drain()
                            Asq[(c, ts_)] = sA.inc(
                                sc.activation(sqd[:], ybf[:, ts_, :], AF.Square,
                                              accum_out=sy2b[:, ts_:ts_ + 1]))
                        if ao_out is not None:
                            aosv = ao_out[s].rearrange("(t p) k -> p t k", p=P)
                            aodump[0] = dma(sc, q_dbg, aosv[:, 2 * c:2 * c + 2, :],
                                            ybf[:, :, :])

                    @blkA3.vector
                    def _(v, c=c):
                        wt.wait(v, "v", Asq[(c, 1)])
                        v.tensor_scalar(lnc[:, 0:2], syb[:], 1.0 / D, None, ALU.mult)
                        v.drain()
                        v.tensor_scalar(lnc[:, 2:3], lnc[:, 0:1], lnc[:, 0:1], None, ALU.mult)
                        v.tensor_scalar(lnc[:, 3:4], lnc[:, 1:2], lnc[:, 1:2], None, ALU.mult)
                        v.drain()
                        v.tensor_scalar(lnc[:, 4:5], sy2b[:, 0:1], 1.0 / D, lnc[:, 2:3],
                                        ALU.mult, ALU.subtract)
                        Dln[c] = sD.inc(v.tensor_scalar(lnc[:, 5:6], sy2b[:, 1:2], 1.0 / D,
                                                        lnc[:, 3:4], ALU.mult, ALU.subtract))

                    @blkA3.scalar
                    def _(sc, c=c):
                        wt.wait(sc, "sc", Dln[c])
                        Aln[c] = sA.inc(sc.activation(lnc[:, 6:8], lnc[:, 4:6], AF.Sqrt,
                                                      bias=epscol[:], scale=1.0))

                    @blkA3.vector
                    def _(v, c=c):
                        wt.wait(v, "v", Aln[c])
                        v.reciprocal(lnc[:, 8:9], lnc[:, 6:7])
                        v.reciprocal(lnc[:, 9:10], lnc[:, 7:8])
                        v.drain()
                        wt.wait(v, "v", Gx[c])
                        wt.wait(v, "v", Gx1.get(c - 2))
                        if ao_out is not None:
                            wt.wait(v, "v", aodump[0])
                        for ts_ in range(2):
                            v.tensor_scalar(tpb[:, ts_, :], ybf[:, ts_, :], lnc[:, ts_:ts_ + 1],
                                            lnc[:, 8 + ts_:9 + ts_], ALU.subtract, ALU.mult)
                        v.drain()
                        for ts_ in range(2):
                            Dx1[(c, ts_)] = sD.inc(
                                v.tensor_tensor(x1b[:, c % 2, ts_, :], tpb[:, ts_, :],
                                                xck[:, c % 2, ts_, :], ALU.add))

                    @blkA3.scalar
                    def _(sc, c=c):
                        wt.wait(sc, "sc", Dx1[(c, 1)])
                        dma(sc, q_a3s[c % 2], x1_tmv[:, 2 * c:2 * c + 2, :],
                            x1b[:, c % 2, :, :])
                        if x1_out is not None:
                            x1sv = x1_out[s].rearrange("(t p) k -> p t k", p=P)
                            dma(sc, q_a3s[c % 2], x1sv[:, 2 * c:2 * c + 2, :],
                                x1b[:, c % 2, :, :])
                        Gx1[c] = (q_a3s[c % 2], q_a3s[c % 2].n)

                @blkA3.scalar
                def _(sc):
                    for qq in q_a3s:
                        wt.wait(sc, "sc", (qq, qq.n))

                Gx1_all = dict(Gx1)

            b = contextlib.ExitStack()
            with b:
                x1i = b.enter_context(nc.sbuf_tensor(f"bx1{s}", [P, 2, TSUB, D], F32))
                u2s = b.enter_context(nc.sbuf_tensor(f"bu2{s}", [P, 2, TSUB, D], BF16))
                u2f = b.enter_context(nc.sbuf_tensor(f"bu2f{s}", [P, 2, KT, FT], BF16,
                                                     is_transpose_dest=True))
                w0b = b.enter_context(nc.sbuf_tensor(f"bw0{s}", [P, 2, KT, CH], FP8))
                w1b = b.enter_context(nc.sbuf_tensor(f"bw1{s}", [P, 2, HT, P], FP8))
                w2b = b.enter_context(nc.sbuf_tensor(f"bw2{s}", [P, 2, 4, D], FP8))
                a0b = b.enter_context(nc.sbuf_tensor(f"ba0{s}", [P, HT, FT], BF16))
                a1b = b.enter_context(nc.sbuf_tensor(f"ba1{s}", [P, HT, FT], BF16))
                st6b = b.enter_context(nc.sbuf_tensor(f"bst{s}", [P, 12], F32))
                mv2 = b.enter_context(nc.sbuf_tensor(f"bmv{s}", [P, 2], F32))
                l2m = b.enter_context(nc.sbuf_tensor(f"bl2m{s}", [P, TSUB], F32))
                l2v = b.enter_context(nc.sbuf_tensor(f"bl2v{s}", [P, TSUB], F32))
                l2s = b.enter_context(nc.sbuf_tensor(f"bl2s{s}", [P, TSUB], F32))
                l2r = b.enter_context(nc.sbuf_tensor(f"bl2r{s}", [P, TSUB], F32))
                sqdB = b.enter_context(nc.sbuf_tensor(f"bsq{s}", [P, D], BF16))
                yb2 = b.enter_context(nc.sbuf_tensor(f"byb{s}", [P, 2, D], F32))
                oms = b.enter_context(nc.sbuf_tensor(f"bom{s}", [P, 12], F32))
                tpB = b.enter_context(nc.sbuf_tensor(f"btp{s}", [P, D], F32))
                obB = b.enter_context(nc.sbuf_tensor(f"bob{s}", [P, 2, D], F32))

                blkB = b.enter_context(nc.Block(f"b_{s}"))

                Gx1L = {}
                Dvar = {}
                Asr = {}
                Du2 = {}
                Su2 = {}
                Lu2f = {}
                Lw0 = {}
                Lw1 = {}
                Lw2 = {}
                Tm0 = {}
                Am0 = {}
                Tm1 = {}
                Am1 = {}
                Tm2g = {}
                Aev2 = {}
                Asq2 = {}
                Dst2 = {}
                Asr2 = {}
                Dob = {}
                Oob = {}
                NJG = HT // 2

                for fn in range(NFT):
                    @blkB.gpsimd
                    def _(g, fn=fn):
                        wt.wait(g, "g", Gx1_all[2 * fn + 1])
                        if fn >= 2:
                            wt.wait(g, "g", Du2.get(fn - 2))
                        Gx1L[(fn, "p")] = dma(g, q_bx[fn % 2], x1i[:, fn % 2, :, :],
                                              x1_tmv[:, TSUB * fn:TSUB * (fn + 1), :])

                    @blkB.vector
                    def _(v, fn=fn):
                        wt.wait(v, "v", Gx1L[(fn, "p")])
                        for ts_ in range(TSUB):
                            v.bn_stats(st6b[:, 0:6], x1i[:, fn % 2, ts_, 0:512])
                            v.bn_stats(st6b[:, 6:12], x1i[:, fn % 2, ts_, 512:1024])
                            v.drain()
                            v.bn_aggr(mv2[:], st6b[:])
                            v.drain()
                            v.tensor_copy(l2m[:, ts_:ts_ + 1], mv2[:, 0:1])
                            Dvar[(fn, ts_)] = sD.inc(
                                v.tensor_copy(l2v[:, ts_:ts_ + 1], mv2[:, 1:2]))

                    @blkB.scalar
                    def _(sc, fn=fn):
                        wt.wait(sc, "sc", Dvar[(fn, TSUB - 1)])
                        Asr[fn] = sA.inc(sc.activation(l2s[:], l2v[:], AF.Sqrt,
                                                       bias=epscol[:], scale=1.0))

                    @blkB.vector
                    def _(v, fn=fn):
                        wt.wait(v, "v", Asr[fn])
                        wt.wait(v, "v", Su2.get(fn - 2))
                        for ts_ in range(TSUB):
                            v.reciprocal(l2r[:, ts_:ts_ + 1], l2s[:, ts_:ts_ + 1])
                        v.drain()
                        ins = None
                        for ts_ in range(TSUB):
                            ins = v.tensor_scalar(u2s[:, fn % 2, ts_, :],
                                                  x1i[:, fn % 2, ts_, :],
                                                  l2m[:, ts_:ts_ + 1], l2r[:, ts_:ts_ + 1],
                                                  ALU.subtract, ALU.mult)
                        Du2[fn] = sD.inc(ins)

                    @blkB.scalar
                    def _(sc, fn=fn):
                        wt.wait(sc, "sc", Du2[fn])
                        Su2[fn] = dma(sc, q_bu[fn % 2],
                                      u2_tmv[:, TSUB * fn:TSUB * (fn + 1), :],
                                      u2s[:, fn % 2, :, :])

                for f in range(NFT):
                    fn = f
                    @blkB.sync
                    def _(sync, fn=fn):
                        wt.wait(sync, "sync", Su2[fn])
                        wt.wait(sync, "sync", Tm0.get((fn - 2, NJG - 1)))
                        for j in range(KT):
                            dma(sync, q_bf[fn % 2], u2f[:, fn % 2, j, :],
                                u2_dr[FT * fn:FT * (fn + 1), P * j:P * (j + 1)],
                                transpose=True)
                        Lu2f[fn] = (q_bf[fn % 2], q_bf[fn % 2].n)

                    @blkB.gpsimd
                    def _(g, f=f):
                        wt.wait(g, "g", Du2.get(NFT - 1))
                        wt.wait(g, "g", Dob.get((f - 2, TSUB - 1)))
                        Gx1L[f] = dma(g, q_bx[f % 2], x1i[:, f % 2, :, :],
                                      x1_tmv[:, TSUB * f:TSUB * (f + 1), :])

                    for jg in range(NJG):
                        @blkB.sync
                        def _(sync, f=f, jg=jg):
                            if f == 0 and jg == 0:
                                for tok in misc["k0_done"]:
                                    wt.wait(sync, "sync", tok)
                            pj = (f, jg - 2) if jg >= 2 else (f - 1, NJG - 2 + jg)
                            wt.wait(sync, "sync", Tm0.get(pj))
                            Lw0[(f, jg)] = dma(sync, q_bw0[jg % 2], w0b[:, jg % 2, :, :],
                                               w0_kt[:, :, CH * jg:CH * (jg + 1)])

                        @blkB.tensor
                        def _(t, f=f, jg=jg):
                            wt.wait(t, "t", Lw0[(f, jg)])
                            wt.wait(t, "t", Lu2f[f])
                            pj = (f, jg - 2) if jg >= 2 else (f - 1, NJG - 2 + jg)
                            wt.wait(t, "t", Am0.get(pj))
                            if jg < 2 and f >= 1:
                                wt.wait(t, "t", Am1.get((f - 1, HT - 1)))
                            base = PS_Y0 if jg % 2 == 0 else PS_Y1
                            mm = None
                            for jj in range(2):
                                for j in range(KT):
                                    mm = t.matmul(ps[:, base + 512 * jj:base + 512 * (jj + 1)],
                                                  w0b[:, jg % 2, j, P * jj:P * (jj + 1)],
                                                  u2f[:, f % 2, j, :],
                                                  start=(j == 0), stop=(j == KT - 1))
                            Tm0[(f, jg)] = sT.inc(mm)

                        @blkB.scalar
                        def _(sc, f=f, jg=jg):
                            wt.wait(sc, "sc", Tm0[(f, jg)])
                            if f >= 1:
                                wt.wait(sc, "sc", Tm1.get((f - 1, HT - 1)))
                            base = PS_Y0 if jg % 2 == 0 else PS_Y1
                            ins = None
                            for jj in range(2):
                                ins = sc.activation(a0b[:, 2 * jg + jj, :],
                                                    ps[:, base + 512 * jj:base + 512 * (jj + 1)],
                                                    AF.Relu, bias=zcol[:], scale=s0c2[:])
                            Am0[(f, jg)] = sA.inc(ins)

                    for jg in range(HT):
                        @blkB.sync
                        def _(sync, f=f, jg=jg):
                            if f == 0 and jg == 0:
                                for tok in misc["k1_done"]:
                                    wt.wait(sync, "sync", tok)
                            pj = (f, jg - 2) if jg >= 2 else (f - 1, HT - 2 + jg)
                            wt.wait(sync, "sync", Tm1.get(pj))
                            Lw1[(f, jg)] = dma(sync, q_bw1[jg % 2], w1b[:, jg % 2, :, :],
                                               w1_kt[:, :, P * jg:P * (jg + 1)])

                        @blkB.tensor
                        def _(t, f=f, jg=jg):
                            wt.wait(t, "t", Lw1[(f, jg)])
                            wt.wait(t, "t", Am0[(f, NJG - 1)])
                            if jg >= 2:
                                wt.wait(t, "t", Am1.get((f, jg - 2)))
                            base = PS_Y0 if jg % 2 == 0 else PS_Y1
                            mm = None
                            for k in range(HT):
                                mm = t.matmul(ps[:, base:base + 512],
                                              w1b[:, jg % 2, k, :],
                                              a0b[:, k, :],
                                              start=(k == 0), stop=(k == HT - 1))
                            Tm1[(f, jg)] = sT.inc(mm)

                        @blkB.scalar
                        def _(sc, f=f, jg=jg):
                            wt.wait(sc, "sc", Tm1[(f, jg)])
                            if f >= 1:
                                wt.wait(sc, "sc", Tm2g.get((f - 1, 15)))
                            base = PS_Y0 if jg % 2 == 0 else PS_Y1
                            ins = sc.activation(a1b[:, jg, :], ps[:, base:base + 512],
                                                AF.Relu, bias=zcol[:], scale=scola[:, 1:2])
                            Am1[(f, jg)] = sA.inc(ins)

                    def _m2_wave(wv):
                        for kg in range(8):
                            gi = 8 * wv + kg
                            @blkB.sync
                            def _(sync, f=f, gi=gi):
                                if f == 0 and gi == 0:
                                    for tok in misc["k2_done"]:
                                        wt.wait(sync, "sync", tok)
                                pj = (f, gi - 2) if gi >= 2 else (f - 1, 14 + gi)
                                wt.wait(sync, "sync", Tm2g.get(pj))
                                Lw2[(f, gi)] = dma(sync, q_bw2[gi % 2], w2b[:, gi % 2, :, :],
                                                   w2_kt[:, 4 * (gi % 8):4 * (gi % 8) + 4, :])

                            @blkB.tensor
                            def _(t, f=f, wv=wv, kg=kg, gi=gi):
                                wt.wait(t, "t", Lw2[(f, gi)])
                                wt.wait(t, "t", Am1[(f, HT - 1)])
                                if kg == 0 and wv == 0 and f >= 1:
                                    wt.wait(t, "t", Aev2.get((f - 1, TSUB - 1)))
                                if kg == 0 and wv == 1:
                                    wt.wait(t, "t", Aev2.get((f, 1)))
                                mm = None
                                for tw in range(2):
                                    ts_ = 2 * wv + tw
                                    for kk in range(4):
                                        k = 4 * kg + kk
                                        for h in range(2):
                                            mm = t.matmul(
                                                ps[:, 2048 + 1024 * tw + 512 * h:
                                                   2048 + 1024 * tw + 512 * (h + 1)],
                                                a1b[:, k, P * ts_:P * (ts_ + 1)],
                                                w2b[:, gi % 2, kk, 512 * h:512 * (h + 1)],
                                                start=(k == 0), stop=(k == HT - 1),
                                                skip_group_check=True)
                                Tm2g[(f, gi)] = sT.inc(mm)

                    def _ev(sc, f, ts_):
                        wt.wait(sc, "sc", Tm2g[(f, 7 if ts_ < 2 else 15)])
                        if f >= 1 or ts_ >= 2:
                            pj = (f, ts_ - 2) if ts_ >= 2 else (f - 1, TSUB - 2 + ts_)
                            wt.wait(sc, "sc", Dob.get(pj))
                        Aev2[(f, ts_)] = sA.inc(
                            sc.activation(yb2[:, ts_ % 2, :],
                                          ps[:, 2048 + 1024 * (ts_ % 2):2048 + 1024 * (ts_ % 2) + D],
                                          AF.Copy, scale=scola[:, 2:3],
                                          accum_out=oms[:, ts_:ts_ + 1]))

                    def _sq(sc, f, ts_):
                        sc.drain()
                        Asq2[(f, ts_)] = sA.inc(
                            sc.activation(sqdB[:], yb2[:, ts_ % 2, :], AF.Square,
                                          accum_out=oms[:, 4 + ts_:5 + ts_]))

                    def _dve_stats(v, f, ts_):
                        wt.wait(v, "v", Asq2[(f, ts_)])
                        v.tensor_scalar(oms[:, 8:9], oms[:, ts_:ts_ + 1], 1.0 / D,
                                        None, ALU.mult)
                        v.drain()
                        v.tensor_scalar(oms[:, 9:10], oms[:, 8:9], oms[:, 8:9],
                                        None, ALU.mult)
                        v.drain()
                        Dst2[(f, ts_)] = sD.inc(
                            v.tensor_scalar(oms[:, 10:11], oms[:, 4 + ts_:5 + ts_],
                                            1.0 / D, oms[:, 9:10], ALU.mult, ALU.subtract))

                    def _sqrt(sc, f, ts_):
                        wt.wait(sc, "sc", Dst2[(f, ts_)])
                        Asr2[(f, ts_)] = sA.inc(
                            sc.activation(oms[:, 11:12], oms[:, 10:11], AF.Sqrt,
                                          bias=epscol[:], scale=1.0))

                    def _out(v, f, ts_):
                        wt.wait(v, "v", Asr2[(f, ts_)])
                        wt.wait(v, "v", Gx1L[f])
                        v.reciprocal(oms[:, 10:11], oms[:, 11:12])
                        v.drain()
                        v.tensor_scalar(tpB[:], yb2[:, ts_ % 2, :], oms[:, 8:9],
                                        oms[:, 10:11], ALU.subtract, ALU.mult)
                        v.drain()
                        if f >= 1 or ts_ >= 2:
                            pj = (f, ts_ - 2) if ts_ >= 2 else (f - 1, TSUB - 2 + ts_)
                            wt.wait(v, "v", Oob.get(pj))
                        Dob[(f, ts_)] = sD.inc(
                            v.tensor_tensor(obB[:, ts_ % 2, :], tpB[:],
                                            x1i[:, f % 2, ts_, :], ALU.add))

                    def _store(g, f, ts_):
                        wt.wait(g, "g", Dob[(f, ts_)])
                        Oob[(f, ts_)] = dma(g, q_bo[ts_ % 2], y_s[:, TSUB * f + ts_, :],
                                            obB[:, ts_ % 2, :])

                    _m2_wave(0)

                    @blkB.scalar
                    def _(sc, f=f):
                        _ev(sc, f, 0)
                        _ev(sc, f, 1)

                    _m2_wave(1)

                    for ts_ in range(TSUB):
                        @blkB.scalar
                        def _(sc, f=f, ts_=ts_):
                            _sq(sc, f, ts_)

                        @blkB.vector
                        def _(v, f=f, ts_=ts_):
                            _dve_stats(v, f, ts_)

                        @blkB.scalar
                        def _(sc, f=f, ts_=ts_):
                            _sqrt(sc, f, ts_)

                        @blkB.vector
                        def _(v, f=f, ts_=ts_):
                            _out(v, f, ts_)

                        @blkB.gpsimd
                        def _(g, f=f, ts_=ts_):
                            _store(g, f, ts_)

                        if ts_ + 2 < TSUB:
                            @blkB.scalar
                            def _(sc, f=f, ts_=ts_):
                                _ev(sc, f, ts_ + 2)

                @blkB.gpsimd
                def _(g):
                    for qq in q_bo:
                        wt.wait(g, "g", (qq, qq.n))

                @blkB.scalar
                def _(sc):
                    for qq in q_bu + [q_dbg]:
                        wt.wait(sc, "sc", (qq, qq.n))

        fin = contextlib.ExitStack()
        with fin:
            blkF = fin.enter_context(nc.Block("fin"))

            @blkF.sync
            def _(sync):
                for qq in q_bo + [q_dbg] + q_a3s:
                    wt.wait(sync, "sync", (qq, qq.n))

    return nc


_NC_CACHE = {}


def _get_nc(debug_dump=None, nolora=False, skip_small=False):
    key = (debug_dump, nolora, skip_small)
    if key not in _NC_CACHE:
        _NC_CACHE[key] = build_program(debug_dump, nolora, skip_small)
    return _NC_CACHE[key]


def kernel(L0_attn_x, L0_attn_Wb, L0_attn_init_A, L0_attn_init_B, L0_attn_ln_gamma,
           L0_attn_ln_beta, L0_attn_Wp, L0_ffn_k0, L0_ffn_k1, L0_ffn_k2,
           L0_ffn_s0, L0_ffn_s1, L0_ffn_s2, _debug_dump=None, _nolora=False, _skip_small=False):
    x = np.ascontiguousarray(np.asarray(L0_attn_x, dtype=np.float32))
    wb = np.ascontiguousarray(np.asarray(L0_attn_Wb, dtype=np.float32))
    wp = np.ascontiguousarray(np.asarray(L0_attn_Wp, dtype=np.float32))
    a0 = np.ascontiguousarray(np.asarray(L0_attn_init_A, dtype=np.float32))
    b0 = np.ascontiguousarray(np.asarray(L0_attn_init_B, dtype=np.float32))
    gam = np.ascontiguousarray(np.asarray(L0_attn_ln_gamma, dtype=np.float32)).reshape(1, D)
    bet = np.ascontiguousarray(np.asarray(L0_attn_ln_beta, dtype=np.float32)).reshape(1, D)
    k0 = np.ascontiguousarray(np.asarray(L0_ffn_k0)).view(np.uint8)
    k1 = np.ascontiguousarray(np.asarray(L0_ffn_k1)).view(np.uint8)
    k2 = np.ascontiguousarray(np.asarray(L0_ffn_k2)).view(np.uint8)
    s0 = np.asarray(L0_ffn_s0, dtype=np.float32).reshape(1, 1)
    s1 = np.asarray(L0_ffn_s1, dtype=np.float32).reshape(1, 1)
    s2 = np.asarray(L0_ffn_s2, dtype=np.float32).reshape(1, 1)
    eye = np.eye(P, dtype=np.float32)
    ones = np.ones((P, P), dtype=np.float32)

    nc = _get_nc(_debug_dump, _nolora, _skip_small)
    in_maps = []
    for core in range(NCORES):
        in_maps.append(dict(
            x=np.ascontiguousarray(x[NSEQ * core:NSEQ * (core + 1)]),
            wb=wb, wp=wp, a0=a0, b0=b0, gam=gam, bet=bet,
            k0=k0, k1=k1, k2=k2, s0=s0, s1=s1, s2=s2, eye=eye, ones=ones,
        ))
    res = run_bass_kernel_spmd(nc, in_maps, core_ids=list(range(NCORES)))
    out = np.concatenate([r["y"] for r in res.results], axis=0)
    if _debug_dump == "x1":
        x1 = np.concatenate([r["x1dbg"] for r in res.results], axis=0)
        return out, x1
    if _debug_dump == "all":
        dbg = {"x1": np.concatenate([r["x1dbg"] for r in res.results], axis=0),
               "ao": np.concatenate([r["aodbg"] for r in res.results], axis=0)}
        dbg["cst"] = res.results[0]["cstdbg"]
        for k in ["u_dr", "xn_dr", "v_dr", "u2_dr", "x1_dr", "w0_dr", "wb_bf"]:
            dbg[k] = np.stack([r[k] for r in res.results], axis=0)
        return out, dbg
    return out


# version 08
# speedup vs baseline: 1.0513x; 1.0513x over previous
import numpy as np
import contextlib
import concourse.bass as bass
import concourse.mybir as mybir
from concourse.bass_utils import run_bass_kernel_spmd

F32 = mybir.dt.float32
BF16 = mybir.dt.bfloat16
U8 = mybir.dt.uint8
FP8 = mybir.dt.float8e4
AF = mybir.ActivationFunctionType
ALU = mybir.AluOpType

P = 128
D = 1024
KT = 8
S = 2048
NT = S // P
CH = 256
NCH = S // CH
R = 16
H1 = 4096
HT = H1 // P
FT = 512
NFT = S // FT
TSUB = FT // P
LR = 0.01 * 2.0
EPS = 1e-5
NSEQ = 2
NCORES = 8

PS_Y0 = 0
PS_Y1 = 1024
PS_PI = 2048
PS_PE = 2560
PS_DB = 3072
PS_N = 3584
PS_BC = 3586
PS_MID = 3600


class Sem:
    def __init__(self, stack, nc, name):
        self.h = stack.enter_context(nc.semaphore(name))
        self.n = 0

    def inc(self, instr, k=1):
        instr.then_inc(self.h, k)
        self.n += k
        return (self, self.n)


class Waiter:
    def __init__(self):
        self.seen = {}

    def wait(self, eng, ename, tok):
        if tok is None:
            return
        sem, val = tok
        if val is None or val <= 0:
            return
        key = (ename, id(sem))
        if self.seen.get(key, -1) >= val:
            return
        eng.wait_ge(sem.h, val)
        self.seen[key] = val


SERIALIZE_EVAC = False
NO_SMALL_MM = False


def build_program(debug_dump=None, nolora=False, skip_small=False):
    nc = bass.Bass("TRN2", target_bir_lowering=False)

    x_in = nc.dram_tensor("x", [NSEQ, S, D], F32, kind="ExternalInput").ap()
    wb_in = nc.dram_tensor("wb", [D, D], F32, kind="ExternalInput").ap()
    wp_in = nc.dram_tensor("wp", [D, D], F32, kind="ExternalInput").ap()
    a0_in = nc.dram_tensor("a0", [D, R], F32, kind="ExternalInput").ap()
    b0_in = nc.dram_tensor("b0", [R, D], F32, kind="ExternalInput").ap()
    gam_in = nc.dram_tensor("gam", [1, D], F32, kind="ExternalInput").ap()
    bet_in = nc.dram_tensor("bet", [1, D], F32, kind="ExternalInput").ap()
    k0_in = nc.dram_tensor("k0", [2 * D, H1], U8, kind="ExternalInput").ap()
    k1_in = nc.dram_tensor("k1", [H1, H1], U8, kind="ExternalInput").ap()
    k2_in = nc.dram_tensor("k2", [H1, D], U8, kind="ExternalInput").ap()
    s0_in = nc.dram_tensor("s0", [1, 1], F32, kind="ExternalInput").ap()
    s1_in = nc.dram_tensor("s1", [1, 1], F32, kind="ExternalInput").ap()
    s2_in = nc.dram_tensor("s2", [1, 1], F32, kind="ExternalInput").ap()
    eye_in = nc.dram_tensor("eye", [P, P], F32, kind="ExternalInput").ap()
    ones_in = nc.dram_tensor("ones", [P, P], F32, kind="ExternalInput").ap()
    y_out = nc.dram_tensor("y", [NSEQ, S, D], F32, kind="ExternalOutput").ap()
    x1_out = None
    ao_out = None
    if debug_dump in ("x1", "all"):
        x1_out = nc.dram_tensor("x1dbg", [NSEQ, S, D], F32, kind="ExternalOutput").ap()
    if debug_dump == "all":
        ao_out = nc.dram_tensor("aodbg", [NSEQ, S, D], F32, kind="ExternalOutput").ap()


    scratch_kind = "ExternalOutput" if debug_dump == "all" else "Internal"
    wb_bf = nc.dram_tensor("wb_bf", [D, D], BF16, kind=scratch_kind).ap()
    wp_bf = nc.dram_tensor("wp_bf", [D, D], BF16, kind="Internal").ap()
    w0_dr = nc.dram_tensor("w0_dr", [D, H1], FP8, kind=scratch_kind).ap()
    w1_dr = nc.dram_tensor("w1_dr", [H1, H1], FP8, kind="Internal").ap()
    w2_dr = nc.dram_tensor("w2_dr", [H1, D], FP8, kind="Internal").ap()
    u_dr = nc.dram_tensor("u_dr", [S, D], BF16, kind=scratch_kind).ap()
    xn_dr = nc.dram_tensor("xn_dr", [S, D], BF16, kind=scratch_kind).ap()
    v_dr = nc.dram_tensor("v_dr", [S, D], BF16, kind=scratch_kind).ap()
    u2_dr = nc.dram_tensor("u2_dr", [S, D], BF16, kind=scratch_kind).ap()
    x1_dr = nc.dram_tensor("x1_dr", [S, D], F32, kind=scratch_kind).ap()

    u_tmv = u_dr.rearrange("(t p) k -> p t k", p=P)
    v_tmv = v_dr.rearrange("(t p) k -> p t k", p=P)
    u2_tmv = u2_dr.rearrange("(t p) k -> p t k", p=P)
    x1_tmv = x1_dr.rearrange("(t p) k -> p t k", p=P)
    w0_kt = w0_dr.rearrange("(t p) j -> p t j", p=P)
    w1_kt = w1_dr.rearrange("(t p) j -> p t j", p=P)
    w2_kt = w2_dr.rearrange("(t p) j -> p t j", p=P)
    k2_kt = k2_in.rearrange("(t p) n -> p t n", p=P)

    st = contextlib.ExitStack()
    wt = Waiter()
    with st:
        eyeb = st.enter_context(nc.sbuf_tensor("eyeb", [P, P], F32))
        onesb = st.enter_context(nc.sbuf_tensor("onesb", [P, P], F32))
        onesbb = st.enter_context(nc.sbuf_tensor("onesbb", [P, P], BF16))
        grow = st.enter_context(nc.sbuf_tensor("grow", [P, D], F32))
        brow = st.enter_context(nc.sbuf_tensor("brow", [P, D], F32))
        grep = st.enter_context(nc.sbuf_tensor("grep", [P, D], F32))
        brep = st.enter_context(nc.sbuf_tensor("brep", [P, D], F32))
        spack = st.enter_context(nc.sbuf_tensor("spack", [P, 4], F32))
        scola = st.enter_context(nc.sbuf_tensor("scola", [P, 4], F32))
        s0c2 = st.enter_context(nc.sbuf_tensor("s0c2", [P, 1], F32))
        a0f = st.enter_context(nc.sbuf_tensor("a0f", [P, KT, R], F32))
        a0nb = st.enter_context(nc.sbuf_tensor("a0nb", [P, KT, R], BF16))
        a0t2 = st.enter_context(nc.sbuf_tensor("a0t2", [R, D], F32))
        b0f = st.enter_context(nc.sbuf_tensor("b0f", [R, D], F32))
        b0tf = st.enter_context(nc.sbuf_tensor("b0tf", [P, KT, R], F32))
        b0tb = st.enter_context(nc.sbuf_tensor("b0tb", [P, KT, R], BF16))
        zcol = st.enter_context(nc.sbuf_tensor("zcol", [P, 1], F32))
        epscol = st.enter_context(nc.sbuf_tensor("epscol", [P, 1], F32))
        zrow = st.enter_context(nc.sbuf_tensor("zrow", [1, D], BF16))
        SAb = st.enter_context(nc.sbuf_tensor("SAb", [R, D], F32))
        SBb = st.enter_context(nc.sbuf_tensor("SBb", [P, P], F32))

        ps = st.enter_context(nc.psum_tensor("psar", [P, 4096], F32))

        sT = Sem(st, nc, "sT")
        sD = Sem(st, nc, "sD")
        sA = Sem(st, nc, "sA")
        sP = Sem(st, nc, "sP")
        _semn = [0]

        def mk(tag):
            _semn[0] += 1
            return Sem(st, nc, f"q{_semn[0]}_{tag}")

        def dma(eng, sem, out, in_, transpose=False):
            return sem.inc(eng.dma_start(out=out, in_=in_, transpose=transpose), 16)

        q_cl = mk("cl")
        q_cvL = [mk("cvL0"), mk("cvL1")]
        q_cvS = [mk("cvS0"), mk("cvS1")]
        q_cvA = mk("cvA")
        q_cvAS = [mk("cvAS0"), mk("cvAS1")]
        q_cvP = mk("cvP")
        q_cvPS = [mk("cvPS0"), mk("cvPS1")]
        q_zr = mk("zr")
        q_a1x = [mk("a1x0"), mk("a1x1"), mk("a1x2")]
        q_a1u = [mk("a1u0"), mk("a1u1")]
        q_a1n = [mk("a1n0"), mk("a1n1")]
        q_a2wp = mk("a2wp")
        q_a2s = [mk("a2s0"), mk("a2s1")]
        q_a2v = [mk("a2v0"), mk("a2v1")]
        q_a3wb = mk("a3wb")
        q_a3u = [mk("a3u0"), mk("a3u1")]
        q_a3v = [mk("a3v0"), mk("a3v1")]
        q_a3x = [mk("a3x0"), mk("a3x1")]
        q_a3s = [mk("a3s0"), mk("a3s1")]
        q_bx = [mk("bx0"), mk("bx1")]
        q_bu = [mk("bu0"), mk("bu1")]
        q_bf = [mk("bf0"), mk("bf1")]
        q_bw0 = [mk("bw00"), mk("bw01")]
        q_bw1 = [mk("bw10"), mk("bw11")]
        q_bw2 = [mk("bw20"), mk("bw21")]
        q_bo = [mk("bo0"), mk("bo1")]
        q_dbg = mk("dbg")

        misc = {}

        for s in range(NSEQ):
            x_s = x_in[s].rearrange("(t p) k -> p t k", p=P)
            y_s = y_out[s].rearrange("(t p) k -> p t k", p=P)

            front = contextlib.ExitStack()
            with front:
                if s == 0:
                    cf32 = front.enter_context(nc.sbuf_tensor("p0cf32", [P, 2, D], F32))
                    cbf = front.enter_context(nc.sbuf_tensor("p0cbf", [P, 2, D], BF16))
                    u8a = front.enter_context(nc.sbuf_tensor("p0u8a", [P, 2, H1], U8))
                    u8b = front.enter_context(nc.sbuf_tensor("p0u8b", [P, 2, H1], U8))
                    f8d = front.enter_context(nc.sbuf_tensor("p0f8d", [P, 2, H1], FP8))
                    u8s = front.enter_context(nc.sbuf_tensor("p0u8s", [P, 2, H1], U8))
                    f8s = front.enter_context(nc.sbuf_tensor("p0f8s", [P, 2, H1], FP8))
                    u8p = front.enter_context(nc.sbuf_tensor("p0u8p", [P, 2, H1], U8))
                    f8p = front.enter_context(nc.sbuf_tensor("p0f8p", [P, 2, H1], FP8))
                xin = front.enter_context(nc.sbuf_tensor(f"a1xin{s}", [P, 3, D], F32))
                st6 = front.enter_context(nc.sbuf_tensor(f"a1st6{s}", [P, 12], F32))
                mvb = front.enter_context(nc.sbuf_tensor(f"a1mv{s}", [P, 2], F32))
                stdb = front.enter_context(nc.sbuf_tensor(f"a1std{s}", [P, 3], F32))
                rsb = front.enter_context(nc.sbuf_tensor(f"a1rs{s}", [P, 2], F32))
                acol = front.enter_context(nc.sbuf_tensor(f"a1ac{s}", [P, 1], F32))
                ubf = front.enter_context(nc.sbuf_tensor(f"a1ubf{s}", [P, 2, D], BF16))
                xtb = front.enter_context(nc.sbuf_tensor(f"a1xt{s}", [P, 2, D], F32))
                xgb = front.enter_context(nc.sbuf_tensor(f"a1xg{s}", [P, 2, D], F32))
                xnb = front.enter_context(nc.sbuf_tensor(f"a1xn{s}", [P, 2, D], BF16))

                blkF = front.enter_context(nc.Block(f"front{s}"))
                if s == 0:

                    @blkF.vector
                    def _(v):
                        v.memset(grow[:], 0.0)
                        v.memset(brow[:], 0.0)
                        misc["d_pad0"] = sD.inc(v.memset(spack[:], 0.0))

                    @blkF.sync
                    def _(sync):
                        wt.wait(sync, "sync", misc["d_pad0"])
                        for dst, src in [(eyeb[:], eye_in), (onesb[:], ones_in),
                                         (grow[0:1, :], gam_in), (brow[0:1, :], bet_in),
                                         (b0f[:], b0_in)]:
                            dma(sync, q_cl, dst, src)
                        dma(sync, q_cl, a0f[:], a0_in.rearrange("(j p) r -> p j r", p=P))
                        for i, sc_ in enumerate([s0_in, s1_in, s2_in]):
                            dma(sync, q_cl, spack[0:1, i:i + 1], sc_)
                        misc["Lconst"] = (q_cl, q_cl.n)

                    @blkF.vector
                    def _(v):
                        v.memset(zcol[:], 0.0)
                        v.memset(epscol[:], EPS)
                        v.memset(zrow[:], 0.0)
                        v.memset(SAb[:], 0.0)
                        v.memset(SBb[:], 0.0)
                        wt.wait(v, "v", misc["Lconst"])
                        v.tensor_copy(onesbb[:], onesb[:])
                        misc["d_mem"] = sD.inc(v.tensor_copy(a0nb[:], a0f[:]))

                    @blkF.gpsimd
                    def _(g):
                        wt.wait(g, "g", misc["d_mem"])
                        dma(g, q_zr, xn_dr[S - 1:S, :], zrow[0:1, :])
                        misc["zr"] = (q_zr, q_zr.n)

                    @blkF.tensor
                    def _(t):
                        wt.wait(t, "t", misc["Lconst"])
                        t.matmul(ps[:, 0:512], onesb[:], grow[:, 0:512], start=True, stop=True)
                        mm = t.matmul(ps[:, 512:1024], onesb[:], grow[:, 512:1024],
                                      start=True, stop=True)
                        misc["t_gam"] = sT.inc(mm)

                    @blkF.vector
                    def _(v):
                        wt.wait(v, "v", misc["t_gam"])
                        v.tensor_copy(grep[:, 0:512], ps[:, 0:512])
                        misc["d_gam"] = sD.inc(v.tensor_copy(grep[:, 512:1024], ps[:, 512:1024]))

                    @blkF.tensor
                    def _(t):
                        wt.wait(t, "t", misc["d_gam"])
                        t.matmul(ps[:, 0:512], onesb[:], brow[:, 0:512], start=True, stop=True)
                        mm = t.matmul(ps[:, 512:1024], onesb[:], brow[:, 512:1024],
                                      start=True, stop=True)
                        misc["t_bet"] = sT.inc(mm)

                    @blkF.vector
                    def _(v):
                        wt.wait(v, "v", misc["t_bet"])
                        v.tensor_copy(brep[:, 0:512], ps[:, 0:512])
                        misc["d_bet"] = sD.inc(v.tensor_copy(brep[:, 512:1024], ps[:, 512:1024]))

                    @blkF.tensor
                    def _(t):
                        wt.wait(t, "t", misc["d_bet"])
                        t.matmul(ps[:, PS_PI:PS_PI + 4], onesb[:], spack[:, 0:4],
                                 start=True, stop=True)
                        t.drain()
                        for j in range(KT):
                            t.transpose(ps[0:R, 2560 + 128 * j:2560 + 128 * (j + 1)],
                                        a0f[:, j, :], eyeb[:])
                        mm = None
                        for j in range(KT):
                            mm = t.transpose(ps[:, PS_N + R * j:PS_N + R * (j + 1)],
                                             b0f[0:R, P * j:P * (j + 1)], eyeb[0:R, 0:R])
                        misc["t_misc"] = sT.inc(mm)
                        t.drain()

                    @blkF.vector
                    def _(v):
                        wt.wait(v, "v", misc["t_misc"])
                        v.tensor_copy(scola[:], ps[:, PS_PI:PS_PI + 4])
                        v.tensor_copy(b0tf.rearrange("p j r -> p (j r)")[:], ps[:, PS_N:PS_N + KT * R])
                        v.drain()
                        v.tensor_scalar(s0c2[:], scola[:, 0:1], 2.0, None, ALU.mult)
                        misc["d_misc"] = sD.inc(
                            v.tensor_copy(b0tb.rearrange("p j r -> p (j r)")[:],
                                          ps[:, PS_N:PS_N + KT * R]))

                    @blkF.scalar
                    def _(sc):
                        sc.preload_act_table(AF.Sqrt)
                        wt.wait(sc, "sc", misc["t_misc"])
                        misc["a_misc"] = sA.inc(
                            sc.activation(a0t2[0:R, :], ps[0:R, 2560:3584], AF.Copy,
                                          bias=0.0, scale=2.0))

                    pool_tiles = ([("cast", wp_in, wp_bf, i) for i in range(8)]
                                  + [("cast", wb_in, wb_bf, i) for i in range(8)]
                                  + [("k2", None, None, i) for i in range(8)])
                    poolC = {}
                    poolS = {}
                    for i, (kind, psrc, pdst, j) in enumerate(pool_tiles):
                        @blkF.gpsimd
                        def _(g, i=i, kind=kind, psrc=psrc, j=j):
                            wt.wait(g, "g", poolS.get(i - 2))
                            if kind == "cast":
                                li = dma(g, q_cvP, cf32[:, i % 2, :], psrc[P * j:P * (j + 1), :])
                            else:
                                li = dma(g, q_cvP, u8p[:, i % 2, :].rearrange("p (t n) -> p t n", n=D),
                                         k2_kt[:, 4 * j:4 * (j + 1), :])
                            wt.wait(g, "g", li)
                            if kind == "cast":
                                poolC[i] = sP.inc(g.tensor_copy(cbf[:, i % 2, :], cf32[:, i % 2, :]))
                            else:
                                poolC[i] = sP.inc(g.tensor_scalar(f8p[:, i % 2, :], u8p[:, i % 2, :],
                                                                  4.0, -2.0, ALU.mult, ALU.add))

                        @blkF.sync
                        def _(sync, i=i, kind=kind, pdst=pdst, j=j):
                            wt.wait(sync, "sync", poolC[i])
                            if kind == "cast":
                                dma(sync, q_cvPS[i % 2], pdst[P * j:P * (j + 1), :], cbf[:, i % 2, :])
                            else:
                                dma(sync, q_cvPS[i % 2], w2_kt[:, 4 * j:4 * (j + 1), :],
                                    f8p[:, i % 2, :].rearrange("p (t n) -> p t n", n=D))
                            poolS[i] = (q_cvPS[i % 2], q_cvPS[i % 2].n)
                            if i == 7:
                                misc["wp_done"] = [(qq, qq.n) for qq in q_cvPS]
                            if i == 15:
                                misc["wb_done"] = [(qq, qq.n) for qq in q_cvPS]
                            if i == 23:
                                misc["k2_done"] = [(qq, qq.n) for qq in q_cvPS]

                    def emit_dve_conv():
                        dve_tiles = ([("k0", None, None, i) for i in range(8)]
                                     + [("aff", k1_in, w1_dr, i) for i in range(16)])
                        dveL = {}
                        dveC = {}
                        dveS = {}
                        for i, (kind, src2, dst, j) in enumerate(dve_tiles):
                            @blkF.sync
                            def _(sync, i=i, kind=kind, src2=src2, j=j):
                                wt.wait(sync, "sync", dveC.get(i - 2))
                                if kind == "k0":
                                    dma(sync, q_cvL[i % 2], u8a[:, i % 2, :], k0_in[P * j:P * (j + 1), :])
                                    dma(sync, q_cvL[i % 2], u8b[:, i % 2, :],
                                        k0_in[D + P * j:D + P * (j + 1), :])
                                else:
                                    dma(sync, q_cvL[i % 2], u8a[:, i % 2, :], src2[P * j:P * (j + 1), :])
                                dveL[i] = (q_cvL[i % 2], q_cvL[i % 2].n)

                            @blkF.vector
                            def _(v, i=i, kind=kind):
                                wt.wait(v, "v", dveL[i])
                                wt.wait(v, "v", dveS.get(i - 2))
                                if kind == "k0":
                                    ins = v.tensor_tensor(f8d[:, i % 2, :], u8a[:, i % 2, :],
                                                          u8b[:, i % 2, :], ALU.subtract)
                                else:
                                    ins = v.tensor_scalar(f8d[:, i % 2, :], u8a[:, i % 2, :],
                                                          4.0, -2.0, ALU.mult, ALU.add)
                                dveC[i] = sD.inc(ins)

                            @blkF.sync
                            def _(sync, i=i, kind=kind, dst=dst, j=j):
                                wt.wait(sync, "sync", dveC[i])
                                if kind == "k0":
                                    dma(sync, q_cvS[i % 2], w0_dr[P * j:P * (j + 1), :], f8d[:, i % 2, :])
                                else:
                                    dma(sync, q_cvS[i % 2], dst[P * j:P * (j + 1), :], f8d[:, i % 2, :])
                                dveS[i] = (q_cvS[i % 2], q_cvS[i % 2].n)
                                if i == 7:
                                    misc["k0_done"] = [(qq, qq.n) for qq in q_cvS]

                    def emit_act_conv():
                        act_tiles = [(k1_in, w1_dr, 16 + i) for i in range(16)]

                        @blkF.scalar
                        def _(sc):
                            prevS = {}
                            for i, (src2, dst, j) in enumerate(act_tiles):
                                wt.wait(sc, "sc", prevS.get(i - 2))
                                li = dma(sc, q_cvA, u8s[:, i % 2, :], src2[P * j:P * (j + 1), :])
                                wt.wait(sc, "sc", li)
                                sA.inc(sc.activation(f8s[:, i % 2, :], u8s[:, i % 2, :],
                                                     AF.Copy, bias=-2.0, scale=4.0))
                                sc.drain()
                                prevS[i] = dma(sc, q_cvAS[i % 2], dst[P * j:P * (j + 1), :],
                                               f8s[:, i % 2, :])
                Lx = {}
                Dst = {}
                Dub = {}
                Dxt = {}
                Ast = {}
                Pxn = {}
                Ust = {}
                Nst = {}

                for i in range(NT):
                    @blkF.sync
                    def _(sync, i=i):
                        wt.wait(sync, "sync", Dxt.get(i - 3))
                        Lx[i] = dma(sync, q_a1x[i % 3], xin[:, i % 3, :], x_s[:, i, :])

                    @blkF.vector
                    def _(v, i=i):
                        wt.wait(v, "v", Lx[i])
                        v.bn_stats(st6[:, 0:6], xin[:, i % 3, 0:512])
                        v.bn_stats(st6[:, 6:12], xin[:, i % 3, 512:1024])
                        v.drain()
                        Dst[i] = sD.inc(v.bn_aggr(mvb[:], st6[:]))

                    @blkF.scalar
                    def _(sc, i=i):
                        wt.wait(sc, "sc", Dst[i])
                        Ast[i] = sA.inc(sc.activation(stdb[:, 0:1], mvb[:, 1:2], AF.Sqrt,
                                                      bias=epscol[:], scale=1.0))

                    @blkF.vector
                    def _(v, i=i):
                        wt.wait(v, "v", Ast[i])
                        v.reciprocal(rsb[:, 0:1], stdb[:, 0:1])
                        v.drain()
                        v.tensor_scalar(stdb[:, 1:2], mvb[:, 1:2], rsb[:, 0:1], rsb[:, 0:1],
                                        ALU.mult, ALU.mult)
                        wt.wait(v, "v", Ust.get(i - 2))
                        Dub[i] = sD.inc(v.tensor_scalar(ubf[:, i % 2, :], xin[:, i % 3, :],
                                                        mvb[:, 0:1], rsb[:, 0:1],
                                                        ALU.subtract, ALU.mult))

                    @blkF.scalar
                    def _(sc, i=i):
                        wt.wait(sc, "sc", Dub[i])
                        Ast[i + 100] = sA.inc(sc.activation(stdb[:, 2:3], stdb[:, 1:2],
                                                            AF.Sqrt, bias=epscol[:], scale=1.0))

                    @blkF.vector
                    def _(v, i=i):
                        wt.wait(v, "v", Ast[i + 100])
                        v.reciprocal(rsb[:, 1:2], stdb[:, 2:3])
                        v.drain()
                        v.tensor_scalar(acol[:], rsb[:, 0:1], rsb[:, 1:2], None, ALU.mult)
                        v.drain()
                        Dxt[i] = sD.inc(v.tensor_scalar(xtb[:, i % 2, :], xin[:, i % 3, :],
                                                        mvb[:, 0:1], acol[:],
                                                        ALU.subtract, ALU.mult))
                        v.drain()
                        v.tensor_tensor(xgb[:, i % 2, :], xtb[:, i % 2, :],
                                        grep[:], ALU.mult)
                        v.drain()
                        wt.wait(v, "v", Nst.get(i - 2))
                        Pxn[i] = sD.inc(v.tensor_tensor(xnb[:, i % 2, :], xgb[:, i % 2, :],
                                                        brep[:], ALU.add))

                    @blkF.scalar
                    def _(sc, i=i):
                        wt.wait(sc, "sc", Dub[i])
                        Ust[i] = dma(sc, q_a1u[i % 2], u_tmv[:, i, :], ubf[:, i % 2, :])
                        wt.wait(sc, "sc", Pxn[i])
                        if i == 0:
                            Nst[i] = dma(sc, q_a1n[0], xn_dr[0:P - 1, :], xnb[1:P, 0, :])
                        else:
                            Nst[i] = dma(sc, q_a1n[i % 2],
                                         xn_dr[P * i - 1:P * (i + 1) - 1, :], xnb[:, i % 2, :])

                @blkF.scalar
                def _(sc):
                    for qq in q_a1u + q_a1n:
                        wt.wait(sc, "sc", (qq, qq.n))

                A1u_done = [(qq, qq.n) for qq in q_a1u]
                A1n_done = [(qq, qq.n) for qq in q_a1n]

                wpT = front.enter_context(nc.sbuf_tensor(f"a2wp{s}", [P, KT, D], BF16,
                                                         is_transpose_dest=True))
                shb = front.enter_context(nc.sbuf_tensor(f"a2sh{s}", [P, 2, KT, CH], BF16,
                                                         is_transpose_dest=True))
                vob = front.enter_context(nc.sbuf_tensor(f"a2vo{s}", [P, 2, 2, D], BF16))
                LwpT = [None]
                Lsh = {}
                Tv = {}
                Av = {}
                Sv = {}

                @blkF.sync
                def _(sync):
                    for tok in misc["wp_done"]:
                        wt.wait(sync, "sync", tok)
                    for j in range(KT):
                        dma(sync, q_a2wp, wpT[:, j, :], wp_bf[0:D, P * j:P * (j + 1)],
                            transpose=True)
                    LwpT[0] = (q_a2wp, q_a2wp.n)

                for c in range(NCH):
                    @blkF.sync
                    def _(sync, c=c):
                        wt.wait(sync, "sync", Tv.get((c - 2, 1)))
                        wt.wait(sync, "sync", Nst[2 * c + 1])
                        wt.wait(sync, "sync", Nst[min(2 * c + 2, NT - 1)])
                        wt.wait(sync, "sync", misc["zr"])
                        for j in range(KT):
                            dma(sync, q_a2s[c % 2], shb[:, c % 2, j, :],
                                xn_dr[CH * c:CH * (c + 1), P * j:P * (j + 1)], transpose=True)
                        Lsh[c] = (q_a2s[c % 2], q_a2s[c % 2].n)

                    @blkF.tensor
                    def _(t, c=c):
                        wt.wait(t, "t", Lsh[c])
                        wt.wait(t, "t", LwpT[0])
                        wt.wait(t, "t", Av.get((c - 1, 1)))
                        for ts_ in range(2):
                            base = PS_Y0 if ts_ == 0 else PS_Y1
                            for h in range(2):
                                mm = None
                                for j in range(KT):
                                    mm = t.matmul(ps[:, base + 512 * h:base + 512 * (h + 1)],
                                                  shb[:, c % 2, j, P * ts_:P * (ts_ + 1)],
                                                  wpT[:, j, 512 * h:512 * (h + 1)],
                                                  start=(j == 0), stop=(j == KT - 1))
                            Tv[(c, ts_)] = sT.inc(mm)

                    @blkF.scalar
                    def _(sc, c=c):
                        wt.wait(sc, "sc", Sv.get(c - 2))
                        for ts_ in range(2):
                            base = PS_Y0 if ts_ == 0 else PS_Y1
                            wt.wait(sc, "sc", Tv[(c, ts_)])
                            Av[(c, ts_)] = sA.inc(
                                sc.activation(vob[:, c % 2, ts_, :], ps[:, base:base + D],
                                              AF.Copy, bias=0.0, scale=1.0))
                        sc.drain()
                        for ts_ in range(2):
                            Sv[c] = dma(sc, q_a2v[c % 2], v_tmv[:, 2 * c + ts_, :],
                                        vob[:, c % 2, ts_, :])

                @blkF.scalar
                def _(sc):
                    for qq in q_a2v:
                        wt.wait(sc, "sc", (qq, qq.n))

                if s == 0:
                    emit_dve_conv()
                    emit_act_conv()
                    misc["k1_done"] = [(qq, qq.n) for qq in (q_cvS + q_cvAS)]
                    misc["conv_all"] = [(qq, qq.n) for qq in
                                        (q_cvS + q_cvAS + q_cvPS + [q_cvA, q_cvL[0], q_cvL[1], q_cvP])]

            SvD = dict(Sv)

            a3 = contextlib.ExitStack()
            Gx1_all = {}
            with a3:
                wbT = a3.enter_context(nc.sbuf_tensor(f"a3wb{s}", [P, KT, D], BF16,
                                                      is_transpose_dest=True))
                ufm = a3.enter_context(nc.sbuf_tensor(f"a3uf{s}", [P, 2, KT, CH], BF16,
                                                      is_transpose_dest=True))
                vfm = a3.enter_context(nc.sbuf_tensor(f"a3vf{s}", [P, 2, KT, CH], BF16,
                                                      is_transpose_dest=True))
                utm = a3.enter_context(nc.sbuf_tensor(f"a3ut{s}", [P, 2, 2, D], BF16))
                vtm = a3.enter_context(nc.sbuf_tensor(f"a3vt{s}", [P, 2, 2, D], BF16))
                xck = a3.enter_context(nc.sbuf_tensor(f"a3xc{s}", [P, 2, 2, D], F32))
                pib = a3.enter_context(nc.sbuf_tensor(f"a3pi{s}", [P, 2, P], BF16))
                peb = a3.enter_context(nc.sbuf_tensor(f"a3pe{s}", [P, 32], BF16))
                dAc = a3.enter_context(nc.sbuf_tensor(f"a3da{s}", [R, D], F32))
                dBc = a3.enter_context(nc.sbuf_tensor(f"a3db{s}", [P, P], F32))
                aef = a3.enter_context(nc.sbuf_tensor(f"a3ae{s}", [P, D], BF16))
                bef = a3.enter_context(nc.sbuf_tensor(f"a3be{s}", [P, KT, P], BF16))
                midb = a3.enter_context(nc.sbuf_tensor(f"a3mi{s}", [P, CH], BF16))
                nsq = a3.enter_context(nc.sbuf_tensor(f"a3nq{s}", [1, 8], F32))
                ccb = a3.enter_context(nc.sbuf_tensor(f"a3cc{s}", [P, 2], BF16))
                nab = a3.enter_context(nc.sbuf_tensor(f"a3na{s}", [P, 1], F32))
                nbb = a3.enter_context(nc.sbuf_tensor(f"a3nb{s}", [P, 1], F32))
                nabb = a3.enter_context(nc.sbuf_tensor(f"a3nab{s}", [P, 2], BF16))
                bcb = a3.enter_context(nc.sbuf_tensor(f"a3bc{s}", [P, 2], F32))
                sqd = a3.enter_context(nc.sbuf_tensor(f"a3sq{s}", [P, D], BF16))
                ybf = a3.enter_context(nc.sbuf_tensor(f"a3yb{s}", [P, 2, D], F32))
                syb = a3.enter_context(nc.sbuf_tensor(f"a3sy{s}", [P, 2], F32))
                sy2b = a3.enter_context(nc.sbuf_tensor(f"a3s2{s}", [P, 2], F32))
                lnc = a3.enter_context(nc.sbuf_tensor(f"a3ln{s}", [P, 10], F32))
                tpb = a3.enter_context(nc.sbuf_tensor(f"a3tp{s}", [P, 2, D], F32))
                x1b = a3.enter_context(nc.sbuf_tensor(f"a3x1{s}", [P, 2, 2, D], F32))

                blkA3 = a3.enter_context(nc.Block(f"a3_{s}"))

                LwbT = [None]
                Lu = {}
                Lv = {}
                Gx = {}
                Tpi = {}
                Tpe = {}
                Tdb = {}
                Tda = {}
                Tn1 = {}
                Tbc = {}
                Tmid = {}
                TY = {}
                Dpi = {}
                Dpe = {}
                Ddab = {}
                Dn1 = {}
                Dn2 = {}
                Dchain = {}
                Dx1 = {}
                An = {}
                An2 = {}
                Amid = {}
                Aev = {}
                Asq = {}
                Gx1 = {}
                Dln = {}
                Aln = {}
                d_zero = [None]
                aodump = [None]

                @blkA3.vector
                def _(v):
                    if s > 0:
                        v.memset(SAb[:], 0.0)
                        v.memset(SBb[:], 0.0)
                    v.memset(pib.rearrange("p a b -> p (a b)")[:], 0.0)
                    v.memset(bef.rearrange("p a b -> p (a b)")[:], 0.0)
                    v.memset(midb[:], 0.0)
                    v.memset(aef[:], 0.0)
                    v.memset(nab[:], 0.0)
                    v.memset(nabb[:], 0.0)
                    d_zero[0] = sD.inc(v.memset(ccb[:], 0.0))
                    v.drain()

                wbT_done = [False]

                def _load_wbT(sync):
                    for tok in misc["wb_done"]:
                        wt.wait(sync, "sync", tok)
                    for j in range(KT):
                        dma(sync, q_a3wb, wbT[:, j, :], wb_bf[0:D, P * j:P * (j + 1)],
                            transpose=True)
                    wt.wait(sync, "sync", (q_a3wb, q_a3wb.n))
                    LwbT[0] = sP.inc(sync.nop())
                    wbT_done[0] = True

                for c in range(NCH):
                    @blkA3.sync
                    def _(sync, c=c):
                        wt.wait(sync, "sync", Tmid.get(c - 2))
                        for tok in A1u_done:
                            wt.wait(sync, "sync", tok)
                        for j in range(KT):
                            dma(sync, q_a3u[c % 2], ufm[:, c % 2, j, :],
                                u_dr[CH * c:CH * (c + 1), P * j:P * (j + 1)], transpose=True)
                        tok = dma(sync, q_a3u[c % 2], utm[:, c % 2, :, :],
                                  u_tmv[:, 2 * c:2 * c + 2, :])
                        wt.wait(sync, "sync", tok)
                        Lu[c] = sP.inc(sync.nop())
                        if not wbT_done[0]:
                            _load_wbT(sync)

                    @blkA3.sync
                    def _(sync, c=c):
                        wt.wait(sync, "sync", Tda.get((c - 2, 1)))
                        wt.wait(sync, "sync", SvD[c])
                        for j in range(KT):
                            dma(sync, q_a3v[c % 2], vfm[:, c % 2, j, :],
                                v_dr[CH * c:CH * (c + 1), P * j:P * (j + 1)], transpose=True)
                        tok = dma(sync, q_a3v[c % 2], vtm[:, c % 2, :, :],
                                  v_tmv[:, 2 * c:2 * c + 2, :])
                        wt.wait(sync, "sync", tok)
                        Lv[c] = sP.inc(sync.nop())

                    @blkA3.scalar
                    def _(sc, c=c):
                        wt.wait(sc, "sc", Dx1.get((c - 2, 1)))
                        Gx[c] = dma(sc, q_a3x[c % 2], xck[:, c % 2, :, :],
                                    x_s[:, 2 * c:2 * c + 2, :])

                    @blkA3.tensor
                    def _(t, c=c):
                        wt.wait(t, "t", Lu[c])
                        wt.wait(t, "t", Lv[c])
                        wt.wait(t, "t", misc["d_misc"])
                        wt.wait(t, "t", misc["a_misc"])
                        wt.wait(t, "t", d_zero[0])
                        wt.wait(t, "t", Ddab.get(c - 1))
                        if skip_small:
                            Tpi[c] = sT.inc(t.nop())
                            Tpe[c] = Tpi[c]
                            return
                        if NO_SMALL_MM:
                            Tpi[c] = sT.inc(t.matmul(ps[:, PS_PI:PS_PI + 512],
                                                     ufm[:, c % 2, 0, 0:P],
                                                     wbT[:, 0, 0:512], start=True, stop=True))
                            Tpe[c] = sT.inc(t.matmul(ps[:, PS_PE:PS_PE + 512],
                                                     vfm[:, c % 2, 0, 0:P],
                                                     wbT[:, 0, 0:512], start=True, stop=True))
                            return
                        mm = None
                        for ts_ in range(2):
                            for j in range(KT):
                                mm = t.matmul(ps[:, PS_PI + R * ts_:PS_PI + R * (ts_ + 1)],
                                              ufm[:, c % 2, j, P * ts_:P * (ts_ + 1)],
                                              b0tb[:, j, :], start=(j == 0), stop=(j == KT - 1))
                        Tpi[c] = sT.inc(mm)
                        mm = None
                        for ts_ in range(2):
                            for j in range(KT):
                                mm = t.matmul(ps[:, PS_PE + R * ts_:PS_PE + R * (ts_ + 1)],
                                              vfm[:, c % 2, j, P * ts_:P * (ts_ + 1)],
                                              a0nb[:, j, :], start=(j == 0), stop=(j == KT - 1))
                        Tpe[c] = sT.inc(mm)

                    @blkA3.vector
                    def _(v, c=c):
                        if skip_small:
                            Dpi[c] = Dpe[c] = sD.inc(v.nop())
                            return
                        wt.wait(v, "v", Tpi[c])
                        v.tensor_copy(pib[:, 0, 0:R], ps[:, PS_PI:PS_PI + R])
                        Dpi[c] = sD.inc(v.tensor_copy(pib[:, 1, 0:R],
                                                      ps[:, PS_PI + R:PS_PI + 32]))
                        wt.wait(v, "v", Tpe[c])
                        Dpe[c] = sD.inc(v.tensor_copy(peb[:], ps[:, PS_PE:PS_PE + 32]))

                    @blkA3.tensor
                    def _(t, c=c):
                        wt.wait(t, "t", Aev.get((c - 1, 1)))
                        for ts_ in range(2):
                            base = PS_Y0 if ts_ == 0 else PS_Y1
                            mm = None
                            for h in range(2):
                                for j in range(KT):
                                    mm = t.matmul(ps[:, base + 512 * h:base + 512 * (h + 1)],
                                                  ufm[:, c % 2, j, P * ts_:P * (ts_ + 1)],
                                                  wbT[:, j, 512 * h:512 * (h + 1)],
                                                  start=(j == 0),
                                                  stop=(nolora and j == KT - 1),
                                                  skip_group_check=not nolora)
                            if nolora:
                                TY[(c, ts_)] = sT.inc(mm)
                        if SERIALIZE_EVAC:
                            wt.wait(t, "t", Aev.get((c, 1)))
                        if skip_small:
                            Tdb[c] = sT.inc(t.nop())
                            Tda[(c, 0)] = Tda[(c, 1)] = Tdb[c]
                            return
                        wt.wait(t, "t", Dpe[c])
                        if NO_SMALL_MM:
                            Tdb[c] = sT.inc(t.matmul(ps[:, PS_DB:PS_DB + 512],
                                                     utm[:, c % 2, 0, 0:P],
                                                     wbT[:, 0, 0:512], start=True, stop=True))
                            wt.wait(t, "t", Dpi[c])
                            for h in range(2):
                                off = PS_PI if h == 0 else PS_PE
                                Tda[(c, h)] = sT.inc(t.matmul(ps[:, off:off + 512],
                                                              pib[:, 0, :],
                                                              vtm[:, c % 2, 0, 512 * h:512 * (h + 1)],
                                                              start=True, stop=True))
                            return
                        mm = None
                        for j in range(KT):
                            for ts_ in range(2):
                                mm = t.matmul(ps[:, PS_DB + R * j:PS_DB + R * (j + 1)],
                                              utm[:, c % 2, ts_, P * j:P * (j + 1)],
                                              peb[:, R * ts_:R * (ts_ + 1)],
                                              start=(j == 0 and ts_ == 0),
                                              stop=(j == KT - 1 and ts_ == 1),
                                              skip_group_check=True)
                        Tdb[c] = sT.inc(mm)
                        wt.wait(t, "t", Dpi[c])
                        for h in range(2):
                            off = PS_PI if h == 0 else PS_PE
                            mm = None
                            for ts_ in range(2):
                                mm = t.matmul(ps[:, off:off + 512],
                                              pib[:, ts_, :],
                                              vtm[:, c % 2, ts_, 512 * h:512 * (h + 1)],
                                              start=(ts_ == 0), stop=(ts_ == 1),
                                              skip_group_check=True)
                            Tda[(c, h)] = sT.inc(mm)

                    @blkA3.vector
                    def _(v, c=c):
                        if skip_small:
                            Ddab[c] = sD.inc(v.nop())
                            return
                        wt.wait(v, "v", Tdb[c])
                        v.tensor_copy(dBc[:], ps[:, PS_DB:PS_DB + P])
                        wt.wait(v, "v", Tda[(c, 1)])
                        v.tensor_copy(dAc[0:R, 0:512], ps[0:R, PS_PI:PS_PI + 512])
                        Ddab[c] = sD.inc(
                            v.tensor_copy(dAc[0:R, 512:1024], ps[0:R, PS_PE:PS_PE + 512]))

                    @blkA3.scalar
                    def _(sc, c=c):
                        if skip_small:
                            An[c] = An[c + 100] = sA.inc(sc.nop())
                            return
                        if c == 0:
                            wt.wait(sc, "sc", d_zero[0])
                            wt.wait(sc, "sc", misc["d_mem"])
                        else:
                            wt.wait(sc, "sc", Dchain[c - 1])
                        An[c] = sA.inc(sc.activation(sqd[0:R, :], SAb[:], AF.Square,
                                                     accum_out=nab[0:R, :]))
                        sc.drain()
                        An[c + 100] = sA.inc(sc.activation(sqd[:, 0:P], SBb[:], AF.Square,
                                                           accum_out=nbb[:]))

                    @blkA3.vector
                    def _(v, c=c):
                        if skip_small:
                            An[(c, "cast")] = sD.inc(v.nop())
                            return
                        wt.wait(v, "v", An[c + 100])
                        v.tensor_copy(nabb[:, 0:1], nab[:])
                        An[(c, "cast")] = sD.inc(v.tensor_copy(nabb[:, 1:2], nbb[:]))

                    @blkA3.tensor
                    def _(t, c=c):
                        if skip_small:
                            Tn1[c] = sT.inc(t.nop())
                            return
                        wt.wait(t, "t", An[(c, "cast")])
                        wt.wait(t, "t", Amid.get(c - 1))
                        if NO_SMALL_MM:
                            Tn1[c] = sT.inc(t.matmul(ps[:, PS_N:PS_N + 512], onesbb[:],
                                                     aef[:, 0:512], start=True, stop=True))
                            return
                        t.matmul(ps[:, PS_N:PS_N + 1], onesbb[:], nabb[:, 0:1],
                                 start=True, stop=False, skip_group_check=True)
                        mm = t.matmul(ps[:, PS_N:PS_N + 1], onesbb[:], nabb[:, 1:2],
                                      start=False, stop=True, skip_group_check=True)
                        Tn1[c] = sT.inc(mm)

                    @blkA3.vector
                    def _(v, c=c):
                        if skip_small:
                            Dn1[c] = sD.inc(v.nop())
                            return
                        wt.wait(v, "v", Tn1[c])
                        Dn1[c] = sD.inc(v.tensor_copy(nsq[0:1, 0:1], ps[0:1, PS_N:PS_N + 1]))

                    @blkA3.scalar
                    def _(sc, c=c):
                        if skip_small:
                            An2[c] = sA.inc(sc.nop())
                            return
                        wt.wait(sc, "sc", Dn1[c])
                        An2[c] = sA.inc(sc.activation(nsq[0:1, 1:2], nsq[0:1, 0:1], AF.Sqrt,
                                                      bias=zcol[0:1, :], scale=1.0))

                    @blkA3.vector
                    def _(v, c=c):
                        if skip_small:
                            Dn2[c] = sD.inc(v.nop())
                            return
                        wt.wait(v, "v", An2[c])
                        v.tensor_scalar(nsq[0:1, 2:3], nsq[0:1, 1:2], LR, 1e-6,
                                        ALU.mult, ALU.add)
                        v.drain()
                        v.reciprocal(nsq[0:1, 3:4], nsq[0:1, 2:3])
                        v.drain()
                        v.tensor_scalar(nsq[0:1, 4:5], nsq[0:1, 3:4], 1.0, None, ALU.min)
                        v.drain()
                        v.tensor_scalar(ccb[0:1, 0:1], nsq[0:1, 4:5], -2.0 * LR, None, ALU.mult)
                        Dn2[c] = sD.inc(v.tensor_scalar(ccb[0:1, 1:2], nsq[0:1, 4:5],
                                                        -LR, None, ALU.mult))

                    @blkA3.tensor
                    def _(t, c=c):
                        if skip_small:
                            Tbc[c] = sT.inc(t.nop())
                            return
                        wt.wait(t, "t", Dn2[c])
                        if NO_SMALL_MM:
                            Tbc[c] = sT.inc(t.matmul(ps[:, PS_N:PS_N + 512], onesbb[:],
                                                     aef[:, 0:512], start=True, stop=True,
                                                     skip_group_check=True))
                            return
                        Tbc[c] = sT.inc(t.matmul(ps[:, PS_BC:PS_BC + 2], onesbb[:],
                                                 ccb[:, 0:2], start=True, stop=True,
                                                 skip_group_check=True))

                    @blkA3.vector
                    def _(v, c=c):
                        if skip_small:
                            Dchain[c] = sD.inc(v.nop())
                            return
                        wt.wait(v, "v", Tbc[c])
                        v.tensor_copy(bcb[:], ps[:, PS_BC:PS_BC + 2])
                        v.drain()
                        v.scalar_tensor_tensor(aef[0:R, :], SAb[:], bcb[0:R, 0:1], a0t2[:],
                                               ALU.mult, ALU.add)
                        v.scalar_tensor_tensor(bef[:, :, 0:R],
                                               SBb.rearrange("p (j r) -> p j r", r=R)[:],
                                               bcb[:, 1:2], b0tf[:],
                                               ALU.mult, ALU.add)
                        v.drain()
                        wt.wait(v, "v", Ddab[c])
                        v.tensor_tensor(SAb[:], SAb[:], dAc[:], ALU.add)
                        Dchain[c] = sD.inc(v.tensor_tensor(SBb[:], SBb[:], dBc[:], ALU.add))

                    @blkA3.tensor
                    def _(t, c=c):
                        if skip_small:
                            Tmid[c] = sT.inc(t.nop())
                            return
                        wt.wait(t, "t", Dchain[c])
                        if NO_SMALL_MM:
                            Tmid[c] = sT.inc(t.matmul(ps[:, PS_N:PS_N + 512], onesbb[:],
                                                      aef[:, 0:512], start=True, stop=True,
                                                      skip_group_check=True))
                            return
                        mm = None
                        for j in range(KT):
                            mm = t.matmul(ps[:, PS_MID:PS_MID + CH],
                                          bef[:, j, :],
                                          ufm[:, c % 2, j, :], start=(j == 0),
                                          stop=(j == KT - 1), skip_group_check=True)
                        Tmid[c] = sT.inc(mm)

                    @blkA3.scalar
                    def _(sc, c=c):
                        if skip_small:
                            Amid[c] = sA.inc(sc.nop())
                            return
                        wt.wait(sc, "sc", Tmid[c])
                        Amid[c] = sA.inc(sc.activation(midb[0:R, :],
                                                       ps[0:R, PS_MID:PS_MID + CH],
                                                       AF.Copy, bias=0.0, scale=1.0))

                    if not nolora:
                        @blkA3.tensor
                        def _(t, c=c):
                            wt.wait(t, "t", Amid[c])
                            for ts_ in range(2):
                                base = PS_Y0 if ts_ == 0 else PS_Y1
                                mm = None
                                for h in range(2):
                                    mm = t.matmul(ps[:, base + 512 * h:base + 512 * (h + 1)],
                                                  midb[:, P * ts_:P * (ts_ + 1)],
                                                  aef[:, 512 * h:512 * (h + 1)],
                                                  start=False, stop=True,
                                                  skip_group_check=True)
                                TY[(c, ts_)] = sT.inc(mm)

                    @blkA3.scalar
                    def _(sc, c=c):
                        for ts_ in range(2):
                            base = PS_Y0 if ts_ == 0 else PS_Y1
                            wt.wait(sc, "sc", TY[(c, 1)])
                            wt.wait(sc, "sc", Dx1.get((c - 1, ts_)))
                            Aev[(c, ts_)] = sA.inc(
                                sc.activation(ybf[:, ts_, :], ps[:, base:base + D],
                                              AF.Copy, accum_out=syb[:, ts_:ts_ + 1]))
                        for ts_ in range(2):
                            sc.drain()
                            Asq[(c, ts_)] = sA.inc(
                                sc.activation(sqd[:], ybf[:, ts_, :], AF.Square,
                                              accum_out=sy2b[:, ts_:ts_ + 1]))
                        if ao_out is not None:
                            aosv = ao_out[s].rearrange("(t p) k -> p t k", p=P)
                            aodump[0] = dma(sc, q_dbg, aosv[:, 2 * c:2 * c + 2, :],
                                            ybf[:, :, :])

                    @blkA3.vector
                    def _(v, c=c):
                        wt.wait(v, "v", Asq[(c, 1)])
                        v.tensor_scalar(lnc[:, 0:2], syb[:], 1.0 / D, None, ALU.mult)
                        v.drain()
                        v.tensor_scalar(lnc[:, 2:3], lnc[:, 0:1], lnc[:, 0:1], None, ALU.mult)
                        v.tensor_scalar(lnc[:, 3:4], lnc[:, 1:2], lnc[:, 1:2], None, ALU.mult)
                        v.drain()
                        v.tensor_scalar(lnc[:, 4:5], sy2b[:, 0:1], 1.0 / D, lnc[:, 2:3],
                                        ALU.mult, ALU.subtract)
                        Dln[c] = sD.inc(v.tensor_scalar(lnc[:, 5:6], sy2b[:, 1:2], 1.0 / D,
                                                        lnc[:, 3:4], ALU.mult, ALU.subtract))

                    @blkA3.scalar
                    def _(sc, c=c):
                        wt.wait(sc, "sc", Dln[c])
                        Aln[c] = sA.inc(sc.activation(lnc[:, 6:8], lnc[:, 4:6], AF.Sqrt,
                                                      bias=epscol[:], scale=1.0))

                    @blkA3.vector
                    def _(v, c=c):
                        wt.wait(v, "v", Aln[c])
                        v.reciprocal(lnc[:, 8:9], lnc[:, 6:7])
                        v.reciprocal(lnc[:, 9:10], lnc[:, 7:8])
                        v.drain()
                        wt.wait(v, "v", Gx[c])
                        wt.wait(v, "v", Gx1.get(c - 2))
                        if ao_out is not None:
                            wt.wait(v, "v", aodump[0])
                        for ts_ in range(2):
                            v.tensor_scalar(tpb[:, ts_, :], ybf[:, ts_, :], lnc[:, ts_:ts_ + 1],
                                            lnc[:, 8 + ts_:9 + ts_], ALU.subtract, ALU.mult)
                        v.drain()
                        for ts_ in range(2):
                            Dx1[(c, ts_)] = sD.inc(
                                v.tensor_tensor(x1b[:, c % 2, ts_, :], tpb[:, ts_, :],
                                                xck[:, c % 2, ts_, :], ALU.add))

                    @blkA3.scalar
                    def _(sc, c=c):
                        wt.wait(sc, "sc", Dx1[(c, 1)])
                        dma(sc, q_a3s[c % 2], x1_tmv[:, 2 * c:2 * c + 2, :],
                            x1b[:, c % 2, :, :])
                        if x1_out is not None:
                            x1sv = x1_out[s].rearrange("(t p) k -> p t k", p=P)
                            dma(sc, q_a3s[c % 2], x1sv[:, 2 * c:2 * c + 2, :],
                                x1b[:, c % 2, :, :])
                        Gx1[c] = (q_a3s[c % 2], q_a3s[c % 2].n)

                @blkA3.scalar
                def _(sc):
                    for qq in q_a3s:
                        wt.wait(sc, "sc", (qq, qq.n))

                Gx1_all = dict(Gx1)

            b = contextlib.ExitStack()
            with b:
                x1i = b.enter_context(nc.sbuf_tensor(f"bx1{s}", [P, 2, TSUB, D], F32))
                u2s = b.enter_context(nc.sbuf_tensor(f"bu2{s}", [P, 2, TSUB, D], BF16))
                u2f = b.enter_context(nc.sbuf_tensor(f"bu2f{s}", [P, 2, KT, FT], BF16,
                                                     is_transpose_dest=True))
                w0b = b.enter_context(nc.sbuf_tensor(f"bw0{s}", [P, 2, KT, CH], FP8))
                w1b = b.enter_context(nc.sbuf_tensor(f"bw1{s}", [P, 2, HT, P], FP8))
                w2b = b.enter_context(nc.sbuf_tensor(f"bw2{s}", [P, 2, 4, D], FP8))
                a0b = b.enter_context(nc.sbuf_tensor(f"ba0{s}", [P, HT, FT], BF16))
                a1b = b.enter_context(nc.sbuf_tensor(f"ba1{s}", [P, HT, FT], BF16))
                st6b = b.enter_context(nc.sbuf_tensor(f"bst{s}", [P, 12], F32))
                mv2 = b.enter_context(nc.sbuf_tensor(f"bmv{s}", [P, 2], F32))
                l2m = b.enter_context(nc.sbuf_tensor(f"bl2m{s}", [P, TSUB], F32))
                l2v = b.enter_context(nc.sbuf_tensor(f"bl2v{s}", [P, TSUB], F32))
                l2s = b.enter_context(nc.sbuf_tensor(f"bl2s{s}", [P, TSUB], F32))
                l2r = b.enter_context(nc.sbuf_tensor(f"bl2r{s}", [P, TSUB], F32))
                sqdB = b.enter_context(nc.sbuf_tensor(f"bsq{s}", [P, D], BF16))
                yb2 = b.enter_context(nc.sbuf_tensor(f"byb{s}", [P, 2, D], F32))
                oms = b.enter_context(nc.sbuf_tensor(f"bom{s}", [P, 12], F32))
                tpB = b.enter_context(nc.sbuf_tensor(f"btp{s}", [P, D], F32))
                obB = b.enter_context(nc.sbuf_tensor(f"bob{s}", [P, 2, D], F32))

                blkB = b.enter_context(nc.Block(f"b_{s}"))

                Gx1L = {}
                Dvar = {}
                Asr = {}
                Du2 = {}
                Su2 = {}
                Lu2f = {}
                Lw0 = {}
                Lw1 = {}
                Lw2 = {}
                Tm0 = {}
                Am0 = {}
                Tm1 = {}
                Am1 = {}
                Tm2g = {}
                Aev2 = {}
                Asq2 = {}
                Dst2 = {}
                Asr2 = {}
                Dob = {}
                Oob = {}
                NJG = HT // 2

                for fn in range(NFT):
                    @blkB.gpsimd
                    def _(g, fn=fn):
                        wt.wait(g, "g", Gx1_all[2 * fn + 1])
                        if fn >= 2:
                            wt.wait(g, "g", Du2.get(fn - 2))
                        Gx1L[(fn, "p")] = dma(g, q_bx[fn % 2], x1i[:, fn % 2, :, :],
                                              x1_tmv[:, TSUB * fn:TSUB * (fn + 1), :])

                    @blkB.vector
                    def _(v, fn=fn):
                        wt.wait(v, "v", Gx1L[(fn, "p")])
                        for ts_ in range(TSUB):
                            v.bn_stats(st6b[:, 0:6], x1i[:, fn % 2, ts_, 0:512])
                            v.bn_stats(st6b[:, 6:12], x1i[:, fn % 2, ts_, 512:1024])
                            v.drain()
                            v.bn_aggr(mv2[:], st6b[:])
                            v.drain()
                            v.tensor_copy(l2m[:, ts_:ts_ + 1], mv2[:, 0:1])
                            Dvar[(fn, ts_)] = sD.inc(
                                v.tensor_copy(l2v[:, ts_:ts_ + 1], mv2[:, 1:2]))

                    @blkB.scalar
                    def _(sc, fn=fn):
                        wt.wait(sc, "sc", Dvar[(fn, TSUB - 1)])
                        Asr[fn] = sA.inc(sc.activation(l2s[:], l2v[:], AF.Sqrt,
                                                       bias=epscol[:], scale=1.0))

                    @blkB.vector
                    def _(v, fn=fn):
                        wt.wait(v, "v", Asr[fn])
                        wt.wait(v, "v", Su2.get(fn - 2))
                        for ts_ in range(TSUB):
                            v.reciprocal(l2r[:, ts_:ts_ + 1], l2s[:, ts_:ts_ + 1])
                        v.drain()
                        ins = None
                        for ts_ in range(TSUB):
                            ins = v.tensor_scalar(u2s[:, fn % 2, ts_, :],
                                                  x1i[:, fn % 2, ts_, :],
                                                  l2m[:, ts_:ts_ + 1], l2r[:, ts_:ts_ + 1],
                                                  ALU.subtract, ALU.mult)
                        Du2[fn] = sD.inc(ins)

                    @blkB.scalar
                    def _(sc, fn=fn):
                        wt.wait(sc, "sc", Du2[fn])
                        Su2[fn] = dma(sc, q_bu[fn % 2],
                                      u2_tmv[:, TSUB * fn:TSUB * (fn + 1), :],
                                      u2s[:, fn % 2, :, :])

                for f in range(NFT):
                    fn = f
                    @blkB.sync
                    def _(sync, fn=fn):
                        wt.wait(sync, "sync", Su2[fn])
                        wt.wait(sync, "sync", Tm0.get((fn - 2, NJG - 1)))
                        for j in range(KT):
                            dma(sync, q_bf[fn % 2], u2f[:, fn % 2, j, :],
                                u2_dr[FT * fn:FT * (fn + 1), P * j:P * (j + 1)],
                                transpose=True)
                        Lu2f[fn] = (q_bf[fn % 2], q_bf[fn % 2].n)

                    @blkB.gpsimd
                    def _(g, f=f):
                        wt.wait(g, "g", Du2.get(NFT - 1))
                        wt.wait(g, "g", Dob.get((f - 2, TSUB - 1)))
                        Gx1L[f] = dma(g, q_bx[f % 2], x1i[:, f % 2, :, :],
                                      x1_tmv[:, TSUB * f:TSUB * (f + 1), :])

                    for jg in range(NJG):
                        @blkB.sync
                        def _(sync, f=f, jg=jg):
                            if f == 0 and jg == 0:
                                for tok in misc["k0_done"]:
                                    wt.wait(sync, "sync", tok)
                            pj = (f, jg - 2) if jg >= 2 else (f - 1, NJG - 2 + jg)
                            wt.wait(sync, "sync", Tm0.get(pj))
                            Lw0[(f, jg)] = dma(sync, q_bw0[jg % 2], w0b[:, jg % 2, :, :],
                                               w0_kt[:, :, CH * jg:CH * (jg + 1)])

                        @blkB.tensor
                        def _(t, f=f, jg=jg):
                            wt.wait(t, "t", Lw0[(f, jg)])
                            wt.wait(t, "t", Lu2f[f])
                            pj = (f, jg - 2) if jg >= 2 else (f - 1, NJG - 2 + jg)
                            wt.wait(t, "t", Am0.get(pj))
                            if jg < 2 and f >= 1:
                                wt.wait(t, "t", Am1.get((f - 1, HT - 1)))
                            base = PS_Y0 if jg % 2 == 0 else PS_Y1
                            mm = None
                            for jj in range(2):
                                for j in range(KT):
                                    mm = t.matmul(ps[:, base + 512 * jj:base + 512 * (jj + 1)],
                                                  w0b[:, jg % 2, j, P * jj:P * (jj + 1)],
                                                  u2f[:, f % 2, j, :],
                                                  start=(j == 0), stop=(j == KT - 1))
                            Tm0[(f, jg)] = sT.inc(mm)

                        @blkB.scalar
                        def _(sc, f=f, jg=jg):
                            wt.wait(sc, "sc", Tm0[(f, jg)])
                            if f >= 1:
                                wt.wait(sc, "sc", Tm1.get((f - 1, HT - 1)))
                            base = PS_Y0 if jg % 2 == 0 else PS_Y1
                            ins = None
                            for jj in range(2):
                                ins = sc.activation(a0b[:, 2 * jg + jj, :],
                                                    ps[:, base + 512 * jj:base + 512 * (jj + 1)],
                                                    AF.Relu, bias=zcol[:], scale=s0c2[:])
                            Am0[(f, jg)] = sA.inc(ins)

                    for jg in range(HT):
                        @blkB.sync
                        def _(sync, f=f, jg=jg):
                            if f == 0 and jg == 0:
                                for tok in misc["k1_done"]:
                                    wt.wait(sync, "sync", tok)
                            pj = (f, jg - 2) if jg >= 2 else (f - 1, HT - 2 + jg)
                            wt.wait(sync, "sync", Tm1.get(pj))
                            Lw1[(f, jg)] = dma(sync, q_bw1[jg % 2], w1b[:, jg % 2, :, :],
                                               w1_kt[:, :, P * jg:P * (jg + 1)])

                        @blkB.tensor
                        def _(t, f=f, jg=jg):
                            wt.wait(t, "t", Lw1[(f, jg)])
                            wt.wait(t, "t", Am0[(f, NJG - 1)])
                            if jg >= 2:
                                wt.wait(t, "t", Am1.get((f, jg - 2)))
                            base = PS_Y0 if jg % 2 == 0 else PS_Y1
                            mm = None
                            for k in range(HT):
                                mm = t.matmul(ps[:, base:base + 512],
                                              w1b[:, jg % 2, k, :],
                                              a0b[:, k, :],
                                              start=(k == 0), stop=(k == HT - 1))
                            Tm1[(f, jg)] = sT.inc(mm)

                        @blkB.scalar
                        def _(sc, f=f, jg=jg):
                            wt.wait(sc, "sc", Tm1[(f, jg)])
                            if f >= 1:
                                wt.wait(sc, "sc", Tm2g.get((f - 1, 15)))
                            base = PS_Y0 if jg % 2 == 0 else PS_Y1
                            ins = sc.activation(a1b[:, jg, :], ps[:, base:base + 512],
                                                AF.Relu, bias=zcol[:], scale=scola[:, 1:2])
                            Am1[(f, jg)] = sA.inc(ins)

                    def _m2_wave(wv):
                        for kg in range(8):
                            gi = 8 * wv + kg
                            @blkB.sync
                            def _(sync, f=f, gi=gi):
                                if f == 0 and gi == 0:
                                    for tok in misc["k2_done"]:
                                        wt.wait(sync, "sync", tok)
                                pj = (f, gi - 2) if gi >= 2 else (f - 1, 14 + gi)
                                wt.wait(sync, "sync", Tm2g.get(pj))
                                Lw2[(f, gi)] = dma(sync, q_bw2[gi % 2], w2b[:, gi % 2, :, :],
                                                   w2_kt[:, 4 * (gi % 8):4 * (gi % 8) + 4, :])

                            @blkB.tensor
                            def _(t, f=f, wv=wv, kg=kg, gi=gi):
                                wt.wait(t, "t", Lw2[(f, gi)])
                                wt.wait(t, "t", Am1[(f, HT - 1)])
                                if kg == 0 and wv == 0 and f >= 1:
                                    wt.wait(t, "t", Aev2.get((f - 1, TSUB - 1)))
                                if kg == 0 and wv == 1:
                                    wt.wait(t, "t", Aev2.get((f, 1)))
                                mm = None
                                for tw in range(2):
                                    ts_ = 2 * wv + tw
                                    for kk in range(4):
                                        k = 4 * kg + kk
                                        for h in range(2):
                                            mm = t.matmul(
                                                ps[:, 2048 + 1024 * tw + 512 * h:
                                                   2048 + 1024 * tw + 512 * (h + 1)],
                                                a1b[:, k, P * ts_:P * (ts_ + 1)],
                                                w2b[:, gi % 2, kk, 512 * h:512 * (h + 1)],
                                                start=(k == 0), stop=(k == HT - 1),
                                                skip_group_check=True)
                                Tm2g[(f, gi)] = sT.inc(mm)

                    def _ev(sc, f, ts_):
                        wt.wait(sc, "sc", Tm2g[(f, 7 if ts_ < 2 else 15)])
                        if f >= 1 or ts_ >= 2:
                            pj = (f, ts_ - 2) if ts_ >= 2 else (f - 1, TSUB - 2 + ts_)
                            wt.wait(sc, "sc", Dob.get(pj))
                        Aev2[(f, ts_)] = sA.inc(
                            sc.activation(yb2[:, ts_ % 2, :],
                                          ps[:, 2048 + 1024 * (ts_ % 2):2048 + 1024 * (ts_ % 2) + D],
                                          AF.Copy, scale=scola[:, 2:3],
                                          accum_out=oms[:, ts_:ts_ + 1]))

                    def _sq(sc, f, ts_):
                        sc.drain()
                        Asq2[(f, ts_)] = sA.inc(
                            sc.activation(sqdB[:], yb2[:, ts_ % 2, :], AF.Square,
                                          accum_out=oms[:, 4 + ts_:5 + ts_]))

                    def _dve_stats(v, f, ts_):
                        wt.wait(v, "v", Asq2[(f, ts_)])
                        v.tensor_scalar(oms[:, 8:9], oms[:, ts_:ts_ + 1], 1.0 / D,
                                        None, ALU.mult)
                        v.drain()
                        v.tensor_scalar(oms[:, 9:10], oms[:, 8:9], oms[:, 8:9],
                                        None, ALU.mult)
                        v.drain()
                        Dst2[(f, ts_)] = sD.inc(
                            v.tensor_scalar(oms[:, 10:11], oms[:, 4 + ts_:5 + ts_],
                                            1.0 / D, oms[:, 9:10], ALU.mult, ALU.subtract))

                    def _sqrt(sc, f, ts_):
                        wt.wait(sc, "sc", Dst2[(f, ts_)])
                        Asr2[(f, ts_)] = sA.inc(
                            sc.activation(oms[:, 11:12], oms[:, 10:11], AF.Sqrt,
                                          bias=epscol[:], scale=1.0))

                    def _out(v, f, ts_):
                        wt.wait(v, "v", Asr2[(f, ts_)])
                        wt.wait(v, "v", Gx1L[f])
                        v.reciprocal(oms[:, 10:11], oms[:, 11:12])
                        v.drain()
                        v.tensor_scalar(tpB[:], yb2[:, ts_ % 2, :], oms[:, 8:9],
                                        oms[:, 10:11], ALU.subtract, ALU.mult)
                        v.drain()
                        if f >= 1 or ts_ >= 2:
                            pj = (f, ts_ - 2) if ts_ >= 2 else (f - 1, TSUB - 2 + ts_)
                            wt.wait(v, "v", Oob.get(pj))
                        Dob[(f, ts_)] = sD.inc(
                            v.tensor_tensor(obB[:, ts_ % 2, :], tpB[:],
                                            x1i[:, f % 2, ts_, :], ALU.add))

                    def _store(g, f, ts_):
                        wt.wait(g, "g", Dob[(f, ts_)])
                        Oob[(f, ts_)] = dma(g, q_bo[ts_ % 2], y_s[:, TSUB * f + ts_, :],
                                            obB[:, ts_ % 2, :])

                    _m2_wave(0)

                    @blkB.scalar
                    def _(sc, f=f):
                        _ev(sc, f, 0)
                        _ev(sc, f, 1)

                    _m2_wave(1)

                    for ts_ in range(TSUB):
                        @blkB.scalar
                        def _(sc, f=f, ts_=ts_):
                            _sq(sc, f, ts_)

                        @blkB.vector
                        def _(v, f=f, ts_=ts_):
                            _dve_stats(v, f, ts_)

                        @blkB.scalar
                        def _(sc, f=f, ts_=ts_):
                            _sqrt(sc, f, ts_)

                        @blkB.vector
                        def _(v, f=f, ts_=ts_):
                            _out(v, f, ts_)

                        @blkB.gpsimd
                        def _(g, f=f, ts_=ts_):
                            _store(g, f, ts_)

                        if ts_ + 2 < TSUB:
                            @blkB.scalar
                            def _(sc, f=f, ts_=ts_):
                                _ev(sc, f, ts_ + 2)

                @blkB.gpsimd
                def _(g):
                    for qq in q_bo:
                        wt.wait(g, "g", (qq, qq.n))

                @blkB.scalar
                def _(sc):
                    for qq in q_bu + [q_dbg]:
                        wt.wait(sc, "sc", (qq, qq.n))

        fin = contextlib.ExitStack()
        with fin:
            blkF = fin.enter_context(nc.Block("fin"))

            @blkF.sync
            def _(sync):
                for qq in q_bo + [q_dbg] + q_a3s:
                    wt.wait(sync, "sync", (qq, qq.n))

    return nc


_NC_CACHE = {}


def _get_nc(debug_dump=None, nolora=False, skip_small=False):
    key = (debug_dump, nolora, skip_small)
    if key not in _NC_CACHE:
        _NC_CACHE[key] = build_program(debug_dump, nolora, skip_small)
    return _NC_CACHE[key]


def kernel(L0_attn_x, L0_attn_Wb, L0_attn_init_A, L0_attn_init_B, L0_attn_ln_gamma,
           L0_attn_ln_beta, L0_attn_Wp, L0_ffn_k0, L0_ffn_k1, L0_ffn_k2,
           L0_ffn_s0, L0_ffn_s1, L0_ffn_s2, _debug_dump=None, _nolora=False, _skip_small=False):
    x = np.ascontiguousarray(np.asarray(L0_attn_x, dtype=np.float32))
    wb = np.ascontiguousarray(np.asarray(L0_attn_Wb, dtype=np.float32))
    wp = np.ascontiguousarray(np.asarray(L0_attn_Wp, dtype=np.float32))
    a0 = np.ascontiguousarray(np.asarray(L0_attn_init_A, dtype=np.float32))
    b0 = np.ascontiguousarray(np.asarray(L0_attn_init_B, dtype=np.float32))
    gam = np.ascontiguousarray(np.asarray(L0_attn_ln_gamma, dtype=np.float32)).reshape(1, D)
    bet = np.ascontiguousarray(np.asarray(L0_attn_ln_beta, dtype=np.float32)).reshape(1, D)
    k0 = np.ascontiguousarray(np.asarray(L0_ffn_k0)).view(np.uint8)
    k1 = np.ascontiguousarray(np.asarray(L0_ffn_k1)).view(np.uint8)
    k2 = np.ascontiguousarray(np.asarray(L0_ffn_k2)).view(np.uint8)
    s0 = np.asarray(L0_ffn_s0, dtype=np.float32).reshape(1, 1)
    s1 = np.asarray(L0_ffn_s1, dtype=np.float32).reshape(1, 1)
    s2 = np.asarray(L0_ffn_s2, dtype=np.float32).reshape(1, 1)
    eye = np.eye(P, dtype=np.float32)
    ones = np.ones((P, P), dtype=np.float32)

    nc = _get_nc(_debug_dump, _nolora, _skip_small)
    in_maps = []
    for core in range(NCORES):
        in_maps.append(dict(
            x=np.ascontiguousarray(x[NSEQ * core:NSEQ * (core + 1)]),
            wb=wb, wp=wp, a0=a0, b0=b0, gam=gam, bet=bet,
            k0=k0, k1=k1, k2=k2, s0=s0, s1=s1, s2=s2, eye=eye, ones=ones,
        ))
    res = run_bass_kernel_spmd(nc, in_maps, core_ids=list(range(NCORES)))
    out = np.concatenate([r["y"] for r in res.results], axis=0)
    if _debug_dump == "x1":
        x1 = np.concatenate([r["x1dbg"] for r in res.results], axis=0)
        return out, x1
    if _debug_dump == "all":
        dbg = {"x1": np.concatenate([r["x1dbg"] for r in res.results], axis=0),
               "ao": np.concatenate([r["aodbg"] for r in res.results], axis=0)}
        dbg["cst"] = res.results[0]["cstdbg"]
        for k in ["u_dr", "xn_dr", "v_dr", "u2_dr", "x1_dr", "w0_dr", "wb_bf"]:
            dbg[k] = np.stack([r[k] for r in res.results], axis=0)
        return out, dbg
    return out


# version 09
# speedup vs baseline: 1.0840x; 1.0311x over previous
import numpy as np
import contextlib
import concourse.bass as bass
import concourse.mybir as mybir
from concourse.bass_utils import run_bass_kernel_spmd

F32 = mybir.dt.float32
BF16 = mybir.dt.bfloat16
U8 = mybir.dt.uint8
FP8 = mybir.dt.float8e4
AF = mybir.ActivationFunctionType
ALU = mybir.AluOpType

P = 128
D = 1024
KT = 8
S = 2048
NT = S // P
CH = 256
NCH = S // CH
R = 16
H1 = 4096
HT = H1 // P
FT = 512
NFT = S // FT
TSUB = FT // P
LR = 0.01 * 2.0
EPS = 1e-5
NSEQ = 2
NCORES = 8

PS_Y0 = 0
PS_Y1 = 1024
PS_PI = 2048
PS_PE = 2560
PS_DB = 3072
PS_N = 3584
PS_BC = 3586
PS_MID = 3600


class Sem:
    def __init__(self, stack, nc, name):
        self.h = stack.enter_context(nc.semaphore(name))
        self.n = 0

    def inc(self, instr, k=1):
        instr.then_inc(self.h, k)
        self.n += k
        return (self, self.n)


class Waiter:
    def __init__(self):
        self.seen = {}

    def wait(self, eng, ename, tok):
        if tok is None:
            return
        sem, val = tok
        if val is None or val <= 0:
            return
        key = (ename, id(sem))
        if self.seen.get(key, -1) >= val:
            return
        eng.wait_ge(sem.h, val)
        self.seen[key] = val


SERIALIZE_EVAC = False
NO_SMALL_MM = False


def build_program(debug_dump=None, nolora=False, skip_small=False):
    nc = bass.Bass("TRN2", target_bir_lowering=False)

    x_in = nc.dram_tensor("x", [NSEQ, S, D], F32, kind="ExternalInput").ap()
    wb_in = nc.dram_tensor("wb", [D, D], F32, kind="ExternalInput").ap()
    wp_in = nc.dram_tensor("wp", [D, D], F32, kind="ExternalInput").ap()
    a0_in = nc.dram_tensor("a0", [D, R], F32, kind="ExternalInput").ap()
    b0_in = nc.dram_tensor("b0", [R, D], F32, kind="ExternalInput").ap()
    gam_in = nc.dram_tensor("gam", [1, D], F32, kind="ExternalInput").ap()
    bet_in = nc.dram_tensor("bet", [1, D], F32, kind="ExternalInput").ap()
    k0_in = nc.dram_tensor("k0", [2 * D, H1], U8, kind="ExternalInput").ap()
    k1_in = nc.dram_tensor("k1", [H1, H1], U8, kind="ExternalInput").ap()
    k2_in = nc.dram_tensor("k2", [H1, D], U8, kind="ExternalInput").ap()
    s0_in = nc.dram_tensor("s0", [1, 1], F32, kind="ExternalInput").ap()
    s1_in = nc.dram_tensor("s1", [1, 1], F32, kind="ExternalInput").ap()
    s2_in = nc.dram_tensor("s2", [1, 1], F32, kind="ExternalInput").ap()
    eye_in = nc.dram_tensor("eye", [P, P], F32, kind="ExternalInput").ap()
    ones_in = nc.dram_tensor("ones", [P, P], F32, kind="ExternalInput").ap()
    y_out = nc.dram_tensor("y", [NSEQ, S, D], F32, kind="ExternalOutput").ap()
    x1_out = None
    ao_out = None
    if debug_dump in ("x1", "all"):
        x1_out = nc.dram_tensor("x1dbg", [NSEQ, S, D], F32, kind="ExternalOutput").ap()
    if debug_dump == "all":
        ao_out = nc.dram_tensor("aodbg", [NSEQ, S, D], F32, kind="ExternalOutput").ap()


    scratch_kind = "ExternalOutput" if debug_dump == "all" else "Internal"
    wb_bf = nc.dram_tensor("wb_bf", [D, D], BF16, kind=scratch_kind).ap()
    wp_bf = nc.dram_tensor("wp_bf", [D, D], BF16, kind="Internal").ap()
    w0_dr = nc.dram_tensor("w0_dr", [D, H1], FP8, kind=scratch_kind).ap()
    w1_dr = nc.dram_tensor("w1_dr", [H1, H1], FP8, kind="Internal").ap()
    w2_dr = nc.dram_tensor("w2_dr", [H1, D], FP8, kind="Internal").ap()
    u_dr = nc.dram_tensor("u_dr", [S, D], BF16, kind=scratch_kind).ap()
    xn_dr = nc.dram_tensor("xn_dr", [S, D], BF16, kind=scratch_kind).ap()
    v_dr = nc.dram_tensor("v_dr", [S, D], BF16, kind=scratch_kind).ap()
    u2_dr = nc.dram_tensor("u2_dr", [S, D], BF16, kind=scratch_kind).ap()
    x1_dr = nc.dram_tensor("x1_dr", [S, D], F32, kind=scratch_kind).ap()

    u_tmv = u_dr.rearrange("(t p) k -> p t k", p=P)
    v_tmv = v_dr.rearrange("(t p) k -> p t k", p=P)
    u2_tmv = u2_dr.rearrange("(t p) k -> p t k", p=P)
    x1_tmv = x1_dr.rearrange("(t p) k -> p t k", p=P)
    w0_kt = w0_dr.rearrange("(t p) j -> p t j", p=P)
    w1_kt = w1_dr.rearrange("(t p) j -> p t j", p=P)
    w2_kt = w2_dr.rearrange("(t p) j -> p t j", p=P)
    k2_kt = k2_in.rearrange("(t p) n -> p t n", p=P)

    st = contextlib.ExitStack()
    wt = Waiter()
    with st:
        eyeb = st.enter_context(nc.sbuf_tensor("eyeb", [P, P], F32))
        onesb = st.enter_context(nc.sbuf_tensor("onesb", [P, P], F32))
        onesbb = st.enter_context(nc.sbuf_tensor("onesbb", [P, P], BF16))
        grow = st.enter_context(nc.sbuf_tensor("grow", [P, D], F32))
        brow = st.enter_context(nc.sbuf_tensor("brow", [P, D], F32))
        grep = st.enter_context(nc.sbuf_tensor("grep", [P, D], F32))
        brep = st.enter_context(nc.sbuf_tensor("brep", [P, D], F32))
        spack = st.enter_context(nc.sbuf_tensor("spack", [P, 4], F32))
        scola = st.enter_context(nc.sbuf_tensor("scola", [P, 4], F32))
        s0c2 = st.enter_context(nc.sbuf_tensor("s0c2", [P, 1], F32))
        a0f = st.enter_context(nc.sbuf_tensor("a0f", [P, KT, R], F32))
        a0nb = st.enter_context(nc.sbuf_tensor("a0nb", [P, KT, R], BF16))
        a0t2 = st.enter_context(nc.sbuf_tensor("a0t2", [R, D], F32))
        b0f = st.enter_context(nc.sbuf_tensor("b0f", [R, D], F32))
        b0tf = st.enter_context(nc.sbuf_tensor("b0tf", [P, KT, R], F32))
        b0tb = st.enter_context(nc.sbuf_tensor("b0tb", [P, KT, R], BF16))
        zcol = st.enter_context(nc.sbuf_tensor("zcol", [P, 1], F32))
        epscol = st.enter_context(nc.sbuf_tensor("epscol", [P, 1], F32))
        zrow = st.enter_context(nc.sbuf_tensor("zrow", [1, D], BF16))
        SAb = st.enter_context(nc.sbuf_tensor("SAb", [R, D], F32))
        SBb = st.enter_context(nc.sbuf_tensor("SBb", [P, P], F32))

        ps = st.enter_context(nc.psum_tensor("psar", [P, 4096], F32))

        sT = Sem(st, nc, "sT")
        sD = Sem(st, nc, "sD")
        sA = Sem(st, nc, "sA")
        sP = Sem(st, nc, "sP")
        _semn = [0]

        def mk(tag):
            _semn[0] += 1
            return Sem(st, nc, f"q{_semn[0]}_{tag}")

        def dma(eng, sem, out, in_, transpose=False):
            return sem.inc(eng.dma_start(out=out, in_=in_, transpose=transpose), 16)

        q_cl = mk("cl")
        q_cvL = [mk("cvL0"), mk("cvL1")]
        q_cvS = [mk("cvS0"), mk("cvS1")]
        q_cvA = mk("cvA")
        q_cvAS = [mk("cvAS0"), mk("cvAS1")]
        q_cvP = mk("cvP")
        q_cvPS = [mk("cvPS0"), mk("cvPS1")]
        q_zr = mk("zr")
        q_a1x = [mk("a1x0"), mk("a1x1"), mk("a1x2")]
        q_a1u = [mk("a1u0"), mk("a1u1")]
        q_a1n = [mk("a1n0"), mk("a1n1")]
        q_a2wp = mk("a2wp")
        q_a2s = [mk("a2s0"), mk("a2s1")]
        q_a2v = [mk("a2v0"), mk("a2v1")]
        q_a3wb = mk("a3wb")
        q_a3u = [mk("a3u0"), mk("a3u1")]
        q_a3v = [mk("a3v0"), mk("a3v1")]
        q_a3x = [mk("a3x0"), mk("a3x1")]
        q_a3s = [mk("a3s0"), mk("a3s1")]
        q_bx = [mk("bx0"), mk("bx1")]
        q_bu = [mk("bu0"), mk("bu1")]
        q_bf = [mk("bf0"), mk("bf1")]
        q_bw0 = [mk("bw00"), mk("bw01")]
        q_bw1 = [mk("bw10"), mk("bw11")]
        q_bw2 = [mk("bw20"), mk("bw21")]
        q_bo = [mk("bo0"), mk("bo1")]
        q_dbg = mk("dbg")

        misc = {}

        for s in range(NSEQ):
            x_s = x_in[s].rearrange("(t p) k -> p t k", p=P)
            y_s = y_out[s].rearrange("(t p) k -> p t k", p=P)

            front = contextlib.ExitStack()
            with front:
                if s == 0:
                    cf32 = front.enter_context(nc.sbuf_tensor("p0cf32", [P, 2, D], F32))
                    cbf = front.enter_context(nc.sbuf_tensor("p0cbf", [P, 2, D], BF16))
                    u8a = front.enter_context(nc.sbuf_tensor("p0u8a", [P, 2, H1], U8))
                    u8b = front.enter_context(nc.sbuf_tensor("p0u8b", [P, 2, H1], U8))
                    f8d = front.enter_context(nc.sbuf_tensor("p0f8d", [P, 2, H1], FP8))
                    u8s = front.enter_context(nc.sbuf_tensor("p0u8s", [P, 2, H1], U8))
                    f8s = front.enter_context(nc.sbuf_tensor("p0f8s", [P, 2, H1], FP8))
                    u8p = front.enter_context(nc.sbuf_tensor("p0u8p", [P, 2, H1], U8))
                    f8p = front.enter_context(nc.sbuf_tensor("p0f8p", [P, 2, H1], FP8))
                xin = front.enter_context(nc.sbuf_tensor(f"a1xin{s}", [P, 3, D], F32))
                st6 = front.enter_context(nc.sbuf_tensor(f"a1st6{s}", [P, 12], F32))
                mvb = front.enter_context(nc.sbuf_tensor(f"a1mv{s}", [P, 2], F32))
                stdb = front.enter_context(nc.sbuf_tensor(f"a1std{s}", [P, 3], F32))
                rsb = front.enter_context(nc.sbuf_tensor(f"a1rs{s}", [P, 2], F32))
                acol = front.enter_context(nc.sbuf_tensor(f"a1ac{s}", [P, 1], F32))
                ubf = front.enter_context(nc.sbuf_tensor(f"a1ubf{s}", [P, 2, D], BF16))
                xtb = front.enter_context(nc.sbuf_tensor(f"a1xt{s}", [P, 2, D], F32))
                xgb = front.enter_context(nc.sbuf_tensor(f"a1xg{s}", [P, 2, D], F32))
                xnb = front.enter_context(nc.sbuf_tensor(f"a1xn{s}", [P, 2, D], BF16))

                blkF = front.enter_context(nc.Block(f"front{s}"))
                if s == 0:

                    @blkF.vector
                    def _(v):
                        v.memset(grow[:], 0.0)
                        v.memset(brow[:], 0.0)
                        misc["d_pad0"] = sD.inc(v.memset(spack[:], 0.0))

                    @blkF.sync
                    def _(sync):
                        wt.wait(sync, "sync", misc["d_pad0"])
                        for dst, src in [(eyeb[:], eye_in), (onesb[:], ones_in),
                                         (grow[0:1, :], gam_in), (brow[0:1, :], bet_in),
                                         (b0f[:], b0_in)]:
                            dma(sync, q_cl, dst, src)
                        dma(sync, q_cl, a0f[:], a0_in.rearrange("(j p) r -> p j r", p=P))
                        for i, sc_ in enumerate([s0_in, s1_in, s2_in]):
                            dma(sync, q_cl, spack[0:1, i:i + 1], sc_)
                        misc["Lconst"] = (q_cl, q_cl.n)

                    @blkF.vector
                    def _(v):
                        v.memset(zcol[:], 0.0)
                        v.memset(epscol[:], EPS)
                        v.memset(zrow[:], 0.0)
                        v.memset(SAb[:], 0.0)
                        v.memset(SBb[:], 0.0)
                        wt.wait(v, "v", misc["Lconst"])
                        v.tensor_copy(onesbb[:], onesb[:])
                        misc["d_mem"] = sD.inc(v.tensor_copy(a0nb[:], a0f[:]))

                    @blkF.gpsimd
                    def _(g):
                        wt.wait(g, "g", misc["d_mem"])
                        dma(g, q_zr, xn_dr[S - 1:S, :], zrow[0:1, :])
                        misc["zr"] = (q_zr, q_zr.n)

                    @blkF.tensor
                    def _(t):
                        wt.wait(t, "t", misc["Lconst"])
                        t.matmul(ps[:, 0:512], onesb[:], grow[:, 0:512], start=True, stop=True)
                        mm = t.matmul(ps[:, 512:1024], onesb[:], grow[:, 512:1024],
                                      start=True, stop=True)
                        misc["t_gam"] = sT.inc(mm)

                    @blkF.vector
                    def _(v):
                        wt.wait(v, "v", misc["t_gam"])
                        v.tensor_copy(grep[:, 0:512], ps[:, 0:512])
                        misc["d_gam"] = sD.inc(v.tensor_copy(grep[:, 512:1024], ps[:, 512:1024]))

                    @blkF.tensor
                    def _(t):
                        wt.wait(t, "t", misc["d_gam"])
                        t.matmul(ps[:, 0:512], onesb[:], brow[:, 0:512], start=True, stop=True)
                        mm = t.matmul(ps[:, 512:1024], onesb[:], brow[:, 512:1024],
                                      start=True, stop=True)
                        misc["t_bet"] = sT.inc(mm)

                    @blkF.vector
                    def _(v):
                        wt.wait(v, "v", misc["t_bet"])
                        v.tensor_copy(brep[:, 0:512], ps[:, 0:512])
                        misc["d_bet"] = sD.inc(v.tensor_copy(brep[:, 512:1024], ps[:, 512:1024]))

                    @blkF.tensor
                    def _(t):
                        wt.wait(t, "t", misc["d_bet"])
                        t.matmul(ps[:, PS_PI:PS_PI + 4], onesb[:], spack[:, 0:4],
                                 start=True, stop=True)
                        t.drain()
                        for j in range(KT):
                            t.transpose(ps[0:R, 2560 + 128 * j:2560 + 128 * (j + 1)],
                                        a0f[:, j, :], eyeb[:])
                        mm = None
                        for j in range(KT):
                            mm = t.transpose(ps[:, PS_N + R * j:PS_N + R * (j + 1)],
                                             b0f[0:R, P * j:P * (j + 1)], eyeb[0:R, 0:R])
                        misc["t_misc"] = sT.inc(mm)
                        t.drain()

                    @blkF.vector
                    def _(v):
                        wt.wait(v, "v", misc["t_misc"])
                        v.tensor_copy(scola[:], ps[:, PS_PI:PS_PI + 4])
                        v.tensor_copy(b0tf.rearrange("p j r -> p (j r)")[:], ps[:, PS_N:PS_N + KT * R])
                        v.drain()
                        v.tensor_scalar(s0c2[:], scola[:, 0:1], 2.0, None, ALU.mult)
                        misc["d_misc"] = sD.inc(
                            v.tensor_copy(b0tb.rearrange("p j r -> p (j r)")[:],
                                          ps[:, PS_N:PS_N + KT * R]))

                    @blkF.scalar
                    def _(sc):
                        sc.preload_act_table(AF.Sqrt)
                        wt.wait(sc, "sc", misc["t_misc"])
                        misc["a_misc"] = sA.inc(
                            sc.activation(a0t2[0:R, :], ps[0:R, 2560:3584], AF.Copy,
                                          bias=0.0, scale=2.0))

                    pool_tiles = ([("cast", wp_in, wp_bf, i) for i in range(8)]
                                  + [("cast", wb_in, wb_bf, i) for i in range(8)]
                                  + [("k2", None, None, i) for i in range(8)])
                    poolC = {}
                    poolS = {}
                    for i, (kind, psrc, pdst, j) in enumerate(pool_tiles):
                        @blkF.gpsimd
                        def _(g, i=i, kind=kind, psrc=psrc, pdst=pdst, j=j):
                            wt.wait(g, "g", poolS.get(i - 2))
                            if kind == "cast":
                                li = dma(g, q_cvP, cf32[:, i % 2, :], psrc[P * j:P * (j + 1), :])
                            else:
                                li = dma(g, q_cvP, u8p[:, i % 2, :].rearrange("p (t n) -> p t n", n=D),
                                         k2_kt[:, 4 * j:4 * (j + 1), :])
                            wt.wait(g, "g", li)
                            if kind == "cast":
                                poolC[i] = sP.inc(g.tensor_copy(cbf[:, i % 2, :], cf32[:, i % 2, :]))
                            else:
                                poolC[i] = sP.inc(g.tensor_scalar(f8p[:, i % 2, :], u8p[:, i % 2, :],
                                                                  4.0, -2.0, ALU.mult, ALU.add))
                            g.drain()
                            if kind == "cast":
                                dma(g, q_cvPS[i % 2], pdst[P * j:P * (j + 1), :], cbf[:, i % 2, :])
                            else:
                                dma(g, q_cvPS[i % 2], w2_kt[:, 4 * j:4 * (j + 1), :],
                                    f8p[:, i % 2, :].rearrange("p (t n) -> p t n", n=D))
                            poolS[i] = (q_cvPS[i % 2], q_cvPS[i % 2].n)
                            if i == 7:
                                misc["wp_done"] = [(qq, qq.n) for qq in q_cvPS]
                            if i == 15:
                                misc["wb_done"] = [(qq, qq.n) for qq in q_cvPS]
                            if i == 23:
                                misc["k2_done"] = [(qq, qq.n) for qq in q_cvPS]

                    def emit_dve_conv():
                        dve_tiles = ([("k0", None, None, i) for i in range(8)]
                                     + [("aff", k1_in, w1_dr, i) for i in range(16)])
                        dveL = {}
                        dveC = {}
                        dveS = {}
                        for i, (kind, src2, dst, j) in enumerate(dve_tiles):
                            @blkF.sync
                            def _(sync, i=i, kind=kind, src2=src2, j=j):
                                wt.wait(sync, "sync", dveC.get(i - 2))
                                if kind == "k0":
                                    dma(sync, q_cvL[i % 2], u8a[:, i % 2, :], k0_in[P * j:P * (j + 1), :])
                                    dma(sync, q_cvL[i % 2], u8b[:, i % 2, :],
                                        k0_in[D + P * j:D + P * (j + 1), :])
                                else:
                                    dma(sync, q_cvL[i % 2], u8a[:, i % 2, :], src2[P * j:P * (j + 1), :])
                                dveL[i] = (q_cvL[i % 2], q_cvL[i % 2].n)

                            @blkF.vector
                            def _(v, i=i, kind=kind):
                                wt.wait(v, "v", dveL[i])
                                wt.wait(v, "v", dveS.get(i - 2))
                                if kind == "k0":
                                    ins = v.tensor_tensor(f8d[:, i % 2, :], u8a[:, i % 2, :],
                                                          u8b[:, i % 2, :], ALU.subtract)
                                else:
                                    ins = v.tensor_scalar(f8d[:, i % 2, :], u8a[:, i % 2, :],
                                                          4.0, -2.0, ALU.mult, ALU.add)
                                dveC[i] = sD.inc(ins)

                            @blkF.sync
                            def _(sync, i=i, kind=kind, dst=dst, j=j):
                                wt.wait(sync, "sync", dveC[i])
                                if kind == "k0":
                                    dma(sync, q_cvS[i % 2], w0_dr[P * j:P * (j + 1), :], f8d[:, i % 2, :])
                                else:
                                    dma(sync, q_cvS[i % 2], dst[P * j:P * (j + 1), :], f8d[:, i % 2, :])
                                dveS[i] = (q_cvS[i % 2], q_cvS[i % 2].n)
                                if i == 7:
                                    misc["k0_done"] = [(qq, qq.n) for qq in q_cvS]

                    def emit_act_conv():
                        act_tiles = [(k1_in, w1_dr, 16 + i) for i in range(16)]

                        @blkF.scalar
                        def _(sc):
                            prevS = {}
                            for i, (src2, dst, j) in enumerate(act_tiles):
                                wt.wait(sc, "sc", prevS.get(i - 2))
                                li = dma(sc, q_cvA, u8s[:, i % 2, :], src2[P * j:P * (j + 1), :])
                                wt.wait(sc, "sc", li)
                                sA.inc(sc.activation(f8s[:, i % 2, :], u8s[:, i % 2, :],
                                                     AF.Copy, bias=-2.0, scale=4.0))
                                sc.drain()
                                prevS[i] = dma(sc, q_cvAS[i % 2], dst[P * j:P * (j + 1), :],
                                               f8s[:, i % 2, :])
                Lx = {}
                Dst = {}
                Dub = {}
                Dxt = {}
                Ast = {}
                Pxn = {}
                Ust = {}
                Nst = {}

                for i in range(NT):
                    @blkF.sync
                    def _(sync, i=i):
                        wt.wait(sync, "sync", Dxt.get(i - 3))
                        Lx[i] = dma(sync, q_a1x[i % 3], xin[:, i % 3, :], x_s[:, i, :])

                    @blkF.vector
                    def _(v, i=i):
                        wt.wait(v, "v", Lx[i])
                        v.bn_stats(st6[:, 0:6], xin[:, i % 3, 0:512])
                        v.bn_stats(st6[:, 6:12], xin[:, i % 3, 512:1024])
                        v.drain()
                        Dst[i] = sD.inc(v.bn_aggr(mvb[:], st6[:]))

                    @blkF.scalar
                    def _(sc, i=i):
                        wt.wait(sc, "sc", Dst[i])
                        Ast[i] = sA.inc(sc.activation(stdb[:, 0:1], mvb[:, 1:2], AF.Sqrt,
                                                      bias=epscol[:], scale=1.0))

                    @blkF.vector
                    def _(v, i=i):
                        wt.wait(v, "v", Ast[i])
                        v.reciprocal(rsb[:, 0:1], stdb[:, 0:1])
                        v.drain()
                        v.tensor_scalar(stdb[:, 1:2], mvb[:, 1:2], rsb[:, 0:1], rsb[:, 0:1],
                                        ALU.mult, ALU.mult)
                        wt.wait(v, "v", Ust.get(i - 2))
                        Dub[i] = sD.inc(v.tensor_scalar(ubf[:, i % 2, :], xin[:, i % 3, :],
                                                        mvb[:, 0:1], rsb[:, 0:1],
                                                        ALU.subtract, ALU.mult))

                    @blkF.scalar
                    def _(sc, i=i):
                        wt.wait(sc, "sc", Dub[i])
                        Ast[i + 100] = sA.inc(sc.activation(stdb[:, 2:3], stdb[:, 1:2],
                                                            AF.Sqrt, bias=epscol[:], scale=1.0))

                    @blkF.vector
                    def _(v, i=i):
                        wt.wait(v, "v", Ast[i + 100])
                        v.reciprocal(rsb[:, 1:2], stdb[:, 2:3])
                        v.drain()
                        v.tensor_scalar(acol[:], rsb[:, 0:1], rsb[:, 1:2], None, ALU.mult)
                        v.drain()
                        Dxt[i] = sD.inc(v.tensor_scalar(xtb[:, i % 2, :], xin[:, i % 3, :],
                                                        mvb[:, 0:1], acol[:],
                                                        ALU.subtract, ALU.mult))
                        v.drain()
                        v.tensor_tensor(xgb[:, i % 2, :], xtb[:, i % 2, :],
                                        grep[:], ALU.mult)
                        v.drain()
                        wt.wait(v, "v", Nst.get(i - 2))
                        Pxn[i] = sD.inc(v.tensor_tensor(xnb[:, i % 2, :], xgb[:, i % 2, :],
                                                        brep[:], ALU.add))

                    @blkF.scalar
                    def _(sc, i=i):
                        wt.wait(sc, "sc", Dub[i])
                        Ust[i] = dma(sc, q_a1u[i % 2], u_tmv[:, i, :], ubf[:, i % 2, :])
                        wt.wait(sc, "sc", Pxn[i])
                        if i == 0:
                            Nst[i] = dma(sc, q_a1n[0], xn_dr[0:P - 1, :], xnb[1:P, 0, :])
                        else:
                            Nst[i] = dma(sc, q_a1n[i % 2],
                                         xn_dr[P * i - 1:P * (i + 1) - 1, :], xnb[:, i % 2, :])

                @blkF.scalar
                def _(sc):
                    for qq in q_a1u + q_a1n:
                        wt.wait(sc, "sc", (qq, qq.n))

                A1u_done = [(qq, qq.n) for qq in q_a1u]
                A1n_done = [(qq, qq.n) for qq in q_a1n]

                wpT = front.enter_context(nc.sbuf_tensor(f"a2wp{s}", [P, KT, D], BF16,
                                                         is_transpose_dest=True))
                shb = front.enter_context(nc.sbuf_tensor(f"a2sh{s}", [P, 2, KT, CH], BF16,
                                                         is_transpose_dest=True))
                vob = front.enter_context(nc.sbuf_tensor(f"a2vo{s}", [P, 2, 2, D], BF16))
                LwpT = [None]
                Lsh = {}
                Tv = {}
                Av = {}
                Sv = {}

                @blkF.sync
                def _(sync):
                    for tok in misc["wp_done"]:
                        wt.wait(sync, "sync", tok)
                    for j in range(KT):
                        dma(sync, q_a2wp, wpT[:, j, :], wp_bf[0:D, P * j:P * (j + 1)],
                            transpose=True)
                    LwpT[0] = (q_a2wp, q_a2wp.n)

                for c in range(NCH):
                    @blkF.sync
                    def _(sync, c=c):
                        wt.wait(sync, "sync", Tv.get((c - 2, 1)))
                        wt.wait(sync, "sync", Nst[2 * c + 1])
                        wt.wait(sync, "sync", Nst[min(2 * c + 2, NT - 1)])
                        wt.wait(sync, "sync", misc["zr"])
                        for j in range(KT):
                            dma(sync, q_a2s[c % 2], shb[:, c % 2, j, :],
                                xn_dr[CH * c:CH * (c + 1), P * j:P * (j + 1)], transpose=True)
                        Lsh[c] = (q_a2s[c % 2], q_a2s[c % 2].n)

                    @blkF.tensor
                    def _(t, c=c):
                        wt.wait(t, "t", Lsh[c])
                        wt.wait(t, "t", LwpT[0])
                        wt.wait(t, "t", Av.get((c - 1, 1)))
                        for ts_ in range(2):
                            base = PS_Y0 if ts_ == 0 else PS_Y1
                            for h in range(2):
                                mm = None
                                for j in range(KT):
                                    mm = t.matmul(ps[:, base + 512 * h:base + 512 * (h + 1)],
                                                  shb[:, c % 2, j, P * ts_:P * (ts_ + 1)],
                                                  wpT[:, j, 512 * h:512 * (h + 1)],
                                                  start=(j == 0), stop=(j == KT - 1))
                            Tv[(c, ts_)] = sT.inc(mm)

                    @blkF.scalar
                    def _(sc, c=c):
                        wt.wait(sc, "sc", Sv.get(c - 2))
                        for ts_ in range(2):
                            base = PS_Y0 if ts_ == 0 else PS_Y1
                            wt.wait(sc, "sc", Tv[(c, ts_)])
                            Av[(c, ts_)] = sA.inc(
                                sc.activation(vob[:, c % 2, ts_, :], ps[:, base:base + D],
                                              AF.Copy, bias=0.0, scale=1.0))
                        sc.drain()
                        for ts_ in range(2):
                            Sv[c] = dma(sc, q_a2v[c % 2], v_tmv[:, 2 * c + ts_, :],
                                        vob[:, c % 2, ts_, :])

                @blkF.scalar
                def _(sc):
                    for qq in q_a2v:
                        wt.wait(sc, "sc", (qq, qq.n))

                if s == 0:
                    emit_dve_conv()
                    emit_act_conv()
                    misc["k1_done"] = [(qq, qq.n) for qq in (q_cvS + q_cvAS)]
                    misc["conv_all"] = [(qq, qq.n) for qq in
                                        (q_cvS + q_cvAS + q_cvPS + [q_cvA, q_cvL[0], q_cvL[1], q_cvP])]

            SvD = dict(Sv)

            a3 = contextlib.ExitStack()
            Gx1_all = {}
            with a3:
                wbT = a3.enter_context(nc.sbuf_tensor(f"a3wb{s}", [P, KT, D], BF16,
                                                      is_transpose_dest=True))
                ufm = a3.enter_context(nc.sbuf_tensor(f"a3uf{s}", [P, 2, KT, CH], BF16,
                                                      is_transpose_dest=True))
                vfm = a3.enter_context(nc.sbuf_tensor(f"a3vf{s}", [P, 2, KT, CH], BF16,
                                                      is_transpose_dest=True))
                utm = a3.enter_context(nc.sbuf_tensor(f"a3ut{s}", [P, 2, 2, D], BF16))
                vtm = a3.enter_context(nc.sbuf_tensor(f"a3vt{s}", [P, 2, 2, D], BF16))
                xck = a3.enter_context(nc.sbuf_tensor(f"a3xc{s}", [P, 2, 2, D], F32))
                pib = a3.enter_context(nc.sbuf_tensor(f"a3pi{s}", [P, 2, P], BF16))
                peb = a3.enter_context(nc.sbuf_tensor(f"a3pe{s}", [P, 32], BF16))
                dAc = a3.enter_context(nc.sbuf_tensor(f"a3da{s}", [R, D], F32))
                dBc = a3.enter_context(nc.sbuf_tensor(f"a3db{s}", [P, P], F32))
                aef = a3.enter_context(nc.sbuf_tensor(f"a3ae{s}", [P, D], BF16))
                bef = a3.enter_context(nc.sbuf_tensor(f"a3be{s}", [P, KT, P], BF16))
                midb = a3.enter_context(nc.sbuf_tensor(f"a3mi{s}", [P, CH], BF16))
                nsq = a3.enter_context(nc.sbuf_tensor(f"a3nq{s}", [1, 8], F32))
                ccb = a3.enter_context(nc.sbuf_tensor(f"a3cc{s}", [P, 2], BF16))
                nab = a3.enter_context(nc.sbuf_tensor(f"a3na{s}", [P, 1], F32))
                nbb = a3.enter_context(nc.sbuf_tensor(f"a3nb{s}", [P, 1], F32))
                nabb = a3.enter_context(nc.sbuf_tensor(f"a3nab{s}", [P, 2], BF16))
                bcb = a3.enter_context(nc.sbuf_tensor(f"a3bc{s}", [P, 2], F32))
                sqd = a3.enter_context(nc.sbuf_tensor(f"a3sq{s}", [P, D], BF16))
                ybf = a3.enter_context(nc.sbuf_tensor(f"a3yb{s}", [P, 2, D], F32))
                syb = a3.enter_context(nc.sbuf_tensor(f"a3sy{s}", [P, 2], F32))
                sy2b = a3.enter_context(nc.sbuf_tensor(f"a3s2{s}", [P, 2], F32))
                lnc = a3.enter_context(nc.sbuf_tensor(f"a3ln{s}", [P, 10], F32))
                tpb = a3.enter_context(nc.sbuf_tensor(f"a3tp{s}", [P, 2, D], F32))
                x1b = a3.enter_context(nc.sbuf_tensor(f"a3x1{s}", [P, 2, 2, D], F32))

                blkA3 = a3.enter_context(nc.Block(f"a3_{s}"))

                LwbT = [None]
                Lu = {}
                Lv = {}
                Gx = {}
                Tpi = {}
                Tpe = {}
                Tdb = {}
                Tda = {}
                Tn1 = {}
                Tbc = {}
                Tmid = {}
                TY = {}
                Dpi = {}
                Dpe = {}
                Ddab = {}
                Dn1 = {}
                Dn2 = {}
                Dchain = {}
                Dx1 = {}
                An = {}
                An2 = {}
                Amid = {}
                Aev = {}
                Asq = {}
                Gx1 = {}
                Dln = {}
                Aln = {}
                d_zero = [None]
                aodump = [None]

                @blkA3.vector
                def _(v):
                    if s > 0:
                        v.memset(SAb[:], 0.0)
                        v.memset(SBb[:], 0.0)
                    v.memset(pib.rearrange("p a b -> p (a b)")[:], 0.0)
                    v.memset(bef.rearrange("p a b -> p (a b)")[:], 0.0)
                    v.memset(midb[:], 0.0)
                    v.memset(aef[:], 0.0)
                    v.memset(nab[:], 0.0)
                    v.memset(nabb[:], 0.0)
                    d_zero[0] = sD.inc(v.memset(ccb[:], 0.0))
                    v.drain()

                wbT_done = [False]

                def _load_wbT(sync):
                    for tok in misc["wb_done"]:
                        wt.wait(sync, "sync", tok)
                    for j in range(KT):
                        dma(sync, q_a3wb, wbT[:, j, :], wb_bf[0:D, P * j:P * (j + 1)],
                            transpose=True)
                    wt.wait(sync, "sync", (q_a3wb, q_a3wb.n))
                    LwbT[0] = sP.inc(sync.nop())
                    wbT_done[0] = True

                for c in range(NCH):
                    @blkA3.sync
                    def _(sync, c=c):
                        wt.wait(sync, "sync", Tmid.get(c - 2))
                        for tok in A1u_done:
                            wt.wait(sync, "sync", tok)
                        for j in range(KT):
                            dma(sync, q_a3u[c % 2], ufm[:, c % 2, j, :],
                                u_dr[CH * c:CH * (c + 1), P * j:P * (j + 1)], transpose=True)
                        tok = dma(sync, q_a3u[c % 2], utm[:, c % 2, :, :],
                                  u_tmv[:, 2 * c:2 * c + 2, :])
                        wt.wait(sync, "sync", tok)
                        Lu[c] = sP.inc(sync.nop())
                        if not wbT_done[0]:
                            _load_wbT(sync)

                    @blkA3.sync
                    def _(sync, c=c):
                        wt.wait(sync, "sync", Tda.get((c - 2, 1)))
                        wt.wait(sync, "sync", SvD[c])
                        for j in range(KT):
                            dma(sync, q_a3v[c % 2], vfm[:, c % 2, j, :],
                                v_dr[CH * c:CH * (c + 1), P * j:P * (j + 1)], transpose=True)
                        tok = dma(sync, q_a3v[c % 2], vtm[:, c % 2, :, :],
                                  v_tmv[:, 2 * c:2 * c + 2, :])
                        wt.wait(sync, "sync", tok)
                        Lv[c] = sP.inc(sync.nop())

                    @blkA3.scalar
                    def _(sc, c=c):
                        wt.wait(sc, "sc", Dx1.get((c - 2, 1)))
                        Gx[c] = dma(sc, q_a3x[c % 2], xck[:, c % 2, :, :],
                                    x_s[:, 2 * c:2 * c + 2, :])

                    @blkA3.tensor
                    def _(t, c=c):
                        wt.wait(t, "t", Lu[c])
                        wt.wait(t, "t", Lv[c])
                        wt.wait(t, "t", misc["d_misc"])
                        wt.wait(t, "t", misc["a_misc"])
                        wt.wait(t, "t", d_zero[0])
                        wt.wait(t, "t", Ddab.get(c - 1))
                        if skip_small:
                            Tpi[c] = sT.inc(t.nop())
                            Tpe[c] = Tpi[c]
                            return
                        if NO_SMALL_MM:
                            Tpi[c] = sT.inc(t.matmul(ps[:, PS_PI:PS_PI + 512],
                                                     ufm[:, c % 2, 0, 0:P],
                                                     wbT[:, 0, 0:512], start=True, stop=True))
                            Tpe[c] = sT.inc(t.matmul(ps[:, PS_PE:PS_PE + 512],
                                                     vfm[:, c % 2, 0, 0:P],
                                                     wbT[:, 0, 0:512], start=True, stop=True))
                            return
                        mm = None
                        for ts_ in range(2):
                            for j in range(KT):
                                mm = t.matmul(ps[:, PS_PI + R * ts_:PS_PI + R * (ts_ + 1)],
                                              ufm[:, c % 2, j, P * ts_:P * (ts_ + 1)],
                                              b0tb[:, j, :], start=(j == 0), stop=(j == KT - 1))
                        Tpi[c] = sT.inc(mm)
                        mm = None
                        for ts_ in range(2):
                            for j in range(KT):
                                mm = t.matmul(ps[:, PS_PE + R * ts_:PS_PE + R * (ts_ + 1)],
                                              vfm[:, c % 2, j, P * ts_:P * (ts_ + 1)],
                                              a0nb[:, j, :], start=(j == 0), stop=(j == KT - 1))
                        Tpe[c] = sT.inc(mm)

                    @blkA3.vector
                    def _(v, c=c):
                        if skip_small:
                            Dpi[c] = Dpe[c] = sD.inc(v.nop())
                            return
                        wt.wait(v, "v", Tpi[c])
                        v.tensor_copy(pib[:, 0, 0:R], ps[:, PS_PI:PS_PI + R])
                        Dpi[c] = sD.inc(v.tensor_copy(pib[:, 1, 0:R],
                                                      ps[:, PS_PI + R:PS_PI + 32]))
                        wt.wait(v, "v", Tpe[c])
                        Dpe[c] = sD.inc(v.tensor_copy(peb[:], ps[:, PS_PE:PS_PE + 32]))

                    @blkA3.tensor
                    def _(t, c=c):
                        wt.wait(t, "t", Aev.get((c - 1, 1)))
                        for ts_ in range(2):
                            base = PS_Y0 if ts_ == 0 else PS_Y1
                            mm = None
                            for h in range(2):
                                for j in range(KT):
                                    mm = t.matmul(ps[:, base + 512 * h:base + 512 * (h + 1)],
                                                  ufm[:, c % 2, j, P * ts_:P * (ts_ + 1)],
                                                  wbT[:, j, 512 * h:512 * (h + 1)],
                                                  start=(j == 0),
                                                  stop=(nolora and j == KT - 1),
                                                  skip_group_check=not nolora)
                            if nolora:
                                TY[(c, ts_)] = sT.inc(mm)
                        if SERIALIZE_EVAC:
                            wt.wait(t, "t", Aev.get((c, 1)))
                        if skip_small:
                            Tdb[c] = sT.inc(t.nop())
                            Tda[(c, 0)] = Tda[(c, 1)] = Tdb[c]
                            return
                        wt.wait(t, "t", Dpe[c])
                        if NO_SMALL_MM:
                            Tdb[c] = sT.inc(t.matmul(ps[:, PS_DB:PS_DB + 512],
                                                     utm[:, c % 2, 0, 0:P],
                                                     wbT[:, 0, 0:512], start=True, stop=True))
                            wt.wait(t, "t", Dpi[c])
                            for h in range(2):
                                off = PS_PI if h == 0 else PS_PE
                                Tda[(c, h)] = sT.inc(t.matmul(ps[:, off:off + 512],
                                                              pib[:, 0, :],
                                                              vtm[:, c % 2, 0, 512 * h:512 * (h + 1)],
                                                              start=True, stop=True))
                            return
                        mm = None
                        for j in range(KT):
                            for ts_ in range(2):
                                mm = t.matmul(ps[:, PS_DB + R * j:PS_DB + R * (j + 1)],
                                              utm[:, c % 2, ts_, P * j:P * (j + 1)],
                                              peb[:, R * ts_:R * (ts_ + 1)],
                                              start=(j == 0 and ts_ == 0),
                                              stop=(j == KT - 1 and ts_ == 1),
                                              skip_group_check=True)
                        Tdb[c] = sT.inc(mm)
                        wt.wait(t, "t", Dpi[c])
                        for h in range(2):
                            off = PS_PI if h == 0 else PS_PE
                            mm = None
                            for ts_ in range(2):
                                mm = t.matmul(ps[:, off:off + 512],
                                              pib[:, ts_, :],
                                              vtm[:, c % 2, ts_, 512 * h:512 * (h + 1)],
                                              start=(ts_ == 0), stop=(ts_ == 1),
                                              skip_group_check=True)
                            Tda[(c, h)] = sT.inc(mm)

                    @blkA3.vector
                    def _(v, c=c):
                        if skip_small:
                            Ddab[c] = sD.inc(v.nop())
                            return
                        wt.wait(v, "v", Tdb[c])
                        v.tensor_copy(dBc[:], ps[:, PS_DB:PS_DB + P])
                        wt.wait(v, "v", Tda[(c, 1)])
                        v.tensor_copy(dAc[0:R, 0:512], ps[0:R, PS_PI:PS_PI + 512])
                        Ddab[c] = sD.inc(
                            v.tensor_copy(dAc[0:R, 512:1024], ps[0:R, PS_PE:PS_PE + 512]))

                    @blkA3.scalar
                    def _(sc, c=c):
                        if skip_small:
                            An[c] = An[c + 100] = sA.inc(sc.nop())
                            return
                        if c == 0:
                            wt.wait(sc, "sc", d_zero[0])
                            wt.wait(sc, "sc", misc["d_mem"])
                        else:
                            wt.wait(sc, "sc", Dchain[c - 1])
                        An[c] = sA.inc(sc.activation(sqd[0:R, :], SAb[:], AF.Square,
                                                     accum_out=nab[0:R, :]))
                        sc.drain()
                        An[c + 100] = sA.inc(sc.activation(sqd[:, 0:P], SBb[:], AF.Square,
                                                           accum_out=nbb[:]))

                    @blkA3.vector
                    def _(v, c=c):
                        if skip_small:
                            An[(c, "cast")] = sD.inc(v.nop())
                            return
                        wt.wait(v, "v", An[c + 100])
                        v.tensor_copy(nabb[:, 0:1], nab[:])
                        An[(c, "cast")] = sD.inc(v.tensor_copy(nabb[:, 1:2], nbb[:]))

                    @blkA3.tensor
                    def _(t, c=c):
                        if skip_small:
                            Tn1[c] = sT.inc(t.nop())
                            return
                        wt.wait(t, "t", An[(c, "cast")])
                        wt.wait(t, "t", Amid.get(c - 1))
                        if NO_SMALL_MM:
                            Tn1[c] = sT.inc(t.matmul(ps[:, PS_N:PS_N + 512], onesbb[:],
                                                     aef[:, 0:512], start=True, stop=True))
                            return
                        t.matmul(ps[:, PS_N:PS_N + 1], onesbb[:], nabb[:, 0:1],
                                 start=True, stop=False, skip_group_check=True)
                        mm = t.matmul(ps[:, PS_N:PS_N + 1], onesbb[:], nabb[:, 1:2],
                                      start=False, stop=True, skip_group_check=True)
                        Tn1[c] = sT.inc(mm)

                    @blkA3.vector
                    def _(v, c=c):
                        if skip_small:
                            Dn1[c] = sD.inc(v.nop())
                            return
                        wt.wait(v, "v", Tn1[c])
                        Dn1[c] = sD.inc(v.tensor_copy(nsq[0:1, 0:1], ps[0:1, PS_N:PS_N + 1]))

                    @blkA3.scalar
                    def _(sc, c=c):
                        if skip_small:
                            An2[c] = sA.inc(sc.nop())
                            return
                        wt.wait(sc, "sc", Dn1[c])
                        An2[c] = sA.inc(sc.activation(nsq[0:1, 1:2], nsq[0:1, 0:1], AF.Sqrt,
                                                      bias=zcol[0:1, :], scale=1.0))

                    @blkA3.vector
                    def _(v, c=c):
                        if skip_small:
                            Dn2[c] = sD.inc(v.nop())
                            return
                        wt.wait(v, "v", An2[c])
                        v.tensor_scalar(nsq[0:1, 2:3], nsq[0:1, 1:2], LR, 1e-6,
                                        ALU.mult, ALU.add)
                        v.drain()
                        v.reciprocal(nsq[0:1, 3:4], nsq[0:1, 2:3])
                        v.drain()
                        v.tensor_scalar(nsq[0:1, 4:5], nsq[0:1, 3:4], 1.0, None, ALU.min)
                        v.drain()
                        v.tensor_scalar(ccb[0:1, 0:1], nsq[0:1, 4:5], -2.0 * LR, None, ALU.mult)
                        Dn2[c] = sD.inc(v.tensor_scalar(ccb[0:1, 1:2], nsq[0:1, 4:5],
                                                        -LR, None, ALU.mult))

                    @blkA3.tensor
                    def _(t, c=c):
                        if skip_small:
                            Tbc[c] = sT.inc(t.nop())
                            return
                        wt.wait(t, "t", Dn2[c])
                        if NO_SMALL_MM:
                            Tbc[c] = sT.inc(t.matmul(ps[:, PS_N:PS_N + 512], onesbb[:],
                                                     aef[:, 0:512], start=True, stop=True,
                                                     skip_group_check=True))
                            return
                        Tbc[c] = sT.inc(t.matmul(ps[:, PS_BC:PS_BC + 2], onesbb[:],
                                                 ccb[:, 0:2], start=True, stop=True,
                                                 skip_group_check=True))

                    @blkA3.vector
                    def _(v, c=c):
                        if skip_small:
                            Dchain[c] = sD.inc(v.nop())
                            return
                        wt.wait(v, "v", Tbc[c])
                        v.tensor_copy(bcb[:], ps[:, PS_BC:PS_BC + 2])
                        v.drain()
                        v.scalar_tensor_tensor(aef[0:R, :], SAb[:], bcb[0:R, 0:1], a0t2[:],
                                               ALU.mult, ALU.add)
                        v.scalar_tensor_tensor(bef[:, :, 0:R],
                                               SBb.rearrange("p (j r) -> p j r", r=R)[:],
                                               bcb[:, 1:2], b0tf[:],
                                               ALU.mult, ALU.add)
                        v.drain()
                        wt.wait(v, "v", Ddab[c])
                        v.tensor_tensor(SAb[:], SAb[:], dAc[:], ALU.add)
                        Dchain[c] = sD.inc(v.tensor_tensor(SBb[:], SBb[:], dBc[:], ALU.add))

                    @blkA3.tensor
                    def _(t, c=c):
                        if skip_small:
                            Tmid[c] = sT.inc(t.nop())
                            return
                        wt.wait(t, "t", Dchain[c])
                        if NO_SMALL_MM:
                            Tmid[c] = sT.inc(t.matmul(ps[:, PS_N:PS_N + 512], onesbb[:],
                                                      aef[:, 0:512], start=True, stop=True,
                                                      skip_group_check=True))
                            return
                        mm = None
                        for j in range(KT):
                            mm = t.matmul(ps[:, PS_MID:PS_MID + CH],
                                          bef[:, j, :],
                                          ufm[:, c % 2, j, :], start=(j == 0),
                                          stop=(j == KT - 1), skip_group_check=True)
                        Tmid[c] = sT.inc(mm)

                    @blkA3.scalar
                    def _(sc, c=c):
                        if skip_small:
                            Amid[c] = sA.inc(sc.nop())
                            return
                        wt.wait(sc, "sc", Tmid[c])
                        Amid[c] = sA.inc(sc.activation(midb[0:R, :],
                                                       ps[0:R, PS_MID:PS_MID + CH],
                                                       AF.Copy, bias=0.0, scale=1.0))

                    if not nolora:
                        @blkA3.tensor
                        def _(t, c=c):
                            wt.wait(t, "t", Amid[c])
                            for ts_ in range(2):
                                base = PS_Y0 if ts_ == 0 else PS_Y1
                                mm = None
                                for h in range(2):
                                    mm = t.matmul(ps[:, base + 512 * h:base + 512 * (h + 1)],
                                                  midb[:, P * ts_:P * (ts_ + 1)],
                                                  aef[:, 512 * h:512 * (h + 1)],
                                                  start=False, stop=True,
                                                  skip_group_check=True)
                                TY[(c, ts_)] = sT.inc(mm)

                    @blkA3.scalar
                    def _(sc, c=c):
                        for ts_ in range(2):
                            base = PS_Y0 if ts_ == 0 else PS_Y1
                            wt.wait(sc, "sc", TY[(c, 1)])
                            wt.wait(sc, "sc", Dx1.get((c - 1, ts_)))
                            Aev[(c, ts_)] = sA.inc(
                                sc.activation(ybf[:, ts_, :], ps[:, base:base + D],
                                              AF.Copy, accum_out=syb[:, ts_:ts_ + 1]))
                        for ts_ in range(2):
                            sc.drain()
                            Asq[(c, ts_)] = sA.inc(
                                sc.activation(sqd[:], ybf[:, ts_, :], AF.Square,
                                              accum_out=sy2b[:, ts_:ts_ + 1]))
                        if ao_out is not None:
                            aosv = ao_out[s].rearrange("(t p) k -> p t k", p=P)
                            aodump[0] = dma(sc, q_dbg, aosv[:, 2 * c:2 * c + 2, :],
                                            ybf[:, :, :])

                    @blkA3.vector
                    def _(v, c=c):
                        wt.wait(v, "v", Asq[(c, 1)])
                        v.tensor_scalar(lnc[:, 0:2], syb[:], 1.0 / D, None, ALU.mult)
                        v.drain()
                        v.tensor_scalar(lnc[:, 2:3], lnc[:, 0:1], lnc[:, 0:1], None, ALU.mult)
                        v.tensor_scalar(lnc[:, 3:4], lnc[:, 1:2], lnc[:, 1:2], None, ALU.mult)
                        v.drain()
                        v.tensor_scalar(lnc[:, 4:5], sy2b[:, 0:1], 1.0 / D, lnc[:, 2:3],
                                        ALU.mult, ALU.subtract)
                        Dln[c] = sD.inc(v.tensor_scalar(lnc[:, 5:6], sy2b[:, 1:2], 1.0 / D,
                                                        lnc[:, 3:4], ALU.mult, ALU.subtract))

                    @blkA3.scalar
                    def _(sc, c=c):
                        wt.wait(sc, "sc", Dln[c])
                        Aln[c] = sA.inc(sc.activation(lnc[:, 6:8], lnc[:, 4:6], AF.Sqrt,
                                                      bias=epscol[:], scale=1.0))

                    @blkA3.vector
                    def _(v, c=c):
                        wt.wait(v, "v", Aln[c])
                        v.reciprocal(lnc[:, 8:9], lnc[:, 6:7])
                        v.reciprocal(lnc[:, 9:10], lnc[:, 7:8])
                        v.drain()
                        wt.wait(v, "v", Gx[c])
                        wt.wait(v, "v", Gx1.get(c - 2))
                        if ao_out is not None:
                            wt.wait(v, "v", aodump[0])
                        for ts_ in range(2):
                            v.tensor_scalar(tpb[:, ts_, :], ybf[:, ts_, :], lnc[:, ts_:ts_ + 1],
                                            lnc[:, 8 + ts_:9 + ts_], ALU.subtract, ALU.mult)
                        v.drain()
                        for ts_ in range(2):
                            Dx1[(c, ts_)] = sD.inc(
                                v.tensor_tensor(x1b[:, c % 2, ts_, :], tpb[:, ts_, :],
                                                xck[:, c % 2, ts_, :], ALU.add))

                    @blkA3.scalar
                    def _(sc, c=c):
                        wt.wait(sc, "sc", Dx1[(c, 1)])
                        dma(sc, q_a3s[c % 2], x1_tmv[:, 2 * c:2 * c + 2, :],
                            x1b[:, c % 2, :, :])
                        if x1_out is not None:
                            x1sv = x1_out[s].rearrange("(t p) k -> p t k", p=P)
                            dma(sc, q_a3s[c % 2], x1sv[:, 2 * c:2 * c + 2, :],
                                x1b[:, c % 2, :, :])
                        Gx1[c] = (q_a3s[c % 2], q_a3s[c % 2].n)

                @blkA3.scalar
                def _(sc):
                    for qq in q_a3s:
                        wt.wait(sc, "sc", (qq, qq.n))

                Gx1_all = dict(Gx1)

            b = contextlib.ExitStack()
            with b:
                x1i = b.enter_context(nc.sbuf_tensor(f"bx1{s}", [P, 2, TSUB, D], F32))
                u2s = b.enter_context(nc.sbuf_tensor(f"bu2{s}", [P, 2, TSUB, D], BF16))
                u2f = b.enter_context(nc.sbuf_tensor(f"bu2f{s}", [P, 2, KT, FT], BF16,
                                                     is_transpose_dest=True))
                w0b = b.enter_context(nc.sbuf_tensor(f"bw0{s}", [P, 2, KT, CH], FP8))
                w1b = b.enter_context(nc.sbuf_tensor(f"bw1{s}", [P, 2, HT, P], FP8))
                w2b = b.enter_context(nc.sbuf_tensor(f"bw2{s}", [P, 2, 4, D], FP8))
                a0b = b.enter_context(nc.sbuf_tensor(f"ba0{s}", [P, HT, FT], BF16))
                a1b = b.enter_context(nc.sbuf_tensor(f"ba1{s}", [P, HT, FT], BF16))
                st6b = b.enter_context(nc.sbuf_tensor(f"bst{s}", [P, 12], F32))
                mv2 = b.enter_context(nc.sbuf_tensor(f"bmv{s}", [P, 2], F32))
                l2m = b.enter_context(nc.sbuf_tensor(f"bl2m{s}", [P, TSUB], F32))
                l2v = b.enter_context(nc.sbuf_tensor(f"bl2v{s}", [P, TSUB], F32))
                l2s = b.enter_context(nc.sbuf_tensor(f"bl2s{s}", [P, TSUB], F32))
                l2r = b.enter_context(nc.sbuf_tensor(f"bl2r{s}", [P, TSUB], F32))
                sqdB = b.enter_context(nc.sbuf_tensor(f"bsq{s}", [P, D], BF16))
                yb2 = b.enter_context(nc.sbuf_tensor(f"byb{s}", [P, 2, D], F32))
                oms = b.enter_context(nc.sbuf_tensor(f"bom{s}", [P, 12], F32))
                tpB = b.enter_context(nc.sbuf_tensor(f"btp{s}", [P, D], F32))
                obB = b.enter_context(nc.sbuf_tensor(f"bob{s}", [P, 2, D], F32))

                blkB = b.enter_context(nc.Block(f"b_{s}"))

                Gx1L = {}
                Dvar = {}
                Asr = {}
                Du2 = {}
                Su2 = {}
                Lu2f = {}
                Lw0 = {}
                Lw1 = {}
                Lw2 = {}
                Tm0 = {}
                Am0 = {}
                Tm1 = {}
                Am1 = {}
                Tm2g = {}
                Aev2 = {}
                Asq2 = {}
                Dst2 = {}
                Asr2 = {}
                Dob = {}
                Oob = {}
                NJG = HT // 2

                for fn in range(NFT):
                    @blkB.gpsimd
                    def _(g, fn=fn):
                        wt.wait(g, "g", Gx1_all[2 * fn + 1])
                        if fn >= 2:
                            wt.wait(g, "g", Du2.get(fn - 2))
                        Gx1L[(fn, "p")] = dma(g, q_bx[fn % 2], x1i[:, fn % 2, :, :],
                                              x1_tmv[:, TSUB * fn:TSUB * (fn + 1), :])

                    @blkB.vector
                    def _(v, fn=fn):
                        wt.wait(v, "v", Gx1L[(fn, "p")])
                        for ts_ in range(TSUB):
                            v.bn_stats(st6b[:, 0:6], x1i[:, fn % 2, ts_, 0:512])
                            v.bn_stats(st6b[:, 6:12], x1i[:, fn % 2, ts_, 512:1024])
                            v.drain()
                            v.bn_aggr(mv2[:], st6b[:])
                            v.drain()
                            v.tensor_copy(l2m[:, ts_:ts_ + 1], mv2[:, 0:1])
                            Dvar[(fn, ts_)] = sD.inc(
                                v.tensor_copy(l2v[:, ts_:ts_ + 1], mv2[:, 1:2]))

                    @blkB.scalar
                    def _(sc, fn=fn):
                        wt.wait(sc, "sc", Dvar[(fn, TSUB - 1)])
                        Asr[fn] = sA.inc(sc.activation(l2s[:], l2v[:], AF.Sqrt,
                                                       bias=epscol[:], scale=1.0))

                    @blkB.vector
                    def _(v, fn=fn):
                        wt.wait(v, "v", Asr[fn])
                        wt.wait(v, "v", Su2.get(fn - 2))
                        for ts_ in range(TSUB):
                            v.reciprocal(l2r[:, ts_:ts_ + 1], l2s[:, ts_:ts_ + 1])
                        v.drain()
                        ins = None
                        for ts_ in range(TSUB):
                            ins = v.tensor_scalar(u2s[:, fn % 2, ts_, :],
                                                  x1i[:, fn % 2, ts_, :],
                                                  l2m[:, ts_:ts_ + 1], l2r[:, ts_:ts_ + 1],
                                                  ALU.subtract, ALU.mult)
                        Du2[fn] = sD.inc(ins)

                    @blkB.scalar
                    def _(sc, fn=fn):
                        wt.wait(sc, "sc", Du2[fn])
                        Su2[fn] = dma(sc, q_bu[fn % 2],
                                      u2_tmv[:, TSUB * fn:TSUB * (fn + 1), :],
                                      u2s[:, fn % 2, :, :])

                for f in range(NFT):
                    fn = f
                    @blkB.sync
                    def _(sync, fn=fn):
                        wt.wait(sync, "sync", Su2[fn])
                        wt.wait(sync, "sync", Tm0.get((fn - 2, NJG - 1)))
                        for j in range(KT):
                            dma(sync, q_bf[fn % 2], u2f[:, fn % 2, j, :],
                                u2_dr[FT * fn:FT * (fn + 1), P * j:P * (j + 1)],
                                transpose=True)
                        Lu2f[fn] = (q_bf[fn % 2], q_bf[fn % 2].n)

                    @blkB.gpsimd
                    def _(g, f=f):
                        wt.wait(g, "g", Du2.get(NFT - 1))
                        wt.wait(g, "g", Dob.get((f - 2, TSUB - 1)))
                        Gx1L[f] = dma(g, q_bx[f % 2], x1i[:, f % 2, :, :],
                                      x1_tmv[:, TSUB * f:TSUB * (f + 1), :])

                    for jg in range(NJG):
                        @blkB.sync
                        def _(sync, f=f, jg=jg):
                            if f == 0 and jg == 0:
                                for tok in misc["k0_done"]:
                                    wt.wait(sync, "sync", tok)
                            pj = (f, jg - 2) if jg >= 2 else (f - 1, NJG - 2 + jg)
                            wt.wait(sync, "sync", Tm0.get(pj))
                            Lw0[(f, jg)] = dma(sync, q_bw0[jg % 2], w0b[:, jg % 2, :, :],
                                               w0_kt[:, :, CH * jg:CH * (jg + 1)])

                        @blkB.tensor
                        def _(t, f=f, jg=jg):
                            wt.wait(t, "t", Lw0[(f, jg)])
                            wt.wait(t, "t", Lu2f[f])
                            pj = (f, jg - 2) if jg >= 2 else (f - 1, NJG - 2 + jg)
                            wt.wait(t, "t", Am0.get(pj))
                            if jg < 2 and f >= 1:
                                wt.wait(t, "t", Am1.get((f - 1, HT - 1)))
                            base = PS_Y0 if jg % 2 == 0 else PS_Y1
                            mm = None
                            for jj in range(2):
                                for j in range(KT):
                                    mm = t.matmul(ps[:, base + 512 * jj:base + 512 * (jj + 1)],
                                                  w0b[:, jg % 2, j, P * jj:P * (jj + 1)],
                                                  u2f[:, f % 2, j, :],
                                                  start=(j == 0), stop=(j == KT - 1))
                            Tm0[(f, jg)] = sT.inc(mm)

                        @blkB.scalar
                        def _(sc, f=f, jg=jg):
                            wt.wait(sc, "sc", Tm0[(f, jg)])
                            if f >= 1:
                                wt.wait(sc, "sc", Tm1.get((f - 1, HT - 1)))
                            base = PS_Y0 if jg % 2 == 0 else PS_Y1
                            ins = None
                            for jj in range(2):
                                ins = sc.activation(a0b[:, 2 * jg + jj, :],
                                                    ps[:, base + 512 * jj:base + 512 * (jj + 1)],
                                                    AF.Relu, bias=zcol[:], scale=s0c2[:])
                            Am0[(f, jg)] = sA.inc(ins)

                    for jg in range(HT):
                        @blkB.sync
                        def _(sync, f=f, jg=jg):
                            if f == 0 and jg == 0:
                                for tok in misc["k1_done"]:
                                    wt.wait(sync, "sync", tok)
                            pj = (f, jg - 2) if jg >= 2 else (f - 1, HT - 2 + jg)
                            wt.wait(sync, "sync", Tm1.get(pj))
                            Lw1[(f, jg)] = dma(sync, q_bw1[jg % 2], w1b[:, jg % 2, :, :],
                                               w1_kt[:, :, P * jg:P * (jg + 1)])

                        @blkB.tensor
                        def _(t, f=f, jg=jg):
                            wt.wait(t, "t", Lw1[(f, jg)])
                            wt.wait(t, "t", Am0[(f, NJG - 1)])
                            if jg >= 2:
                                wt.wait(t, "t", Am1.get((f, jg - 2)))
                            base = PS_Y0 if jg % 2 == 0 else PS_Y1
                            mm = None
                            for k in range(HT):
                                mm = t.matmul(ps[:, base:base + 512],
                                              w1b[:, jg % 2, k, :],
                                              a0b[:, k, :],
                                              start=(k == 0), stop=(k == HT - 1))
                            Tm1[(f, jg)] = sT.inc(mm)

                        @blkB.scalar
                        def _(sc, f=f, jg=jg):
                            wt.wait(sc, "sc", Tm1[(f, jg)])
                            if f >= 1:
                                wt.wait(sc, "sc", Tm2g.get((f - 1, 15)))
                            base = PS_Y0 if jg % 2 == 0 else PS_Y1
                            ins = sc.activation(a1b[:, jg, :], ps[:, base:base + 512],
                                                AF.Relu, bias=zcol[:], scale=scola[:, 1:2])
                            Am1[(f, jg)] = sA.inc(ins)

                    def _m2_wave(wv):
                        for kg in range(8):
                            gi = 8 * wv + kg
                            @blkB.sync
                            def _(sync, f=f, gi=gi):
                                if f == 0 and gi == 0:
                                    for tok in misc["k2_done"]:
                                        wt.wait(sync, "sync", tok)
                                pj = (f, gi - 2) if gi >= 2 else (f - 1, 14 + gi)
                                wt.wait(sync, "sync", Tm2g.get(pj))
                                Lw2[(f, gi)] = dma(sync, q_bw2[gi % 2], w2b[:, gi % 2, :, :],
                                                   w2_kt[:, 4 * (gi % 8):4 * (gi % 8) + 4, :])

                            @blkB.tensor
                            def _(t, f=f, wv=wv, kg=kg, gi=gi):
                                wt.wait(t, "t", Lw2[(f, gi)])
                                wt.wait(t, "t", Am1[(f, HT - 1)])
                                if kg == 0 and wv == 0 and f >= 1:
                                    wt.wait(t, "t", Aev2.get((f - 1, TSUB - 1)))
                                if kg == 0 and wv == 1:
                                    wt.wait(t, "t", Aev2.get((f, 1)))
                                mm = None
                                for tw in range(2):
                                    ts_ = 2 * wv + tw
                                    for kk in range(4):
                                        k = 4 * kg + kk
                                        for h in range(2):
                                            mm = t.matmul(
                                                ps[:, 2048 + 1024 * tw + 512 * h:
                                                   2048 + 1024 * tw + 512 * (h + 1)],
                                                a1b[:, k, P * ts_:P * (ts_ + 1)],
                                                w2b[:, gi % 2, kk, 512 * h:512 * (h + 1)],
                                                start=(k == 0), stop=(k == HT - 1),
                                                skip_group_check=True)
                                Tm2g[(f, gi)] = sT.inc(mm)

                    def _ev(sc, f, ts_):
                        wt.wait(sc, "sc", Tm2g[(f, 7 if ts_ < 2 else 15)])
                        if f >= 1 or ts_ >= 2:
                            pj = (f, ts_ - 2) if ts_ >= 2 else (f - 1, TSUB - 2 + ts_)
                            wt.wait(sc, "sc", Dob.get(pj))
                        Aev2[(f, ts_)] = sA.inc(
                            sc.activation(yb2[:, ts_ % 2, :],
                                          ps[:, 2048 + 1024 * (ts_ % 2):2048 + 1024 * (ts_ % 2) + D],
                                          AF.Copy, scale=scola[:, 2:3],
                                          accum_out=oms[:, ts_:ts_ + 1]))

                    def _sq(sc, f, ts_):
                        sc.drain()
                        Asq2[(f, ts_)] = sA.inc(
                            sc.activation(sqdB[:], yb2[:, ts_ % 2, :], AF.Square,
                                          accum_out=oms[:, 4 + ts_:5 + ts_]))

                    def _dve_stats(v, f, ts_):
                        wt.wait(v, "v", Asq2[(f, ts_)])
                        v.tensor_scalar(oms[:, 8:9], oms[:, ts_:ts_ + 1], 1.0 / D,
                                        None, ALU.mult)
                        v.drain()
                        v.tensor_scalar(oms[:, 9:10], oms[:, 8:9], oms[:, 8:9],
                                        None, ALU.mult)
                        v.drain()
                        Dst2[(f, ts_)] = sD.inc(
                            v.tensor_scalar(oms[:, 10:11], oms[:, 4 + ts_:5 + ts_],
                                            1.0 / D, oms[:, 9:10], ALU.mult, ALU.subtract))

                    def _sqrt(sc, f, ts_):
                        wt.wait(sc, "sc", Dst2[(f, ts_)])
                        Asr2[(f, ts_)] = sA.inc(
                            sc.activation(oms[:, 11:12], oms[:, 10:11], AF.Sqrt,
                                          bias=epscol[:], scale=1.0))

                    def _out(v, f, ts_):
                        wt.wait(v, "v", Asr2[(f, ts_)])
                        wt.wait(v, "v", Gx1L[f])
                        v.reciprocal(oms[:, 10:11], oms[:, 11:12])
                        v.drain()
                        v.tensor_scalar(tpB[:], yb2[:, ts_ % 2, :], oms[:, 8:9],
                                        oms[:, 10:11], ALU.subtract, ALU.mult)
                        v.drain()
                        if f >= 1 or ts_ >= 2:
                            pj = (f, ts_ - 2) if ts_ >= 2 else (f - 1, TSUB - 2 + ts_)
                            wt.wait(v, "v", Oob.get(pj))
                        Dob[(f, ts_)] = sD.inc(
                            v.tensor_tensor(obB[:, ts_ % 2, :], tpB[:],
                                            x1i[:, f % 2, ts_, :], ALU.add))

                    def _store(g, f, ts_):
                        wt.wait(g, "g", Dob[(f, ts_)])
                        Oob[(f, ts_)] = dma(g, q_bo[ts_ % 2], y_s[:, TSUB * f + ts_, :],
                                            obB[:, ts_ % 2, :])

                    _m2_wave(0)

                    @blkB.scalar
                    def _(sc, f=f):
                        _ev(sc, f, 0)
                        _ev(sc, f, 1)

                    _m2_wave(1)

                    for ts_ in range(TSUB):
                        @blkB.scalar
                        def _(sc, f=f, ts_=ts_):
                            _sq(sc, f, ts_)

                        @blkB.vector
                        def _(v, f=f, ts_=ts_):
                            _dve_stats(v, f, ts_)

                        @blkB.scalar
                        def _(sc, f=f, ts_=ts_):
                            _sqrt(sc, f, ts_)

                        @blkB.vector
                        def _(v, f=f, ts_=ts_):
                            _out(v, f, ts_)

                        @blkB.gpsimd
                        def _(g, f=f, ts_=ts_):
                            _store(g, f, ts_)

                        if ts_ + 2 < TSUB:
                            @blkB.scalar
                            def _(sc, f=f, ts_=ts_):
                                _ev(sc, f, ts_ + 2)

                @blkB.gpsimd
                def _(g):
                    for qq in q_bo:
                        wt.wait(g, "g", (qq, qq.n))

                @blkB.scalar
                def _(sc):
                    for qq in q_bu + [q_dbg]:
                        wt.wait(sc, "sc", (qq, qq.n))

        fin = contextlib.ExitStack()
        with fin:
            blkF = fin.enter_context(nc.Block("fin"))

            @blkF.sync
            def _(sync):
                for qq in q_bo + [q_dbg] + q_a3s:
                    wt.wait(sync, "sync", (qq, qq.n))

    return nc


_NC_CACHE = {}


def _get_nc(debug_dump=None, nolora=False, skip_small=False):
    key = (debug_dump, nolora, skip_small)
    if key not in _NC_CACHE:
        _NC_CACHE[key] = build_program(debug_dump, nolora, skip_small)
    return _NC_CACHE[key]


def kernel(L0_attn_x, L0_attn_Wb, L0_attn_init_A, L0_attn_init_B, L0_attn_ln_gamma,
           L0_attn_ln_beta, L0_attn_Wp, L0_ffn_k0, L0_ffn_k1, L0_ffn_k2,
           L0_ffn_s0, L0_ffn_s1, L0_ffn_s2, _debug_dump=None, _nolora=False, _skip_small=False):
    x = np.ascontiguousarray(np.asarray(L0_attn_x, dtype=np.float32))
    wb = np.ascontiguousarray(np.asarray(L0_attn_Wb, dtype=np.float32))
    wp = np.ascontiguousarray(np.asarray(L0_attn_Wp, dtype=np.float32))
    a0 = np.ascontiguousarray(np.asarray(L0_attn_init_A, dtype=np.float32))
    b0 = np.ascontiguousarray(np.asarray(L0_attn_init_B, dtype=np.float32))
    gam = np.ascontiguousarray(np.asarray(L0_attn_ln_gamma, dtype=np.float32)).reshape(1, D)
    bet = np.ascontiguousarray(np.asarray(L0_attn_ln_beta, dtype=np.float32)).reshape(1, D)
    k0 = np.ascontiguousarray(np.asarray(L0_ffn_k0)).view(np.uint8)
    k1 = np.ascontiguousarray(np.asarray(L0_ffn_k1)).view(np.uint8)
    k2 = np.ascontiguousarray(np.asarray(L0_ffn_k2)).view(np.uint8)
    s0 = np.asarray(L0_ffn_s0, dtype=np.float32).reshape(1, 1)
    s1 = np.asarray(L0_ffn_s1, dtype=np.float32).reshape(1, 1)
    s2 = np.asarray(L0_ffn_s2, dtype=np.float32).reshape(1, 1)
    eye = np.eye(P, dtype=np.float32)
    ones = np.ones((P, P), dtype=np.float32)

    nc = _get_nc(_debug_dump, _nolora, _skip_small)
    in_maps = []
    for core in range(NCORES):
        in_maps.append(dict(
            x=np.ascontiguousarray(x[NSEQ * core:NSEQ * (core + 1)]),
            wb=wb, wp=wp, a0=a0, b0=b0, gam=gam, bet=bet,
            k0=k0, k1=k1, k2=k2, s0=s0, s1=s1, s2=s2, eye=eye, ones=ones,
        ))
    res = run_bass_kernel_spmd(nc, in_maps, core_ids=list(range(NCORES)))
    out = np.concatenate([r["y"] for r in res.results], axis=0)
    if _debug_dump == "x1":
        x1 = np.concatenate([r["x1dbg"] for r in res.results], axis=0)
        return out, x1
    if _debug_dump == "all":
        dbg = {"x1": np.concatenate([r["x1dbg"] for r in res.results], axis=0),
               "ao": np.concatenate([r["aodbg"] for r in res.results], axis=0)}
        dbg["cst"] = res.results[0]["cstdbg"]
        for k in ["u_dr", "xn_dr", "v_dr", "u2_dr", "x1_dr", "w0_dr", "wb_bf"]:
            dbg[k] = np.stack([r[k] for r in res.results], axis=0)
        return out, dbg
    return out


# version 10 (final)
# speedup vs baseline: 1.1049x; 1.0192x over previous
import numpy as np
import contextlib
import concourse.bass as bass
import concourse.mybir as mybir
from concourse.bass_utils import run_bass_kernel_spmd

F32 = mybir.dt.float32
BF16 = mybir.dt.bfloat16
U8 = mybir.dt.uint8
FP8 = mybir.dt.float8e4
AF = mybir.ActivationFunctionType
ALU = mybir.AluOpType

P = 128
D = 1024
KT = 8
S = 2048
NT = S // P
CH = 256
NCH = S // CH
R = 16
H1 = 4096
HT = H1 // P
FT = 512
NFT = S // FT
TSUB = FT // P
LR = 0.01 * 2.0
EPS = 1e-5
NSEQ = 2
NCORES = 8

PS_Y0 = 0
PS_Y1 = 1024
PS_PI = 2048
PS_PE = 2560
PS_DB = 3072
PS_N = 3584
PS_BC = 3586
PS_MID = 3600


class Sem:
    def __init__(self, stack, nc, name):
        self.h = stack.enter_context(nc.semaphore(name))
        self.n = 0

    def inc(self, instr, k=1):
        instr.then_inc(self.h, k)
        self.n += k
        return (self, self.n)


class Waiter:
    def __init__(self):
        self.seen = {}

    def wait(self, eng, ename, tok):
        if tok is None:
            return
        sem, val = tok
        if val is None or val <= 0:
            return
        key = (ename, id(sem))
        if self.seen.get(key, -1) >= val:
            return
        eng.wait_ge(sem.h, val)
        self.seen[key] = val


SERIALIZE_EVAC = False
NO_SMALL_MM = False


def build_program(debug_dump=None, nolora=False, skip_small=False):
    nc = bass.Bass("TRN2", target_bir_lowering=False)

    x_in = nc.dram_tensor("x", [NSEQ, S, D], F32, kind="ExternalInput").ap()
    wb_in = nc.dram_tensor("wb", [D, D], F32, kind="ExternalInput").ap()
    wp_in = nc.dram_tensor("wp", [D, D], F32, kind="ExternalInput").ap()
    a0_in = nc.dram_tensor("a0", [D, R], F32, kind="ExternalInput").ap()
    b0_in = nc.dram_tensor("b0", [R, D], F32, kind="ExternalInput").ap()
    gam_in = nc.dram_tensor("gam", [1, D], F32, kind="ExternalInput").ap()
    bet_in = nc.dram_tensor("bet", [1, D], F32, kind="ExternalInput").ap()
    k0_in = nc.dram_tensor("k0", [2 * D, H1], U8, kind="ExternalInput").ap()
    k1_in = nc.dram_tensor("k1", [H1, H1], U8, kind="ExternalInput").ap()
    k2_in = nc.dram_tensor("k2", [H1, D], U8, kind="ExternalInput").ap()
    s0_in = nc.dram_tensor("s0", [1, 1], F32, kind="ExternalInput").ap()
    s1_in = nc.dram_tensor("s1", [1, 1], F32, kind="ExternalInput").ap()
    s2_in = nc.dram_tensor("s2", [1, 1], F32, kind="ExternalInput").ap()
    eye_in = nc.dram_tensor("eye", [P, P], F32, kind="ExternalInput").ap()
    ones_in = nc.dram_tensor("ones", [P, P], F32, kind="ExternalInput").ap()
    y_out = nc.dram_tensor("y", [NSEQ, S, D], F32, kind="ExternalOutput").ap()
    x1_out = None
    ao_out = None
    if debug_dump in ("x1", "all"):
        x1_out = nc.dram_tensor("x1dbg", [NSEQ, S, D], F32, kind="ExternalOutput").ap()
    if debug_dump == "all":
        ao_out = nc.dram_tensor("aodbg", [NSEQ, S, D], F32, kind="ExternalOutput").ap()


    scratch_kind = "ExternalOutput" if debug_dump == "all" else "Internal"
    wb_bf = nc.dram_tensor("wb_bf", [D, D], BF16, kind=scratch_kind).ap()
    wp_bf = nc.dram_tensor("wp_bf", [D, D], BF16, kind="Internal").ap()
    w0_dr = nc.dram_tensor("w0_dr", [D, H1], FP8, kind=scratch_kind).ap()
    w1_dr = nc.dram_tensor("w1_dr", [H1, H1], FP8, kind="Internal").ap()
    w2_dr = nc.dram_tensor("w2_dr", [H1, D], FP8, kind="Internal").ap()
    u_dr = nc.dram_tensor("u_dr", [S, D], BF16, kind=scratch_kind).ap()
    xn_dr = nc.dram_tensor("xn_dr", [S, D], BF16, kind=scratch_kind).ap()
    v_dr = nc.dram_tensor("v_dr", [S, D], BF16, kind=scratch_kind).ap()
    u2_dr = nc.dram_tensor("u2_dr", [S, D], BF16, kind=scratch_kind).ap()
    x1_dr = nc.dram_tensor("x1_dr", [S, D], F32, kind=scratch_kind).ap()

    u_tmv = u_dr.rearrange("(t p) k -> p t k", p=P)
    v_tmv = v_dr.rearrange("(t p) k -> p t k", p=P)
    u2_tmv = u2_dr.rearrange("(t p) k -> p t k", p=P)
    x1_tmv = x1_dr.rearrange("(t p) k -> p t k", p=P)
    w0_kt = w0_dr.rearrange("(t p) j -> p t j", p=P)
    w1_kt = w1_dr.rearrange("(t p) j -> p t j", p=P)
    w2_kt = w2_dr.rearrange("(t p) j -> p t j", p=P)
    k2_kt = k2_in.rearrange("(t p) n -> p t n", p=P)

    st = contextlib.ExitStack()
    wt = Waiter()
    with st:
        eyeb = st.enter_context(nc.sbuf_tensor("eyeb", [P, P], F32))
        onesb = st.enter_context(nc.sbuf_tensor("onesb", [P, P], F32))
        onesbb = st.enter_context(nc.sbuf_tensor("onesbb", [P, P], BF16))
        grow = st.enter_context(nc.sbuf_tensor("grow", [P, D], F32))
        brow = st.enter_context(nc.sbuf_tensor("brow", [P, D], F32))
        grep = st.enter_context(nc.sbuf_tensor("grep", [P, D], F32))
        brep = st.enter_context(nc.sbuf_tensor("brep", [P, D], F32))
        spack = st.enter_context(nc.sbuf_tensor("spack", [P, 4], F32))
        scola = st.enter_context(nc.sbuf_tensor("scola", [P, 4], F32))
        s0c2 = st.enter_context(nc.sbuf_tensor("s0c2", [P, 1], F32))
        a0f = st.enter_context(nc.sbuf_tensor("a0f", [P, KT, R], F32))
        a0nb = st.enter_context(nc.sbuf_tensor("a0nb", [P, KT, R], BF16))
        a0t2 = st.enter_context(nc.sbuf_tensor("a0t2", [R, D], F32))
        b0f = st.enter_context(nc.sbuf_tensor("b0f", [R, D], F32))
        b0tf = st.enter_context(nc.sbuf_tensor("b0tf", [P, KT, R], F32))
        b0tb = st.enter_context(nc.sbuf_tensor("b0tb", [P, KT, R], BF16))
        zcol = st.enter_context(nc.sbuf_tensor("zcol", [P, 1], F32))
        epscol = st.enter_context(nc.sbuf_tensor("epscol", [P, 1], F32))
        zrow = st.enter_context(nc.sbuf_tensor("zrow", [1, D], BF16))
        SAb = st.enter_context(nc.sbuf_tensor("SAb", [R, D], F32))
        SBb = st.enter_context(nc.sbuf_tensor("SBb", [P, P], F32))

        ps = st.enter_context(nc.psum_tensor("psar", [P, 4096], F32))

        sT = Sem(st, nc, "sT")
        sD = Sem(st, nc, "sD")
        sA = Sem(st, nc, "sA")
        sP = Sem(st, nc, "sP")
        _semn = [0]

        def mk(tag):
            _semn[0] += 1
            return Sem(st, nc, f"q{_semn[0]}_{tag}")

        def dma(eng, sem, out, in_, transpose=False):
            return sem.inc(eng.dma_start(out=out, in_=in_, transpose=transpose), 16)

        q_cl = mk("cl")
        q_cvL = [mk("cvL0"), mk("cvL1")]
        q_cvS = [mk("cvS0"), mk("cvS1")]
        q_cvA = mk("cvA")
        q_cvAS = [mk("cvAS0"), mk("cvAS1")]
        q_cvP = mk("cvP")
        q_cvPS = [mk("cvPS0"), mk("cvPS1")]
        q_zr = mk("zr")
        q_a1x = [mk("a1x0"), mk("a1x1"), mk("a1x2")]
        q_a1u = [mk("a1u0"), mk("a1u1")]
        q_a1n = [mk("a1n0"), mk("a1n1")]
        q_a2wp = mk("a2wp")
        q_a2s = [mk("a2s0"), mk("a2s1")]
        q_a2v = [mk("a2v0"), mk("a2v1")]
        q_a3wb = mk("a3wb")
        q_a3u = [mk("a3u0"), mk("a3u1")]
        q_a3v = [mk("a3v0"), mk("a3v1")]
        q_a3x = [mk("a3x0"), mk("a3x1")]
        q_a3s = [mk("a3s0"), mk("a3s1")]
        q_bx = [mk("bx0"), mk("bx1")]
        q_bu = [mk("bu0"), mk("bu1")]
        q_bf = [mk("bf0"), mk("bf1")]
        q_bw0 = [mk("bw00"), mk("bw01")]
        q_bw1 = [mk("bw10"), mk("bw11")]
        q_bw2 = [mk("bw20"), mk("bw21")]
        q_bo = [mk("bo0"), mk("bo1")]
        q_dbg = mk("dbg")

        misc = {}

        for s in range(NSEQ):
            x_s = x_in[s].rearrange("(t p) k -> p t k", p=P)
            y_s = y_out[s].rearrange("(t p) k -> p t k", p=P)

            front = contextlib.ExitStack()
            with front:
                if s == 0:
                    cf32 = front.enter_context(nc.sbuf_tensor("p0cf32", [P, 2, D], F32))
                    cbf = front.enter_context(nc.sbuf_tensor("p0cbf", [P, 2, D], BF16))
                    u8a = front.enter_context(nc.sbuf_tensor("p0u8a", [P, 2, H1], U8))
                    u8b = front.enter_context(nc.sbuf_tensor("p0u8b", [P, 2, H1], U8))
                    f8d = front.enter_context(nc.sbuf_tensor("p0f8d", [P, 2, H1], FP8))
                    u8s = front.enter_context(nc.sbuf_tensor("p0u8s", [P, 2, H1], U8))
                    f8s = front.enter_context(nc.sbuf_tensor("p0f8s", [P, 2, H1], FP8))
                    u8p = front.enter_context(nc.sbuf_tensor("p0u8p", [P, 2, H1], U8))
                    f8p = front.enter_context(nc.sbuf_tensor("p0f8p", [P, 2, H1], FP8))
                xin = front.enter_context(nc.sbuf_tensor(f"a1xin{s}", [P, 3, D], F32))
                st6 = front.enter_context(nc.sbuf_tensor(f"a1st6{s}", [P, 12], F32))
                mvb = front.enter_context(nc.sbuf_tensor(f"a1mv{s}", [P, 2], F32))
                stdb = front.enter_context(nc.sbuf_tensor(f"a1std{s}", [P, 3], F32))
                rsb = front.enter_context(nc.sbuf_tensor(f"a1rs{s}", [P, 2], F32))
                acol = front.enter_context(nc.sbuf_tensor(f"a1ac{s}", [P, 1], F32))
                ubf = front.enter_context(nc.sbuf_tensor(f"a1ubf{s}", [P, 2, D], BF16))
                xtb = front.enter_context(nc.sbuf_tensor(f"a1xt{s}", [P, 2, D], F32))
                xgb = front.enter_context(nc.sbuf_tensor(f"a1xg{s}", [P, 2, D], F32))
                xnb = front.enter_context(nc.sbuf_tensor(f"a1xn{s}", [P, 2, D], BF16))

                blkF = front.enter_context(nc.Block(f"front{s}"))
                if s == 0:

                    @blkF.vector
                    def _(v):
                        v.memset(grow[:], 0.0)
                        v.memset(brow[:], 0.0)
                        misc["d_pad0"] = sD.inc(v.memset(spack[:], 0.0))

                    @blkF.sync
                    def _(sync):
                        wt.wait(sync, "sync", misc["d_pad0"])
                        for dst, src in [(eyeb[:], eye_in), (onesb[:], ones_in),
                                         (grow[0:1, :], gam_in), (brow[0:1, :], bet_in),
                                         (b0f[:], b0_in)]:
                            dma(sync, q_cl, dst, src)
                        dma(sync, q_cl, a0f[:], a0_in.rearrange("(j p) r -> p j r", p=P))
                        for i, sc_ in enumerate([s0_in, s1_in, s2_in]):
                            dma(sync, q_cl, spack[0:1, i:i + 1], sc_)
                        misc["Lconst"] = (q_cl, q_cl.n)

                    @blkF.vector
                    def _(v):
                        v.memset(zcol[:], 0.0)
                        v.memset(epscol[:], EPS)
                        v.memset(zrow[:], 0.0)
                        v.memset(SAb[:], 0.0)
                        v.memset(SBb[:], 0.0)
                        wt.wait(v, "v", misc["Lconst"])
                        v.tensor_copy(onesbb[:], onesb[:])
                        misc["d_mem"] = sD.inc(v.tensor_copy(a0nb[:], a0f[:]))

                    @blkF.gpsimd
                    def _(g):
                        wt.wait(g, "g", misc["d_mem"])
                        dma(g, q_zr, xn_dr[S - 1:S, :], zrow[0:1, :])
                        misc["zr"] = (q_zr, q_zr.n)

                    @blkF.tensor
                    def _(t):
                        wt.wait(t, "t", misc["Lconst"])
                        t.matmul(ps[:, 0:512], onesb[:], grow[:, 0:512], start=True, stop=True)
                        mm = t.matmul(ps[:, 512:1024], onesb[:], grow[:, 512:1024],
                                      start=True, stop=True)
                        misc["t_gam"] = sT.inc(mm)

                    @blkF.vector
                    def _(v):
                        wt.wait(v, "v", misc["t_gam"])
                        v.tensor_copy(grep[:, 0:512], ps[:, 0:512])
                        misc["d_gam"] = sD.inc(v.tensor_copy(grep[:, 512:1024], ps[:, 512:1024]))

                    @blkF.tensor
                    def _(t):
                        wt.wait(t, "t", misc["d_gam"])
                        t.matmul(ps[:, 0:512], onesb[:], brow[:, 0:512], start=True, stop=True)
                        mm = t.matmul(ps[:, 512:1024], onesb[:], brow[:, 512:1024],
                                      start=True, stop=True)
                        misc["t_bet"] = sT.inc(mm)

                    @blkF.vector
                    def _(v):
                        wt.wait(v, "v", misc["t_bet"])
                        v.tensor_copy(brep[:, 0:512], ps[:, 0:512])
                        misc["d_bet"] = sD.inc(v.tensor_copy(brep[:, 512:1024], ps[:, 512:1024]))

                    @blkF.tensor
                    def _(t):
                        wt.wait(t, "t", misc["d_bet"])
                        t.matmul(ps[:, PS_PI:PS_PI + 4], onesb[:], spack[:, 0:4],
                                 start=True, stop=True)
                        t.drain()
                        for j in range(KT):
                            t.transpose(ps[0:R, 2560 + 128 * j:2560 + 128 * (j + 1)],
                                        a0f[:, j, :], eyeb[:])
                        mm = None
                        for j in range(KT):
                            mm = t.transpose(ps[:, PS_N + R * j:PS_N + R * (j + 1)],
                                             b0f[0:R, P * j:P * (j + 1)], eyeb[0:R, 0:R])
                        misc["t_misc"] = sT.inc(mm)
                        t.drain()

                    @blkF.vector
                    def _(v):
                        wt.wait(v, "v", misc["t_misc"])
                        v.tensor_copy(scola[:], ps[:, PS_PI:PS_PI + 4])
                        v.tensor_copy(b0tf.rearrange("p j r -> p (j r)")[:], ps[:, PS_N:PS_N + KT * R])
                        v.drain()
                        v.tensor_scalar(s0c2[:], scola[:, 0:1], 2.0, None, ALU.mult)
                        misc["d_misc"] = sD.inc(
                            v.tensor_copy(b0tb.rearrange("p j r -> p (j r)")[:],
                                          ps[:, PS_N:PS_N + KT * R]))

                    @blkF.scalar
                    def _(sc):
                        sc.preload_act_table(AF.Sqrt)
                        wt.wait(sc, "sc", misc["t_misc"])
                        misc["a_misc"] = sA.inc(
                            sc.activation(a0t2[0:R, :], ps[0:R, 2560:3584], AF.Copy,
                                          bias=0.0, scale=2.0))

                    pool_tiles = ([("cast", wp_in, wp_bf, i) for i in range(8)]
                                  + [("cast", wb_in, wb_bf, i) for i in range(8)]
                                  + [("k2", None, None, i) for i in range(8)])
                    poolC = {}
                    poolS = {}
                    for i, (kind, psrc, pdst, j) in enumerate(pool_tiles):
                        @blkF.gpsimd
                        def _(g, i=i, kind=kind, psrc=psrc, pdst=pdst, j=j):
                            wt.wait(g, "g", poolS.get(i - 2))
                            if kind == "cast":
                                li = dma(g, q_cvP, cf32[:, i % 2, :], psrc[P * j:P * (j + 1), :])
                            else:
                                li = dma(g, q_cvP, u8p[:, i % 2, :].rearrange("p (t n) -> p t n", n=D),
                                         k2_kt[:, 4 * j:4 * (j + 1), :])
                            wt.wait(g, "g", li)
                            if kind == "cast":
                                poolC[i] = sP.inc(g.tensor_copy(cbf[:, i % 2, :], cf32[:, i % 2, :]))
                            else:
                                poolC[i] = sP.inc(g.tensor_scalar(f8p[:, i % 2, :], u8p[:, i % 2, :],
                                                                  4.0, -2.0, ALU.mult, ALU.add))
                            g.drain()
                            if kind == "cast":
                                dma(g, q_cvPS[i % 2], pdst[P * j:P * (j + 1), :], cbf[:, i % 2, :])
                            else:
                                dma(g, q_cvPS[i % 2], w2_kt[:, 4 * j:4 * (j + 1), :],
                                    f8p[:, i % 2, :].rearrange("p (t n) -> p t n", n=D))
                            poolS[i] = (q_cvPS[i % 2], q_cvPS[i % 2].n)
                            if i == 7:
                                misc["wp_done"] = [(qq, qq.n) for qq in q_cvPS]
                            if i == 15:
                                misc["wb_done"] = [(qq, qq.n) for qq in q_cvPS]
                            if i == 23:
                                misc["k2_done"] = [(qq, qq.n) for qq in q_cvPS]

                    def emit_dve_conv():
                        dve_tiles = ([("k0", None, None, i) for i in range(8)]
                                     + [("aff", k1_in, w1_dr, i) for i in range(16)])
                        dveL = {}
                        dveC = {}
                        dveS = {}
                        for i, (kind, src2, dst, j) in enumerate(dve_tiles):
                            @blkF.sync
                            def _(sync, i=i, kind=kind, src2=src2, j=j):
                                wt.wait(sync, "sync", dveC.get(i - 2))
                                if kind == "k0":
                                    dma(sync, q_cvL[i % 2], u8a[:, i % 2, :], k0_in[P * j:P * (j + 1), :])
                                    dma(sync, q_cvL[i % 2], u8b[:, i % 2, :],
                                        k0_in[D + P * j:D + P * (j + 1), :])
                                else:
                                    dma(sync, q_cvL[i % 2], u8a[:, i % 2, :], src2[P * j:P * (j + 1), :])
                                dveL[i] = (q_cvL[i % 2], q_cvL[i % 2].n)

                            @blkF.vector
                            def _(v, i=i, kind=kind):
                                wt.wait(v, "v", dveL[i])
                                wt.wait(v, "v", dveS.get(i - 2))
                                if kind == "k0":
                                    ins = v.tensor_tensor(f8d[:, i % 2, :], u8a[:, i % 2, :],
                                                          u8b[:, i % 2, :], ALU.subtract)
                                else:
                                    ins = v.tensor_scalar(f8d[:, i % 2, :], u8a[:, i % 2, :],
                                                          4.0, -2.0, ALU.mult, ALU.add)
                                dveC[i] = sD.inc(ins)

                            @blkF.sync
                            def _(sync, i=i, kind=kind, dst=dst, j=j):
                                wt.wait(sync, "sync", dveC[i])
                                if kind == "k0":
                                    dma(sync, q_cvS[i % 2], w0_dr[P * j:P * (j + 1), :], f8d[:, i % 2, :])
                                else:
                                    dma(sync, q_cvS[i % 2], dst[P * j:P * (j + 1), :], f8d[:, i % 2, :])
                                dveS[i] = (q_cvS[i % 2], q_cvS[i % 2].n)
                                if i == 7:
                                    misc["k0_done"] = [(qq, qq.n) for qq in q_cvS]

                    def emit_act_conv():
                        act_tiles = [(k1_in, w1_dr, 16 + i) for i in range(16)]

                        @blkF.scalar
                        def _(sc):
                            prevS = {}
                            for i, (src2, dst, j) in enumerate(act_tiles):
                                wt.wait(sc, "sc", prevS.get(i - 2))
                                li = dma(sc, q_cvA, u8s[:, i % 2, :], src2[P * j:P * (j + 1), :])
                                wt.wait(sc, "sc", li)
                                sA.inc(sc.activation(f8s[:, i % 2, :], u8s[:, i % 2, :],
                                                     AF.Copy, bias=-2.0, scale=4.0))
                                sc.drain()
                                prevS[i] = dma(sc, q_cvAS[i % 2], dst[P * j:P * (j + 1), :],
                                               f8s[:, i % 2, :])
                Lx = {}
                Dst = {}
                Dub = {}
                Dxt = {}
                Ast = {}
                Pxn = {}
                Ust = {}
                Nst = {}

                for i in range(NT):
                    @blkF.sync
                    def _(sync, i=i):
                        wt.wait(sync, "sync", Dxt.get(i - 3))
                        Lx[i] = dma(sync, q_a1x[i % 3], xin[:, i % 3, :], x_s[:, i, :])

                    @blkF.vector
                    def _(v, i=i):
                        wt.wait(v, "v", Lx[i])
                        v.bn_stats(st6[:, 0:6], xin[:, i % 3, 0:512])
                        v.bn_stats(st6[:, 6:12], xin[:, i % 3, 512:1024])
                        v.drain()
                        Dst[i] = sD.inc(v.bn_aggr(mvb[:], st6[:]))

                    @blkF.scalar
                    def _(sc, i=i):
                        wt.wait(sc, "sc", Dst[i])
                        Ast[i] = sA.inc(sc.activation(stdb[:, 0:1], mvb[:, 1:2], AF.Sqrt,
                                                      bias=epscol[:], scale=1.0))

                    @blkF.vector
                    def _(v, i=i):
                        wt.wait(v, "v", Ast[i])
                        v.reciprocal(rsb[:, 0:1], stdb[:, 0:1])
                        v.drain()
                        v.tensor_scalar(stdb[:, 1:2], mvb[:, 1:2], rsb[:, 0:1], rsb[:, 0:1],
                                        ALU.mult, ALU.mult)
                        wt.wait(v, "v", Ust.get(i - 2))
                        Dub[i] = sD.inc(v.tensor_scalar(ubf[:, i % 2, :], xin[:, i % 3, :],
                                                        mvb[:, 0:1], rsb[:, 0:1],
                                                        ALU.subtract, ALU.mult))

                    @blkF.scalar
                    def _(sc, i=i):
                        wt.wait(sc, "sc", Dub[i])
                        Ast[i + 100] = sA.inc(sc.activation(stdb[:, 2:3], stdb[:, 1:2],
                                                            AF.Sqrt, bias=epscol[:], scale=1.0))

                    @blkF.vector
                    def _(v, i=i):
                        wt.wait(v, "v", Ast[i + 100])
                        v.reciprocal(rsb[:, 1:2], stdb[:, 2:3])
                        v.drain()
                        v.tensor_scalar(acol[:], rsb[:, 0:1], rsb[:, 1:2], None, ALU.mult)
                        v.drain()
                        Dxt[i] = sD.inc(v.tensor_scalar(xtb[:, i % 2, :], xin[:, i % 3, :],
                                                        mvb[:, 0:1], acol[:],
                                                        ALU.subtract, ALU.mult))
                        v.drain()
                        v.tensor_tensor(xgb[:, i % 2, :], xtb[:, i % 2, :],
                                        grep[:], ALU.mult)
                        v.drain()
                        wt.wait(v, "v", Nst.get(i - 2))
                        Pxn[i] = sD.inc(v.tensor_tensor(xnb[:, i % 2, :], xgb[:, i % 2, :],
                                                        brep[:], ALU.add))

                    @blkF.scalar
                    def _(sc, i=i):
                        wt.wait(sc, "sc", Dub[i])
                        Ust[i] = dma(sc, q_a1u[i % 2], u_tmv[:, i, :], ubf[:, i % 2, :])
                        wt.wait(sc, "sc", Pxn[i])
                        if i == 0:
                            Nst[i] = dma(sc, q_a1n[0], xn_dr[0:P - 1, :], xnb[1:P, 0, :])
                        else:
                            Nst[i] = dma(sc, q_a1n[i % 2],
                                         xn_dr[P * i - 1:P * (i + 1) - 1, :], xnb[:, i % 2, :])

                @blkF.scalar
                def _(sc):
                    for qq in q_a1u + q_a1n:
                        wt.wait(sc, "sc", (qq, qq.n))

                A1u_done = [(qq, qq.n) for qq in q_a1u]
                A1n_done = [(qq, qq.n) for qq in q_a1n]

                wpT = front.enter_context(nc.sbuf_tensor(f"a2wp{s}", [P, KT, D], BF16,
                                                         is_transpose_dest=True))
                shb = front.enter_context(nc.sbuf_tensor(f"a2sh{s}", [P, 2, KT, CH], BF16,
                                                         is_transpose_dest=True))
                vob = front.enter_context(nc.sbuf_tensor(f"a2vo{s}", [P, 2, 2, D], BF16))
                LwpT = [None]
                Lsh = {}
                Tv = {}
                Av = {}
                Sv = {}

                @blkF.sync
                def _(sync):
                    for tok in misc["wp_done"]:
                        wt.wait(sync, "sync", tok)
                    for j in range(KT):
                        dma(sync, q_a2wp, wpT[:, j, :], wp_bf[0:D, P * j:P * (j + 1)],
                            transpose=True)
                    LwpT[0] = (q_a2wp, q_a2wp.n)

                for c in range(NCH):
                    @blkF.sync
                    def _(sync, c=c):
                        wt.wait(sync, "sync", Tv.get((c - 2, 1)))
                        wt.wait(sync, "sync", Nst[2 * c + 1])
                        wt.wait(sync, "sync", Nst[min(2 * c + 2, NT - 1)])
                        wt.wait(sync, "sync", misc["zr"])
                        for j in range(KT):
                            dma(sync, q_a2s[c % 2], shb[:, c % 2, j, :],
                                xn_dr[CH * c:CH * (c + 1), P * j:P * (j + 1)], transpose=True)
                        Lsh[c] = (q_a2s[c % 2], q_a2s[c % 2].n)

                    @blkF.tensor
                    def _(t, c=c):
                        wt.wait(t, "t", Lsh[c])
                        wt.wait(t, "t", LwpT[0])
                        wt.wait(t, "t", Av.get((c - 1, 1)))
                        for ts_ in range(2):
                            base = PS_Y0 if ts_ == 0 else PS_Y1
                            for h in range(2):
                                mm = None
                                for j in range(KT):
                                    mm = t.matmul(ps[:, base + 512 * h:base + 512 * (h + 1)],
                                                  shb[:, c % 2, j, P * ts_:P * (ts_ + 1)],
                                                  wpT[:, j, 512 * h:512 * (h + 1)],
                                                  start=(j == 0), stop=(j == KT - 1))
                            Tv[(c, ts_)] = sT.inc(mm)

                    @blkF.scalar
                    def _(sc, c=c):
                        wt.wait(sc, "sc", Sv.get(c - 2))
                        for ts_ in range(2):
                            base = PS_Y0 if ts_ == 0 else PS_Y1
                            wt.wait(sc, "sc", Tv[(c, ts_)])
                            Av[(c, ts_)] = sA.inc(
                                sc.activation(vob[:, c % 2, ts_, :], ps[:, base:base + D],
                                              AF.Copy, bias=0.0, scale=1.0))
                        sc.drain()
                        for ts_ in range(2):
                            Sv[c] = dma(sc, q_a2v[c % 2], v_tmv[:, 2 * c + ts_, :],
                                        vob[:, c % 2, ts_, :])

                @blkF.scalar
                def _(sc):
                    for qq in q_a2v:
                        wt.wait(sc, "sc", (qq, qq.n))

                if s == 0:
                    emit_act_conv()
                    misc["k1_act"] = [(qq, qq.n) for qq in q_cvAS]
                    misc["conv_all"] = [(qq, qq.n) for qq in
                                        (q_cvAS + q_cvPS + [q_cvA, q_cvP])]

            SvD = dict(Sv)

            a3 = contextlib.ExitStack()
            Gx1_all = {}
            with a3:
                wbT = a3.enter_context(nc.sbuf_tensor(f"a3wb{s}", [P, KT, D], BF16,
                                                      is_transpose_dest=True))
                ufm = a3.enter_context(nc.sbuf_tensor(f"a3uf{s}", [P, 2, KT, CH], BF16,
                                                      is_transpose_dest=True))
                vfm = a3.enter_context(nc.sbuf_tensor(f"a3vf{s}", [P, 2, KT, CH], BF16,
                                                      is_transpose_dest=True))
                utm = a3.enter_context(nc.sbuf_tensor(f"a3ut{s}", [P, 2, 2, D], BF16))
                vtm = a3.enter_context(nc.sbuf_tensor(f"a3vt{s}", [P, 2, 2, D], BF16))
                xck = a3.enter_context(nc.sbuf_tensor(f"a3xc{s}", [P, 2, 2, D], F32))
                pib = a3.enter_context(nc.sbuf_tensor(f"a3pi{s}", [P, 2, P], BF16))
                peb = a3.enter_context(nc.sbuf_tensor(f"a3pe{s}", [P, 32], BF16))
                dAc = a3.enter_context(nc.sbuf_tensor(f"a3da{s}", [R, D], F32))
                dBc = a3.enter_context(nc.sbuf_tensor(f"a3db{s}", [P, P], F32))
                aef = a3.enter_context(nc.sbuf_tensor(f"a3ae{s}", [P, D], BF16))
                bef = a3.enter_context(nc.sbuf_tensor(f"a3be{s}", [P, KT, P], BF16))
                midb = a3.enter_context(nc.sbuf_tensor(f"a3mi{s}", [P, CH], BF16))
                nsq = a3.enter_context(nc.sbuf_tensor(f"a3nq{s}", [1, 8], F32))
                ccb = a3.enter_context(nc.sbuf_tensor(f"a3cc{s}", [P, 2], BF16))
                nab = a3.enter_context(nc.sbuf_tensor(f"a3na{s}", [P, 1], F32))
                nbb = a3.enter_context(nc.sbuf_tensor(f"a3nb{s}", [P, 1], F32))
                nabb = a3.enter_context(nc.sbuf_tensor(f"a3nab{s}", [P, 2], BF16))
                bcb = a3.enter_context(nc.sbuf_tensor(f"a3bc{s}", [P, 2], F32))
                sqd = a3.enter_context(nc.sbuf_tensor(f"a3sq{s}", [P, D], BF16))
                ybf = a3.enter_context(nc.sbuf_tensor(f"a3yb{s}", [P, 2, D], F32))
                syb = a3.enter_context(nc.sbuf_tensor(f"a3sy{s}", [P, 2], F32))
                sy2b = a3.enter_context(nc.sbuf_tensor(f"a3s2{s}", [P, 2], F32))
                lnc = a3.enter_context(nc.sbuf_tensor(f"a3ln{s}", [P, 10], F32))
                tpb = a3.enter_context(nc.sbuf_tensor(f"a3tp{s}", [P, 2, D], F32))
                x1b = a3.enter_context(nc.sbuf_tensor(f"a3x1{s}", [P, 2, 2, D], F32))

                if s == 0:
                    cva = a3.enter_context(nc.sbuf_tensor("a3cva", [P, 2, H1], U8))
                    cvb = a3.enter_context(nc.sbuf_tensor("a3cvb", [P, 2, H1], U8))
                    cvf = a3.enter_context(nc.sbuf_tensor("a3cvf", [P, 2, H1], FP8))

                blkA3 = a3.enter_context(nc.Block(f"a3_{s}"))

                if s == 0:
                    cvS = {}
                    cv_tiles = ([("k0", i) for i in range(8)]
                                + [("aff", i) for i in range(16)])
                    for i, (kind, j) in enumerate(cv_tiles):
                        @blkA3.gpsimd
                        def _(g, i=i, kind=kind, j=j):
                            wt.wait(g, "g", cvS.get(i - 2))
                            if kind == "k0":
                                dma(g, q_cvL[i % 2], cva[:, i % 2, :], k0_in[P * j:P * (j + 1), :])
                                li = dma(g, q_cvL[i % 2], cvb[:, i % 2, :],
                                         k0_in[D + P * j:D + P * (j + 1), :])
                            else:
                                li = dma(g, q_cvL[i % 2], cva[:, i % 2, :], k1_in[P * j:P * (j + 1), :])
                            wt.wait(g, "g", li)
                            if kind == "k0":
                                g.tensor_tensor(cvf[:, i % 2, :], cva[:, i % 2, :],
                                                cvb[:, i % 2, :], ALU.subtract)
                            else:
                                g.tensor_scalar(cvf[:, i % 2, :], cva[:, i % 2, :],
                                                4.0, -2.0, ALU.mult, ALU.add)
                            g.drain()
                            if kind == "k0":
                                cvS[i] = dma(g, q_cvS[i % 2], w0_dr[P * j:P * (j + 1), :],
                                             cvf[:, i % 2, :])
                            else:
                                cvS[i] = dma(g, q_cvS[i % 2], w1_dr[P * j:P * (j + 1), :],
                                             cvf[:, i % 2, :])
                    misc["k0_done"] = [(qq, qq.n) for qq in q_cvS]
                    misc["k1_done"] = misc["k1_act"] + [(qq, qq.n) for qq in q_cvS]

                LwbT = [None]
                Lu = {}
                Lv = {}
                Gx = {}
                Tpi = {}
                Tpe = {}
                Tdb = {}
                Tda = {}
                Tn1 = {}
                Tbc = {}
                Tmid = {}
                TY = {}
                Dpi = {}
                Dpe = {}
                Ddab = {}
                Dn1 = {}
                Dn2 = {}
                Dchain = {}
                Dx1 = {}
                An = {}
                An2 = {}
                Amid = {}
                Aev = {}
                Asq = {}
                Gx1 = {}
                Dln = {}
                Aln = {}
                d_zero = [None]
                aodump = [None]

                @blkA3.vector
                def _(v):
                    if s > 0:
                        v.memset(SAb[:], 0.0)
                        v.memset(SBb[:], 0.0)
                    v.memset(pib.rearrange("p a b -> p (a b)")[:], 0.0)
                    v.memset(bef.rearrange("p a b -> p (a b)")[:], 0.0)
                    v.memset(midb[:], 0.0)
                    v.memset(aef[:], 0.0)
                    v.memset(nab[:], 0.0)
                    v.memset(nabb[:], 0.0)
                    d_zero[0] = sD.inc(v.memset(ccb[:], 0.0))
                    v.drain()

                wbT_done = [False]

                def _load_wbT(sync):
                    for tok in misc["wb_done"]:
                        wt.wait(sync, "sync", tok)
                    for j in range(KT):
                        dma(sync, q_a3wb, wbT[:, j, :], wb_bf[0:D, P * j:P * (j + 1)],
                            transpose=True)
                    wt.wait(sync, "sync", (q_a3wb, q_a3wb.n))
                    LwbT[0] = sP.inc(sync.nop())
                    wbT_done[0] = True

                for c in range(NCH):
                    @blkA3.sync
                    def _(sync, c=c):
                        wt.wait(sync, "sync", Tmid.get(c - 2))
                        for tok in A1u_done:
                            wt.wait(sync, "sync", tok)
                        for j in range(KT):
                            dma(sync, q_a3u[c % 2], ufm[:, c % 2, j, :],
                                u_dr[CH * c:CH * (c + 1), P * j:P * (j + 1)], transpose=True)
                        tok = dma(sync, q_a3u[c % 2], utm[:, c % 2, :, :],
                                  u_tmv[:, 2 * c:2 * c + 2, :])
                        wt.wait(sync, "sync", tok)
                        Lu[c] = sP.inc(sync.nop())
                        if not wbT_done[0]:
                            _load_wbT(sync)

                    @blkA3.sync
                    def _(sync, c=c):
                        wt.wait(sync, "sync", Tda.get((c - 2, 1)))
                        wt.wait(sync, "sync", SvD[c])
                        for j in range(KT):
                            dma(sync, q_a3v[c % 2], vfm[:, c % 2, j, :],
                                v_dr[CH * c:CH * (c + 1), P * j:P * (j + 1)], transpose=True)
                        tok = dma(sync, q_a3v[c % 2], vtm[:, c % 2, :, :],
                                  v_tmv[:, 2 * c:2 * c + 2, :])
                        wt.wait(sync, "sync", tok)
                        Lv[c] = sP.inc(sync.nop())

                    @blkA3.scalar
                    def _(sc, c=c):
                        wt.wait(sc, "sc", Dx1.get((c - 2, 1)))
                        Gx[c] = dma(sc, q_a3x[c % 2], xck[:, c % 2, :, :],
                                    x_s[:, 2 * c:2 * c + 2, :])

                    @blkA3.tensor
                    def _(t, c=c):
                        wt.wait(t, "t", Lu[c])
                        wt.wait(t, "t", Lv[c])
                        wt.wait(t, "t", misc["d_misc"])
                        wt.wait(t, "t", misc["a_misc"])
                        wt.wait(t, "t", d_zero[0])
                        wt.wait(t, "t", Ddab.get(c - 1))
                        if skip_small:
                            Tpi[c] = sT.inc(t.nop())
                            Tpe[c] = Tpi[c]
                            return
                        if NO_SMALL_MM:
                            Tpi[c] = sT.inc(t.matmul(ps[:, PS_PI:PS_PI + 512],
                                                     ufm[:, c % 2, 0, 0:P],
                                                     wbT[:, 0, 0:512], start=True, stop=True))
                            Tpe[c] = sT.inc(t.matmul(ps[:, PS_PE:PS_PE + 512],
                                                     vfm[:, c % 2, 0, 0:P],
                                                     wbT[:, 0, 0:512], start=True, stop=True))
                            return
                        mm = None
                        for ts_ in range(2):
                            for j in range(KT):
                                mm = t.matmul(ps[:, PS_PI + R * ts_:PS_PI + R * (ts_ + 1)],
                                              ufm[:, c % 2, j, P * ts_:P * (ts_ + 1)],
                                              b0tb[:, j, :], start=(j == 0), stop=(j == KT - 1))
                        Tpi[c] = sT.inc(mm)
                        mm = None
                        for ts_ in range(2):
                            for j in range(KT):
                                mm = t.matmul(ps[:, PS_PE + R * ts_:PS_PE + R * (ts_ + 1)],
                                              vfm[:, c % 2, j, P * ts_:P * (ts_ + 1)],
                                              a0nb[:, j, :], start=(j == 0), stop=(j == KT - 1))
                        Tpe[c] = sT.inc(mm)

                    @blkA3.vector
                    def _(v, c=c):
                        if skip_small:
                            Dpi[c] = Dpe[c] = sD.inc(v.nop())
                            return
                        wt.wait(v, "v", Tpi[c])
                        v.tensor_copy(pib[:, 0, 0:R], ps[:, PS_PI:PS_PI + R])
                        Dpi[c] = sD.inc(v.tensor_copy(pib[:, 1, 0:R],
                                                      ps[:, PS_PI + R:PS_PI + 32]))
                        wt.wait(v, "v", Tpe[c])
                        Dpe[c] = sD.inc(v.tensor_copy(peb[:], ps[:, PS_PE:PS_PE + 32]))

                    @blkA3.tensor
                    def _(t, c=c):
                        wt.wait(t, "t", Aev.get((c - 1, 1)))
                        for ts_ in range(2):
                            base = PS_Y0 if ts_ == 0 else PS_Y1
                            mm = None
                            for h in range(2):
                                for j in range(KT):
                                    mm = t.matmul(ps[:, base + 512 * h:base + 512 * (h + 1)],
                                                  ufm[:, c % 2, j, P * ts_:P * (ts_ + 1)],
                                                  wbT[:, j, 512 * h:512 * (h + 1)],
                                                  start=(j == 0),
                                                  stop=(nolora and j == KT - 1),
                                                  skip_group_check=not nolora)
                            if nolora:
                                TY[(c, ts_)] = sT.inc(mm)
                        if SERIALIZE_EVAC:
                            wt.wait(t, "t", Aev.get((c, 1)))
                        if skip_small:
                            Tdb[c] = sT.inc(t.nop())
                            Tda[(c, 0)] = Tda[(c, 1)] = Tdb[c]
                            return
                        wt.wait(t, "t", Dpe[c])
                        if NO_SMALL_MM:
                            Tdb[c] = sT.inc(t.matmul(ps[:, PS_DB:PS_DB + 512],
                                                     utm[:, c % 2, 0, 0:P],
                                                     wbT[:, 0, 0:512], start=True, stop=True))
                            wt.wait(t, "t", Dpi[c])
                            for h in range(2):
                                off = PS_PI if h == 0 else PS_PE
                                Tda[(c, h)] = sT.inc(t.matmul(ps[:, off:off + 512],
                                                              pib[:, 0, :],
                                                              vtm[:, c % 2, 0, 512 * h:512 * (h + 1)],
                                                              start=True, stop=True))
                            return
                        mm = None
                        for j in range(KT):
                            for ts_ in range(2):
                                mm = t.matmul(ps[:, PS_DB + R * j:PS_DB + R * (j + 1)],
                                              utm[:, c % 2, ts_, P * j:P * (j + 1)],
                                              peb[:, R * ts_:R * (ts_ + 1)],
                                              start=(j == 0 and ts_ == 0),
                                              stop=(j == KT - 1 and ts_ == 1),
                                              skip_group_check=True)
                        Tdb[c] = sT.inc(mm)
                        wt.wait(t, "t", Dpi[c])
                        for h in range(2):
                            off = PS_PI if h == 0 else PS_PE
                            mm = None
                            for ts_ in range(2):
                                mm = t.matmul(ps[:, off:off + 512],
                                              pib[:, ts_, :],
                                              vtm[:, c % 2, ts_, 512 * h:512 * (h + 1)],
                                              start=(ts_ == 0), stop=(ts_ == 1),
                                              skip_group_check=True)
                            Tda[(c, h)] = sT.inc(mm)

                    @blkA3.vector
                    def _(v, c=c):
                        if skip_small:
                            Ddab[c] = sD.inc(v.nop())
                            return
                        wt.wait(v, "v", Tdb[c])
                        v.tensor_copy(dBc[:], ps[:, PS_DB:PS_DB + P])
                        wt.wait(v, "v", Tda[(c, 1)])
                        v.tensor_copy(dAc[0:R, 0:512], ps[0:R, PS_PI:PS_PI + 512])
                        Ddab[c] = sD.inc(
                            v.tensor_copy(dAc[0:R, 512:1024], ps[0:R, PS_PE:PS_PE + 512]))

                    @blkA3.scalar
                    def _(sc, c=c):
                        if skip_small:
                            An[c] = An[c + 100] = sA.inc(sc.nop())
                            return
                        if c == 0:
                            wt.wait(sc, "sc", d_zero[0])
                            wt.wait(sc, "sc", misc["d_mem"])
                        else:
                            wt.wait(sc, "sc", Dchain[c - 1])
                        An[c] = sA.inc(sc.activation(sqd[0:R, :], SAb[:], AF.Square,
                                                     accum_out=nab[0:R, :]))
                        sc.drain()
                        An[c + 100] = sA.inc(sc.activation(sqd[:, 0:P], SBb[:], AF.Square,
                                                           accum_out=nbb[:]))

                    @blkA3.vector
                    def _(v, c=c):
                        if skip_small:
                            An[(c, "cast")] = sD.inc(v.nop())
                            return
                        wt.wait(v, "v", An[c + 100])
                        v.tensor_copy(nabb[:, 0:1], nab[:])
                        An[(c, "cast")] = sD.inc(v.tensor_copy(nabb[:, 1:2], nbb[:]))

                    @blkA3.tensor
                    def _(t, c=c):
                        if skip_small:
                            Tn1[c] = sT.inc(t.nop())
                            return
                        wt.wait(t, "t", An[(c, "cast")])
                        wt.wait(t, "t", Amid.get(c - 1))
                        if NO_SMALL_MM:
                            Tn1[c] = sT.inc(t.matmul(ps[:, PS_N:PS_N + 512], onesbb[:],
                                                     aef[:, 0:512], start=True, stop=True))
                            return
                        t.matmul(ps[:, PS_N:PS_N + 1], onesbb[:], nabb[:, 0:1],
                                 start=True, stop=False, skip_group_check=True)
                        mm = t.matmul(ps[:, PS_N:PS_N + 1], onesbb[:], nabb[:, 1:2],
                                      start=False, stop=True, skip_group_check=True)
                        Tn1[c] = sT.inc(mm)

                    @blkA3.vector
                    def _(v, c=c):
                        if skip_small:
                            Dn1[c] = sD.inc(v.nop())
                            return
                        wt.wait(v, "v", Tn1[c])
                        Dn1[c] = sD.inc(v.tensor_copy(nsq[0:1, 0:1], ps[0:1, PS_N:PS_N + 1]))

                    @blkA3.scalar
                    def _(sc, c=c):
                        if skip_small:
                            An2[c] = sA.inc(sc.nop())
                            return
                        wt.wait(sc, "sc", Dn1[c])
                        An2[c] = sA.inc(sc.activation(nsq[0:1, 1:2], nsq[0:1, 0:1], AF.Sqrt,
                                                      bias=zcol[0:1, :], scale=1.0))

                    @blkA3.vector
                    def _(v, c=c):
                        if skip_small:
                            Dn2[c] = sD.inc(v.nop())
                            return
                        wt.wait(v, "v", An2[c])
                        v.tensor_scalar(nsq[0:1, 2:3], nsq[0:1, 1:2], LR, 1e-6,
                                        ALU.mult, ALU.add)
                        v.drain()
                        v.reciprocal(nsq[0:1, 3:4], nsq[0:1, 2:3])
                        v.drain()
                        v.tensor_scalar(nsq[0:1, 4:5], nsq[0:1, 3:4], 1.0, None, ALU.min)
                        v.drain()
                        v.tensor_scalar(ccb[0:1, 0:1], nsq[0:1, 4:5], -2.0 * LR, None, ALU.mult)
                        Dn2[c] = sD.inc(v.tensor_scalar(ccb[0:1, 1:2], nsq[0:1, 4:5],
                                                        -LR, None, ALU.mult))

                    @blkA3.tensor
                    def _(t, c=c):
                        if skip_small:
                            Tbc[c] = sT.inc(t.nop())
                            return
                        wt.wait(t, "t", Dn2[c])
                        if NO_SMALL_MM:
                            Tbc[c] = sT.inc(t.matmul(ps[:, PS_N:PS_N + 512], onesbb[:],
                                                     aef[:, 0:512], start=True, stop=True,
                                                     skip_group_check=True))
                            return
                        Tbc[c] = sT.inc(t.matmul(ps[:, PS_BC:PS_BC + 2], onesbb[:],
                                                 ccb[:, 0:2], start=True, stop=True,
                                                 skip_group_check=True))

                    @blkA3.vector
                    def _(v, c=c):
                        if skip_small:
                            Dchain[c] = sD.inc(v.nop())
                            return
                        wt.wait(v, "v", Tbc[c])
                        v.tensor_copy(bcb[:], ps[:, PS_BC:PS_BC + 2])
                        v.drain()
                        v.scalar_tensor_tensor(aef[0:R, :], SAb[:], bcb[0:R, 0:1], a0t2[:],
                                               ALU.mult, ALU.add)
                        v.scalar_tensor_tensor(bef[:, :, 0:R],
                                               SBb.rearrange("p (j r) -> p j r", r=R)[:],
                                               bcb[:, 1:2], b0tf[:],
                                               ALU.mult, ALU.add)
                        v.drain()
                        wt.wait(v, "v", Ddab[c])
                        v.tensor_tensor(SAb[:], SAb[:], dAc[:], ALU.add)
                        Dchain[c] = sD.inc(v.tensor_tensor(SBb[:], SBb[:], dBc[:], ALU.add))

                    @blkA3.tensor
                    def _(t, c=c):
                        if skip_small:
                            Tmid[c] = sT.inc(t.nop())
                            return
                        wt.wait(t, "t", Dchain[c])
                        if NO_SMALL_MM:
                            Tmid[c] = sT.inc(t.matmul(ps[:, PS_N:PS_N + 512], onesbb[:],
                                                      aef[:, 0:512], start=True, stop=True,
                                                      skip_group_check=True))
                            return
                        mm = None
                        for j in range(KT):
                            mm = t.matmul(ps[:, PS_MID:PS_MID + CH],
                                          bef[:, j, :],
                                          ufm[:, c % 2, j, :], start=(j == 0),
                                          stop=(j == KT - 1), skip_group_check=True)
                        Tmid[c] = sT.inc(mm)

                    @blkA3.scalar
                    def _(sc, c=c):
                        if skip_small:
                            Amid[c] = sA.inc(sc.nop())
                            return
                        wt.wait(sc, "sc", Tmid[c])
                        Amid[c] = sA.inc(sc.activation(midb[0:R, :],
                                                       ps[0:R, PS_MID:PS_MID + CH],
                                                       AF.Copy, bias=0.0, scale=1.0))

                    if not nolora:
                        @blkA3.tensor
                        def _(t, c=c):
                            wt.wait(t, "t", Amid[c])
                            for ts_ in range(2):
                                base = PS_Y0 if ts_ == 0 else PS_Y1
                                mm = None
                                for h in range(2):
                                    mm = t.matmul(ps[:, base + 512 * h:base + 512 * (h + 1)],
                                                  midb[:, P * ts_:P * (ts_ + 1)],
                                                  aef[:, 512 * h:512 * (h + 1)],
                                                  start=False, stop=True,
                                                  skip_group_check=True)
                                TY[(c, ts_)] = sT.inc(mm)

                    @blkA3.scalar
                    def _(sc, c=c):
                        for ts_ in range(2):
                            base = PS_Y0 if ts_ == 0 else PS_Y1
                            wt.wait(sc, "sc", TY[(c, 1)])
                            wt.wait(sc, "sc", Dx1.get((c - 1, ts_)))
                            Aev[(c, ts_)] = sA.inc(
                                sc.activation(ybf[:, ts_, :], ps[:, base:base + D],
                                              AF.Copy, accum_out=syb[:, ts_:ts_ + 1]))
                        for ts_ in range(2):
                            sc.drain()
                            Asq[(c, ts_)] = sA.inc(
                                sc.activation(sqd[:], ybf[:, ts_, :], AF.Square,
                                              accum_out=sy2b[:, ts_:ts_ + 1]))
                        if ao_out is not None:
                            aosv = ao_out[s].rearrange("(t p) k -> p t k", p=P)
                            aodump[0] = dma(sc, q_dbg, aosv[:, 2 * c:2 * c + 2, :],
                                            ybf[:, :, :])

                    @blkA3.vector
                    def _(v, c=c):
                        wt.wait(v, "v", Asq[(c, 1)])
                        v.tensor_scalar(lnc[:, 0:2], syb[:], 1.0 / D, None, ALU.mult)
                        v.drain()
                        v.tensor_scalar(lnc[:, 2:3], lnc[:, 0:1], lnc[:, 0:1], None, ALU.mult)
                        v.tensor_scalar(lnc[:, 3:4], lnc[:, 1:2], lnc[:, 1:2], None, ALU.mult)
                        v.drain()
                        v.tensor_scalar(lnc[:, 4:5], sy2b[:, 0:1], 1.0 / D, lnc[:, 2:3],
                                        ALU.mult, ALU.subtract)
                        Dln[c] = sD.inc(v.tensor_scalar(lnc[:, 5:6], sy2b[:, 1:2], 1.0 / D,
                                                        lnc[:, 3:4], ALU.mult, ALU.subtract))

                    @blkA3.scalar
                    def _(sc, c=c):
                        wt.wait(sc, "sc", Dln[c])
                        Aln[c] = sA.inc(sc.activation(lnc[:, 6:8], lnc[:, 4:6], AF.Sqrt,
                                                      bias=epscol[:], scale=1.0))

                    @blkA3.vector
                    def _(v, c=c):
                        wt.wait(v, "v", Aln[c])
                        v.reciprocal(lnc[:, 8:9], lnc[:, 6:7])
                        v.reciprocal(lnc[:, 9:10], lnc[:, 7:8])
                        v.drain()
                        wt.wait(v, "v", Gx[c])
                        wt.wait(v, "v", Gx1.get(c - 2))
                        if ao_out is not None:
                            wt.wait(v, "v", aodump[0])
                        for ts_ in range(2):
                            v.tensor_scalar(tpb[:, ts_, :], ybf[:, ts_, :], lnc[:, ts_:ts_ + 1],
                                            lnc[:, 8 + ts_:9 + ts_], ALU.subtract, ALU.mult)
                        v.drain()
                        for ts_ in range(2):
                            Dx1[(c, ts_)] = sD.inc(
                                v.tensor_tensor(x1b[:, c % 2, ts_, :], tpb[:, ts_, :],
                                                xck[:, c % 2, ts_, :], ALU.add))

                    @blkA3.scalar
                    def _(sc, c=c):
                        wt.wait(sc, "sc", Dx1[(c, 1)])
                        dma(sc, q_a3s[c % 2], x1_tmv[:, 2 * c:2 * c + 2, :],
                            x1b[:, c % 2, :, :])
                        if x1_out is not None:
                            x1sv = x1_out[s].rearrange("(t p) k -> p t k", p=P)
                            dma(sc, q_a3s[c % 2], x1sv[:, 2 * c:2 * c + 2, :],
                                x1b[:, c % 2, :, :])
                        Gx1[c] = (q_a3s[c % 2], q_a3s[c % 2].n)

                @blkA3.scalar
                def _(sc):
                    for qq in q_a3s:
                        wt.wait(sc, "sc", (qq, qq.n))

                Gx1_all = dict(Gx1)

            b = contextlib.ExitStack()
            with b:
                x1i = b.enter_context(nc.sbuf_tensor(f"bx1{s}", [P, 2, TSUB, D], F32))
                u2s = b.enter_context(nc.sbuf_tensor(f"bu2{s}", [P, 2, TSUB, D], BF16))
                u2f = b.enter_context(nc.sbuf_tensor(f"bu2f{s}", [P, 2, KT, FT], BF16,
                                                     is_transpose_dest=True))
                w0b = b.enter_context(nc.sbuf_tensor(f"bw0{s}", [P, 2, KT, CH], FP8))
                w1b = b.enter_context(nc.sbuf_tensor(f"bw1{s}", [P, 2, HT, P], FP8))
                w2b = b.enter_context(nc.sbuf_tensor(f"bw2{s}", [P, 2, 4, D], FP8))
                a0b = b.enter_context(nc.sbuf_tensor(f"ba0{s}", [P, HT, FT], BF16))
                a1b = b.enter_context(nc.sbuf_tensor(f"ba1{s}", [P, HT, FT], BF16))
                st6b = b.enter_context(nc.sbuf_tensor(f"bst{s}", [P, 12], F32))
                mv2 = b.enter_context(nc.sbuf_tensor(f"bmv{s}", [P, 2], F32))
                l2m = b.enter_context(nc.sbuf_tensor(f"bl2m{s}", [P, TSUB], F32))
                l2v = b.enter_context(nc.sbuf_tensor(f"bl2v{s}", [P, TSUB], F32))
                l2s = b.enter_context(nc.sbuf_tensor(f"bl2s{s}", [P, TSUB], F32))
                l2r = b.enter_context(nc.sbuf_tensor(f"bl2r{s}", [P, TSUB], F32))
                sqdB = b.enter_context(nc.sbuf_tensor(f"bsq{s}", [P, D], BF16))
                yb2 = b.enter_context(nc.sbuf_tensor(f"byb{s}", [P, 2, D], F32))
                oms = b.enter_context(nc.sbuf_tensor(f"bom{s}", [P, 12], F32))
                tpB = b.enter_context(nc.sbuf_tensor(f"btp{s}", [P, D], F32))
                obB = b.enter_context(nc.sbuf_tensor(f"bob{s}", [P, 2, D], F32))

                blkB = b.enter_context(nc.Block(f"b_{s}"))

                Gx1L = {}
                Dvar = {}
                Asr = {}
                Du2 = {}
                Su2 = {}
                Lu2f = {}
                Lw0 = {}
                Lw1 = {}
                Lw2 = {}
                Tm0 = {}
                Am0 = {}
                Tm1 = {}
                Am1 = {}
                Tm2g = {}
                Aev2 = {}
                Asq2 = {}
                Dst2 = {}
                Asr2 = {}
                Dob = {}
                Oob = {}
                NJG = HT // 2

                for fn in range(NFT):
                    @blkB.gpsimd
                    def _(g, fn=fn):
                        wt.wait(g, "g", Gx1_all[2 * fn + 1])
                        if fn >= 2:
                            wt.wait(g, "g", Du2.get(fn - 2))
                        Gx1L[(fn, "p")] = dma(g, q_bx[fn % 2], x1i[:, fn % 2, :, :],
                                              x1_tmv[:, TSUB * fn:TSUB * (fn + 1), :])

                    @blkB.vector
                    def _(v, fn=fn):
                        wt.wait(v, "v", Gx1L[(fn, "p")])
                        for ts_ in range(TSUB):
                            v.bn_stats(st6b[:, 0:6], x1i[:, fn % 2, ts_, 0:512])
                            v.bn_stats(st6b[:, 6:12], x1i[:, fn % 2, ts_, 512:1024])
                            v.drain()
                            v.bn_aggr(mv2[:], st6b[:])
                            v.drain()
                            v.tensor_copy(l2m[:, ts_:ts_ + 1], mv2[:, 0:1])
                            Dvar[(fn, ts_)] = sD.inc(
                                v.tensor_copy(l2v[:, ts_:ts_ + 1], mv2[:, 1:2]))

                    @blkB.scalar
                    def _(sc, fn=fn):
                        wt.wait(sc, "sc", Dvar[(fn, TSUB - 1)])
                        Asr[fn] = sA.inc(sc.activation(l2s[:], l2v[:], AF.Sqrt,
                                                       bias=epscol[:], scale=1.0))

                    @blkB.vector
                    def _(v, fn=fn):
                        wt.wait(v, "v", Asr[fn])
                        wt.wait(v, "v", Su2.get(fn - 2))
                        for ts_ in range(TSUB):
                            v.reciprocal(l2r[:, ts_:ts_ + 1], l2s[:, ts_:ts_ + 1])
                        v.drain()
                        ins = None
                        for ts_ in range(TSUB):
                            ins = v.tensor_scalar(u2s[:, fn % 2, ts_, :],
                                                  x1i[:, fn % 2, ts_, :],
                                                  l2m[:, ts_:ts_ + 1], l2r[:, ts_:ts_ + 1],
                                                  ALU.subtract, ALU.mult)
                        Du2[fn] = sD.inc(ins)

                    @blkB.scalar
                    def _(sc, fn=fn):
                        wt.wait(sc, "sc", Du2[fn])
                        Su2[fn] = dma(sc, q_bu[fn % 2],
                                      u2_tmv[:, TSUB * fn:TSUB * (fn + 1), :],
                                      u2s[:, fn % 2, :, :])

                for f in range(NFT):
                    fn = f
                    @blkB.sync
                    def _(sync, fn=fn):
                        wt.wait(sync, "sync", Su2[fn])
                        wt.wait(sync, "sync", Tm0.get((fn - 2, NJG - 1)))
                        for j in range(KT):
                            dma(sync, q_bf[fn % 2], u2f[:, fn % 2, j, :],
                                u2_dr[FT * fn:FT * (fn + 1), P * j:P * (j + 1)],
                                transpose=True)
                        Lu2f[fn] = (q_bf[fn % 2], q_bf[fn % 2].n)

                    @blkB.gpsimd
                    def _(g, f=f):
                        wt.wait(g, "g", Du2.get(NFT - 1))
                        wt.wait(g, "g", Dob.get((f - 2, TSUB - 1)))
                        Gx1L[f] = dma(g, q_bx[f % 2], x1i[:, f % 2, :, :],
                                      x1_tmv[:, TSUB * f:TSUB * (f + 1), :])

                    for jg in range(NJG):
                        @blkB.sync
                        def _(sync, f=f, jg=jg):
                            if f == 0 and jg == 0:
                                for tok in misc["k0_done"]:
                                    wt.wait(sync, "sync", tok)
                            pj = (f, jg - 2) if jg >= 2 else (f - 1, NJG - 2 + jg)
                            wt.wait(sync, "sync", Tm0.get(pj))
                            Lw0[(f, jg)] = dma(sync, q_bw0[jg % 2], w0b[:, jg % 2, :, :],
                                               w0_kt[:, :, CH * jg:CH * (jg + 1)])

                        @blkB.tensor
                        def _(t, f=f, jg=jg):
                            wt.wait(t, "t", Lw0[(f, jg)])
                            wt.wait(t, "t", Lu2f[f])
                            pj = (f, jg - 2) if jg >= 2 else (f - 1, NJG - 2 + jg)
                            wt.wait(t, "t", Am0.get(pj))
                            if jg < 2 and f >= 1:
                                wt.wait(t, "t", Am1.get((f - 1, HT - 1)))
                            base = PS_Y0 if jg % 2 == 0 else PS_Y1
                            mm = None
                            for jj in range(2):
                                for j in range(KT):
                                    mm = t.matmul(ps[:, base + 512 * jj:base + 512 * (jj + 1)],
                                                  w0b[:, jg % 2, j, P * jj:P * (jj + 1)],
                                                  u2f[:, f % 2, j, :],
                                                  start=(j == 0), stop=(j == KT - 1))
                            Tm0[(f, jg)] = sT.inc(mm)

                        @blkB.scalar
                        def _(sc, f=f, jg=jg):
                            wt.wait(sc, "sc", Tm0[(f, jg)])
                            if f >= 1:
                                wt.wait(sc, "sc", Tm1.get((f - 1, HT - 1)))
                            base = PS_Y0 if jg % 2 == 0 else PS_Y1
                            ins = None
                            for jj in range(2):
                                ins = sc.activation(a0b[:, 2 * jg + jj, :],
                                                    ps[:, base + 512 * jj:base + 512 * (jj + 1)],
                                                    AF.Relu, bias=zcol[:], scale=s0c2[:])
                            Am0[(f, jg)] = sA.inc(ins)

                    for jg in range(HT):
                        @blkB.sync
                        def _(sync, f=f, jg=jg):
                            if f == 0 and jg == 0:
                                for tok in misc["k1_done"]:
                                    wt.wait(sync, "sync", tok)
                            pj = (f, jg - 2) if jg >= 2 else (f - 1, HT - 2 + jg)
                            wt.wait(sync, "sync", Tm1.get(pj))
                            Lw1[(f, jg)] = dma(sync, q_bw1[jg % 2], w1b[:, jg % 2, :, :],
                                               w1_kt[:, :, P * jg:P * (jg + 1)])

                        @blkB.tensor
                        def _(t, f=f, jg=jg):
                            wt.wait(t, "t", Lw1[(f, jg)])
                            wt.wait(t, "t", Am0[(f, NJG - 1)])
                            if jg >= 2:
                                wt.wait(t, "t", Am1.get((f, jg - 2)))
                            base = PS_Y0 if jg % 2 == 0 else PS_Y1
                            mm = None
                            for k in range(HT):
                                mm = t.matmul(ps[:, base:base + 512],
                                              w1b[:, jg % 2, k, :],
                                              a0b[:, k, :],
                                              start=(k == 0), stop=(k == HT - 1))
                            Tm1[(f, jg)] = sT.inc(mm)

                        @blkB.scalar
                        def _(sc, f=f, jg=jg):
                            wt.wait(sc, "sc", Tm1[(f, jg)])
                            if f >= 1:
                                wt.wait(sc, "sc", Tm2g.get((f - 1, 15)))
                            base = PS_Y0 if jg % 2 == 0 else PS_Y1
                            ins = sc.activation(a1b[:, jg, :], ps[:, base:base + 512],
                                                AF.Relu, bias=zcol[:], scale=scola[:, 1:2])
                            Am1[(f, jg)] = sA.inc(ins)

                    def _m2_wave(wv):
                        for kg in range(8):
                            gi = 8 * wv + kg
                            @blkB.sync
                            def _(sync, f=f, gi=gi):
                                if f == 0 and gi == 0:
                                    for tok in misc["k2_done"]:
                                        wt.wait(sync, "sync", tok)
                                pj = (f, gi - 2) if gi >= 2 else (f - 1, 14 + gi)
                                wt.wait(sync, "sync", Tm2g.get(pj))
                                Lw2[(f, gi)] = dma(sync, q_bw2[gi % 2], w2b[:, gi % 2, :, :],
                                                   w2_kt[:, 4 * (gi % 8):4 * (gi % 8) + 4, :])

                            @blkB.tensor
                            def _(t, f=f, wv=wv, kg=kg, gi=gi):
                                wt.wait(t, "t", Lw2[(f, gi)])
                                wt.wait(t, "t", Am1[(f, HT - 1)])
                                if kg == 0 and wv == 0 and f >= 1:
                                    wt.wait(t, "t", Aev2.get((f - 1, TSUB - 1)))
                                if kg == 0 and wv == 1:
                                    wt.wait(t, "t", Aev2.get((f, 1)))
                                mm = None
                                for tw in range(2):
                                    ts_ = 2 * wv + tw
                                    for kk in range(4):
                                        k = 4 * kg + kk
                                        for h in range(2):
                                            mm = t.matmul(
                                                ps[:, 2048 + 1024 * tw + 512 * h:
                                                   2048 + 1024 * tw + 512 * (h + 1)],
                                                a1b[:, k, P * ts_:P * (ts_ + 1)],
                                                w2b[:, gi % 2, kk, 512 * h:512 * (h + 1)],
                                                start=(k == 0), stop=(k == HT - 1),
                                                skip_group_check=True)
                                Tm2g[(f, gi)] = sT.inc(mm)

                    def _ev(sc, f, ts_):
                        wt.wait(sc, "sc", Tm2g[(f, 7 if ts_ < 2 else 15)])
                        if f >= 1 or ts_ >= 2:
                            pj = (f, ts_ - 2) if ts_ >= 2 else (f - 1, TSUB - 2 + ts_)
                            wt.wait(sc, "sc", Dob.get(pj))
                        Aev2[(f, ts_)] = sA.inc(
                            sc.activation(yb2[:, ts_ % 2, :],
                                          ps[:, 2048 + 1024 * (ts_ % 2):2048 + 1024 * (ts_ % 2) + D],
                                          AF.Copy, scale=scola[:, 2:3],
                                          accum_out=oms[:, ts_:ts_ + 1]))

                    def _sq(sc, f, ts_):
                        sc.drain()
                        Asq2[(f, ts_)] = sA.inc(
                            sc.activation(sqdB[:], yb2[:, ts_ % 2, :], AF.Square,
                                          accum_out=oms[:, 4 + ts_:5 + ts_]))

                    def _dve_stats(v, f, ts_):
                        wt.wait(v, "v", Asq2[(f, ts_)])
                        v.tensor_scalar(oms[:, 8:9], oms[:, ts_:ts_ + 1], 1.0 / D,
                                        None, ALU.mult)
                        v.drain()
                        v.tensor_scalar(oms[:, 9:10], oms[:, 8:9], oms[:, 8:9],
                                        None, ALU.mult)
                        v.drain()
                        Dst2[(f, ts_)] = sD.inc(
                            v.tensor_scalar(oms[:, 10:11], oms[:, 4 + ts_:5 + ts_],
                                            1.0 / D, oms[:, 9:10], ALU.mult, ALU.subtract))

                    def _sqrt(sc, f, ts_):
                        wt.wait(sc, "sc", Dst2[(f, ts_)])
                        Asr2[(f, ts_)] = sA.inc(
                            sc.activation(oms[:, 11:12], oms[:, 10:11], AF.Sqrt,
                                          bias=epscol[:], scale=1.0))

                    def _out(v, f, ts_):
                        wt.wait(v, "v", Asr2[(f, ts_)])
                        wt.wait(v, "v", Gx1L[f])
                        v.reciprocal(oms[:, 10:11], oms[:, 11:12])
                        v.drain()
                        v.tensor_scalar(tpB[:], yb2[:, ts_ % 2, :], oms[:, 8:9],
                                        oms[:, 10:11], ALU.subtract, ALU.mult)
                        v.drain()
                        if f >= 1 or ts_ >= 2:
                            pj = (f, ts_ - 2) if ts_ >= 2 else (f - 1, TSUB - 2 + ts_)
                            wt.wait(v, "v", Oob.get(pj))
                        Dob[(f, ts_)] = sD.inc(
                            v.tensor_tensor(obB[:, ts_ % 2, :], tpB[:],
                                            x1i[:, f % 2, ts_, :], ALU.add))

                    def _store(g, f, ts_):
                        wt.wait(g, "g", Dob[(f, ts_)])
                        Oob[(f, ts_)] = dma(g, q_bo[ts_ % 2], y_s[:, TSUB * f + ts_, :],
                                            obB[:, ts_ % 2, :])

                    _m2_wave(0)

                    @blkB.scalar
                    def _(sc, f=f):
                        _ev(sc, f, 0)
                        _ev(sc, f, 1)

                    _m2_wave(1)

                    for ts_ in range(TSUB):
                        @blkB.scalar
                        def _(sc, f=f, ts_=ts_):
                            _sq(sc, f, ts_)

                        @blkB.vector
                        def _(v, f=f, ts_=ts_):
                            _dve_stats(v, f, ts_)

                        @blkB.scalar
                        def _(sc, f=f, ts_=ts_):
                            _sqrt(sc, f, ts_)

                        @blkB.vector
                        def _(v, f=f, ts_=ts_):
                            _out(v, f, ts_)

                        @blkB.gpsimd
                        def _(g, f=f, ts_=ts_):
                            _store(g, f, ts_)

                        if ts_ + 2 < TSUB:
                            @blkB.scalar
                            def _(sc, f=f, ts_=ts_):
                                _ev(sc, f, ts_ + 2)

                @blkB.gpsimd
                def _(g):
                    for qq in q_bo:
                        wt.wait(g, "g", (qq, qq.n))

                @blkB.scalar
                def _(sc):
                    for qq in q_bu + [q_dbg]:
                        wt.wait(sc, "sc", (qq, qq.n))

        fin = contextlib.ExitStack()
        with fin:
            blkF = fin.enter_context(nc.Block("fin"))

            @blkF.sync
            def _(sync):
                for qq in q_bo + [q_dbg] + q_a3s:
                    wt.wait(sync, "sync", (qq, qq.n))

    return nc


_NC_CACHE = {}


def _get_nc(debug_dump=None, nolora=False, skip_small=False):
    key = (debug_dump, nolora, skip_small)
    if key not in _NC_CACHE:
        _NC_CACHE[key] = build_program(debug_dump, nolora, skip_small)
    return _NC_CACHE[key]


def kernel(L0_attn_x, L0_attn_Wb, L0_attn_init_A, L0_attn_init_B, L0_attn_ln_gamma,
           L0_attn_ln_beta, L0_attn_Wp, L0_ffn_k0, L0_ffn_k1, L0_ffn_k2,
           L0_ffn_s0, L0_ffn_s1, L0_ffn_s2, _debug_dump=None, _nolora=False, _skip_small=False):
    x = np.ascontiguousarray(np.asarray(L0_attn_x, dtype=np.float32))
    wb = np.ascontiguousarray(np.asarray(L0_attn_Wb, dtype=np.float32))
    wp = np.ascontiguousarray(np.asarray(L0_attn_Wp, dtype=np.float32))
    a0 = np.ascontiguousarray(np.asarray(L0_attn_init_A, dtype=np.float32))
    b0 = np.ascontiguousarray(np.asarray(L0_attn_init_B, dtype=np.float32))
    gam = np.ascontiguousarray(np.asarray(L0_attn_ln_gamma, dtype=np.float32)).reshape(1, D)
    bet = np.ascontiguousarray(np.asarray(L0_attn_ln_beta, dtype=np.float32)).reshape(1, D)
    k0 = np.ascontiguousarray(np.asarray(L0_ffn_k0)).view(np.uint8)
    k1 = np.ascontiguousarray(np.asarray(L0_ffn_k1)).view(np.uint8)
    k2 = np.ascontiguousarray(np.asarray(L0_ffn_k2)).view(np.uint8)
    s0 = np.asarray(L0_ffn_s0, dtype=np.float32).reshape(1, 1)
    s1 = np.asarray(L0_ffn_s1, dtype=np.float32).reshape(1, 1)
    s2 = np.asarray(L0_ffn_s2, dtype=np.float32).reshape(1, 1)
    eye = np.eye(P, dtype=np.float32)
    ones = np.ones((P, P), dtype=np.float32)

    nc = _get_nc(_debug_dump, _nolora, _skip_small)
    in_maps = []
    for core in range(NCORES):
        in_maps.append(dict(
            x=np.ascontiguousarray(x[NSEQ * core:NSEQ * (core + 1)]),
            wb=wb, wp=wp, a0=a0, b0=b0, gam=gam, bet=bet,
            k0=k0, k1=k1, k2=k2, s0=s0, s1=s1, s2=s2, eye=eye, ones=ones,
        ))
    res = run_bass_kernel_spmd(nc, in_maps, core_ids=list(range(NCORES)))
    out = np.concatenate([r["y"] for r in res.results], axis=0)
    if _debug_dump == "x1":
        x1 = np.concatenate([r["x1dbg"] for r in res.results], axis=0)
        return out, x1
    if _debug_dump == "all":
        dbg = {"x1": np.concatenate([r["x1dbg"] for r in res.results], axis=0),
               "ao": np.concatenate([r["aodbg"] for r in res.results], axis=0)}
        dbg["cst"] = res.results[0]["cstdbg"]
        for k in ["u_dr", "xn_dr", "v_dr", "u2_dr", "x1_dr", "w0_dr", "wb_bf"]:
            dbg[k] = np.stack([r[k] for r in res.results], axis=0)
        return out, dbg
    return out
